# version 05
# baseline (speedup 1.0000x reference)
import numpy as np
import concourse.bass as bass
import concourse.mybir as mybir
from concourse.bass_utils import run_bass_kernel_spmd

AF = mybir.ActivationFunctionType
OP = mybir.AluOpType
AX = mybir.AxisListType
F32 = mybir.dt.float32
BF16 = mybir.dt.bfloat16

N = 2048
D = 1024
NCH = 8
CW = N + 1
H = 16
DH = 64
NSC = 16
NIT = 4

ENGS = ("tensor", "vector", "scalar", "sync", "gpsimd")


class Prog:

    NLANE = 4

    def __init__(self, nc):
        self.nc = nc
        self.steps = {e: [] for e in ENGS}
        self.sems = {}
        self.cnt = {}
        self._sem_ctx = []
        self.lane_rr = {}

    def sem(self, name):
        if name not in self.sems:
            cm = self.nc.semaphore(name)
            h = cm.__enter__()
            self._sem_ctx.append(cm)
            self.sems[name] = h
            self.cnt[name] = 0
        return name

    ENG_SEM = {"vector": "V", "scalar": "S", "gpsimd": "GC"}

    def step(self, eng, fn, waits=(), inc=None, amt=1):
        w = tuple(ev for ev in waits if ev is not None)
        ev = None
        if inc is not None and amt == 16:
            li = self.lane_rr.get(eng, 0)
            self.lane_rr[eng] = (li + 1) % self.NLANE
            lane = f"{inc}{li}"
            self.sem(lane)
            if self.cnt[lane] > 0:
                w = w + ((lane, self.cnt[lane]),)
            self.cnt[lane] += 16
            ev = (lane, self.cnt[lane])
            self.steps[eng].append((fn, w, (lane, 16)))
            return ev
        es = self.ENG_SEM.get(eng)
        if es is not None:
            inc = es
            amt = 1
        if inc is not None:
            self.sem(inc)
            if es is not None and self.cnt[inc] > 0:
                w = w + ((inc, self.cnt[inc]),)
            self.cnt[inc] += amt
            ev = (inc, self.cnt[inc])
        self.steps[eng].append((fn, w, (inc, amt) if inc else None))
        return ev

    def emit(self, block):
        for e in ENGS:
            if self.steps[e]:
                self._emit_engine(block, e)

    def _emit_engine(self, block, e):
        prog = self

        def runner(eng):
            seen = {}
            for fn, waits, inc in prog.steps[e]:
                for (s, v) in waits:
                    if seen.get(s, 0) < v:
                        eng.wait_ge(prog.sems[s], v)
                        seen[s] = v
                inst = fn(eng)
                if inc is not None:
                    if isinstance(inst, (list, tuple)):
                        inst = inst[-1]
                    inst.then_inc(prog.sems[inc[0]], inc[1])

        getattr(block, e)(runner)


class Arena:
    def __init__(self, nc):
        self.nc = nc
        self.top = (nc.sbuf_base + 63) // 64 * 64
        self.limit = nc.sbuf_top
        self.tensors = {}

    @staticmethod
    def _size(shape, dtype):
        es = 2 if dtype == BF16 else 4
        n = 1
        for s in shape[1:]:
            n *= s
        return n * es

    def alloc(self, name, shape, dtype, align=64):
        off = (self.top + align - 1) // align * align
        t = self.alloc_at(name, shape, dtype, off)
        self.top = off + self._size(shape, dtype)
        return t

    def alloc_at(self, name, shape, dtype, off):
        t = self.nc.alloc_sbuf_tensor_at(name, shape, dtype, offset=off)
        end = off + self._size(shape, dtype)
        assert end <= self.limit, f"SBUF overflow at {name}: {end}"
        self.tensors[name] = (t, off)
        return t

    def off(self, name):
        return self.tensors[name][1]


def host_consts():
    import ml_dtypes
    bf = ml_dtypes.bfloat16
    c = {}
    a = np.zeros((128, 128), np.float32); a[:, :64] = 1
    c["statA"] = a.astype(bf)
    a = np.zeros((128, 128), np.float32); a[:, 64:] = 1
    c["statB"] = a.astype(bf)
    c["rep64f"] = np.full((128, 128), 1.0 / 64, np.float32)
    c["ones1r"] = np.ones((1, 512), np.float32).astype(bf)
    a = np.zeros((64, 2), np.float32); a[:, 0] = 1
    c["c64a"] = a.astype(bf)
    a = np.zeros((64, 2), np.float32); a[:, 1] = 1
    c["c64b"] = a.astype(bf)
    a = np.zeros((1, 2), np.float32); a[0, 0] = 1
    c["c1a"] = a.astype(bf)
    a = np.zeros((1, 2), np.float32); a[0, 1] = 1
    c["c1b"] = a.astype(bf)
    c["ones64"] = np.ones((128, 64), np.float32).astype(bf)
    c["sc_k"] = np.array([[-1.0 / 64], [1.0]], np.float32)
    c["sc_q"] = np.array([[1.0], [-1.0 / 64]], np.float32)
    imm = np.array([1e-5, -0.5, 1.0 / 1024, 1.0 / 32, 32.0, 8.0, -1.0, 0.0],
                   np.float32)
    c["immc"] = np.tile(imm[None, :], (128, 1))
    return c


def host_phase_consts():
    import ml_dtypes
    bf = ml_dtypes.bfloat16
    c = {}
    hr = np.zeros((16, 16 * 64), np.float32)
    for h in range(16):
        hr[h, h * 64:(h + 1) * 64] = 1
    c["hrep"] = hr.astype(bf)
    lm = np.zeros((128, 4 * 512), np.float32)
    for v in range(4):
        for p in range(128):
            lm[p, v * 512 + 128 * v + p:(v + 1) * 512] = 1
    c["l1m"] = lm.astype(bf)
    hs = np.zeros((128, 8 * 16), np.float32)
    ht = np.zeros((16, 8 * 128), np.float32)
    for cc in range(8):
        for p in range(128):
            h = 2 * cc + (1 if p >= 64 else 0)
            hs[p, cc * 16 + h] = 1
            ht[h, cc * 128 + p] = 1
    c["hsel"] = hs.astype(bf)
    c["hselT"] = ht.astype(bf)
    lm0 = np.zeros((128, 4 * 256), np.float32)
    for jc in range(4):
        for p in range(128):
            jl = jc * 128 + p
            lo, hi = max(0, jl - 256), min(255, jl)
            if lo <= hi:
                lm0[p, jc * 256 + lo: jc * 256 + hi + 1] = 1
    c["l0m"] = lm0.astype(bf)
    dh, wnd, nn = 64, 256, 2048
    half = np.arange(0, dh, 2, dtype=np.float64)
    inv_freq = 1.0 / 10000.0 ** (half / dh)
    t = np.arange(nn, dtype=np.float64)
    fr = t[:, None] * inv_freq[None, :]
    base = (half + 0.4 * dh) / (1.4 * dh)
    power = (t - nn // 2) / (wnd // 2)
    sc = base[None, :] ** power[:, None]
    cosf = np.cos(fr); sinf = np.sin(fr)
    qc = np.concatenate([cosf * sc, cosf * sc], 1).T * 8
    kc_ = np.concatenate([cosf / sc, cosf / sc], 1).T
    qsn = np.concatenate([-sinf * sc, sinf * sc], 1).T * 8
    ksn = np.concatenate([-sinf / sc, sinf / sc], 1).T
    def swap32(t):
        return np.vstack([t[32:], t[:32]])
    c["rtAq"] = np.ascontiguousarray(np.vstack([qc, qc]).astype(np.float32)).astype(bf)
    c["rtAk"] = np.ascontiguousarray(np.vstack([kc_, kc_]).astype(np.float32)).astype(bf)
    c["rtBq"] = np.ascontiguousarray(np.vstack([swap32(qsn), swap32(qsn)]).astype(np.float32)).astype(bf)
    c["rtBk"] = np.ascontiguousarray(np.vstack([swap32(ksn), swap32(ksn)]).astype(np.float32)).astype(bf)
    return c


class Ctx:
    pass


def build_base(nc):
    P = Prog(nc)
    ar = Arena(nc)
    cx = Ctx()
    cx.nc, cx.P, cx.ar = nc, P, ar
    cx.pbig = [nc.alloc_psum_tensor(f"pbig{i}", [128, 1024], F32) for i in range(2)]
    cx.psm = [nc.alloc_psum_tensor(f"psm{i}", [128, 512], F32) for i in range(4)]

    cx.xT = ar.alloc("xT", [128, NCH * CW], F32)
    cx.xb = ar.alloc("xb", [128, NCH * CW], BF16)

    xoff = ar.off("xT")
    cx.subout = ar.alloc_at("subout", [128, NCH * N], BF16, xoff)
    cx.OT = ar.alloc_at("OT", [128, NCH * N], BF16, xoff + NCH * N * 2)

    hc = host_consts()
    cx.C = {}
    cx.const_evs = []
    for name, arr in hc.items():
        dt = F32 if arr.dtype == np.float32 else BF16
        dr = nc.dram_tensor("c_" + name, list(arr.shape), dt, kind="ExternalInput").ap()
        sb = ar.alloc("s_" + name, list(arr.shape), dt)
        ev = P.step("sync", lambda eng, dr=dr, sb=sb: eng.dma_start(out=sb[:, :], in_=dr[:, :]),
                    inc="DS", amt=16)
        cx.C[name] = sb
        cx.const_evs.append(ev)
    cx.host_const_arrays = hc

    cx.ln_xs = [ar.alloc(f"ln_xs{i}", [128, N], BF16) for i in range(2)]
    cx.ln_sq = [ar.alloc(f"ln_sq{i}", [128, N], BF16) for i in range(2)]
    f12 = ar.alloc("ln_f12", [128, 512], F32)
    cx.ln_f1 = f12
    cx.ln_f2 = f12
    cx.f1ap = f12[0:64, :]
    cx.f2ap = f12[64:128, :]

    cx.vR = ar.alloc("vR", [128, NSC * 128], BF16)
    cx.Vc = ar.alloc("Vc", [128, NSC * 128], BF16)
    cx.ln_tmp = [ar.alloc_at(f"ln_tmp{i}", [128, 512], F32, ar.off("Vc") + i * 2048) for i in range(2)]
    cx.xo_t = [ar.alloc_at(f"xo_t{i}", [128, 512], F32, ar.off("vR") + i * 2048) for i in range(2)]
    cx.trec = ar.alloc_at("trec", [64, 512], F32, ar.off("ln_f12"))
    wso = ar.limit - 6144
    cx.ws = {m: ar.alloc_at(f"ws_{m}", [128, 8 * 128], BF16, wso + i * 2048)
             for i, m in enumerate(("q", "k", "v"))}
    cx.wos = [ar.alloc_at("wos0", [128, 8 * 128], BF16, wso),
              ar.alloc_at("wos1", [128, 8 * 128], BF16, wso + 2048)]
    cx.Ebuf = [ar.alloc(f"Ebuf{i}", [128, 1024], BF16) for i in range(2)]
    cx.wev = {m: None for m in ("q", "k", "v")}
    cx.strip_consumed = {m: None for m in ("q", "k", "v")}
    cx.psm_free = [None] * 4
    cx.pb_free = [None, None]

    cx.b0_bq32 = ar.alloc("b0_bq32", [128, NCH], F32)
    cx.b0_bkc = ar.alloc("b0_bkc", [128, NCH], F32)
    cx.b0_boc = ar.alloc("b0_boc", [128, NCH], F32)
    cx.b0_bvc = ar.alloc("b0_bvc", [128, NCH], F32)
    cx.b0_boc2 = ar.alloc("b0_boc2", [128, NCH], F32)
    cx.b0_ncacc = ar.alloc("b0_ncacc", [128, 32], F32)
    cx.b0_nc16 = ar.alloc("b0_nc16", [128, 16], F32)
    cx.b0_cln16 = ar.alloc("b0_cln16", [128, 16], F32)
    cx.b0_c16 = ar.alloc("b0_c16", [128, 16], F32)
    cx.b1_stash = {}
    for nm, wdt in (("qsc", 6), ("ksc", 6), ("lng", 48), ("lnb", 48),
                    ("b1a", 132), ("b1g", 132), ("b2s", 48)):
        cx.b1_stash[nm] = ar.alloc("b1s_" + nm, [128, wdt], F32)
    cx.b3_bb = ar.alloc("b3_bc", [128, 4], F32)
    cx.b3_bb2 = ar.alloc("b3_bc2", [128, 4], F32)
    cx.ln_gbt = ar.alloc_at("ln_gbt", [128, 512], F32, ar.off("ln_sq1"))
    cx.b1_t0 = ar.alloc_at("b1_t0", [128, 512], BF16, ar.off("ln_xs1"))
    cx.b1_t1 = ar.alloc_at("b1_t1", [128, 512], BF16, ar.off("ln_xs1") + 1024)
    cx.b1_t2 = ar.alloc_at("b1_t2", [128, 512], BF16, ar.off("ln_xs1") + 2048)

    pc = host_phase_consts()
    cx.host_pc_arrays = pc
    cx.PCdr = {}
    for name, arr in pc.items():
        cx.PCdr[name] = nc.dram_tensor("pc_" + name, list(arr.shape), BF16,
                                       kind="ExternalInput").ap()
    cx.R_OFF = (ar.top + 63) // 64 * 64
    cx.R_END = ar.limit - 6144
    cx.sqb = ar.alloc_at("sqb", [64, N], BF16, cx.R_OFF + 65536)
    cx.b1_sqq = ar.alloc_at("b1_sqq", [128, 512], BF16, cx.R_OFF + 66048)
    cx.b1_q2c = ar.alloc_at("b1_q2c", [16, 512], F32, cx.R_OFF + 67072)
    cx.b1_rfq = ar.alloc_at("b1_rfq", [16, 512], BF16, cx.R_OFF + 69120)
    cx.b1_rfq2 = ar.alloc_at("b1_rfq2", [16, 512], BF16, cx.R_OFF + 70144)
    cx.pc_free_ev = None
    cx.bs_free_ev = None
    return cx


def load_pc(cx, names, waits=()):
    off = cx.R_OFF
    out = {}
    evs = []
    for name in names:
        arr = cx.host_pc_arrays[name]
        sb = cx.ar.alloc_at(f"pc_{name}_{len(cx.ar.tensors)}", list(arr.shape), BF16, off)
        off += Arena._size(list(arr.shape), BF16)
        off = (off + 63) // 64 * 64
        ev = cx.P.step("sync",
                       lambda eng, sb=sb, dr=cx.PCdr[name]: eng.dma_start(out=sb[:, :], in_=dr[:, :]),
                       waits=list(waits), inc="DS", amt=16)
        out[name] = sb
        evs.append(ev)
    return out, evs


def strip_src(w, c):
    return w.rearrange("(kc p) (mc m) -> p kc mc m", p=128, m=128)[:, :, c, :]


def load_strip(cx, m, wdr, c):
    dst = cx.ws[m][:, :].rearrange("p (kc q) -> p kc q", q=128)
    ev = cx.P.step("gpsimd",
                   lambda eng, t=dst, s=strip_src(wdr, c): eng.dma_start(out=t, in_=s),
                   waits=[cx.strip_consumed[m]], inc="DG", amt=16)
    cx.wev[m] = ev
    return ev


def emit_outproj(cx, wo, bias_col, OT_last_ev, strip_war, pevs, bv_col=None,
                 bias_out=None):
    P, C = cx.P, cx.C
    wo_evs = {}
    wo_consumed = [None, None]

    def load_wo_strip(ec):
        dst = cx.wos[ec % 2][:, :].rearrange("p (kc q) -> p kc q", q=128)
        war = [wo_consumed[ec % 2]]
        if ec < 2:
            war += strip_war
        wo_evs[ec] = P.step("gpsimd",
                            lambda eng, d=dst, ec=ec: eng.dma_start(out=d, in_=strip_src(wo, ec)),
                            waits=war, inc="DG", amt=16)

    load_wo_strip(0)
    load_wo_strip(1)
    sub_evs = []
    bo_evs = {}
    for ec in range(8):
        if bv_col is not None:
            for dc in range(NCH):
                bmm = P.step("tensor",
                             lambda eng, ec=ec, dc=dc: eng.matmul(
                                 cx.pbig[0][:, 0:1],
                                 cx.wos[ec % 2][:, dc * 128:(dc + 1) * 128],
                                 bv_col[:, dc:dc + 1],
                                 start=(dc == 0), stop=(dc == NCH - 1)),
                             waits=[wo_evs[ec], cx.pb_free[0]] + pevs if dc == 0 else (),
                             inc="T" if dc == NCH - 1 else None)
            bo_evs[ec] = P.step("vector",
                                lambda eng, ec=ec: eng.tensor_tensor(
                                    bias_out[:, ec:ec + 1], bias_col[:, ec:ec + 1],
                                    cx.pbig[0][:, 0:1], OP.add),
                                waits=[bmm], inc="V")
            cx.pb_free[0] = bo_evs[ec]
        for it in range(NIT):
            w0 = [wo_evs[ec], cx.psm_free[it]] + ([OT_last_ev] if ec == 0 and it == 0 else [])
            for dc in range(NCH):
                ev = P.step("tensor",
                            lambda eng, ec=ec, it=it, dc=dc: eng.matmul(
                                cx.psm[it][:, :],
                                cx.wos[ec % 2][:, dc * 128:(dc + 1) * 128],
                                cx.OT[:, dc * N + it * 512: dc * N + (it + 1) * 512],
                                start=(dc == 0), stop=(dc == NCH - 1)),
                            waits=w0 if dc == 0 else (),
                            inc="T" if dc == NCH - 1 else None)
            mm_done = ev
            if bv_col is not None:
                sev = P.step("scalar",
                             lambda eng, ec=ec, it=it: eng.activation(
                                 cx.subout[:, ec * N + it * 512: ec * N + (it + 1) * 512],
                                 cx.psm[it][:, :], AF.Identity, bias=bias_out[:, ec:ec + 1]),
                             waits=[mm_done, bo_evs[ec]] + pevs, inc="S")
            elif bias_col is not None:
                sev = P.step("scalar",
                             lambda eng, ec=ec, it=it: eng.activation(
                                 cx.subout[:, ec * N + it * 512: ec * N + (it + 1) * 512],
                                 cx.psm[it][:, :], AF.Identity, bias=bias_col[:, ec:ec + 1]),
                             waits=[mm_done] + pevs, inc="S")
            else:
                sev = P.step("scalar",
                             lambda eng, ec=ec, it=it: eng.activation(
                                 cx.subout[:, ec * N + it * 512: ec * N + (it + 1) * 512],
                                 cx.psm[it][:, :], AF.Copy),
                             waits=[mm_done] + pevs, inc="S")
            cx.psm_free[it] = sev
            sub_evs.append(sev)
        wo_consumed[ec % 2] = mm_done
        if ec + 2 < 8:
            load_wo_strip(ec + 2)
    cx.strip_consumed["q"] = wo_consumed[0]
    cx.strip_consumed["k"] = wo_consumed[1]
    return sub_evs


def emit_init(cx):
    P = cx.P
    xTz = cx.xT[:, :].rearrange("p (c w) -> p c w", w=CW)[:, :, 0:1]
    xbz = cx.xb[:, :].rearrange("p (c w) -> p c w", w=CW)[:, :, 0:1]
    e1 = P.step("vector", lambda eng: eng.memset(xTz, 0.0), inc="V")
    e2 = P.step("vector", lambda eng: eng.memset(xbz, 0.0), inc="V")
    cx.init_evs = [e1, e2] + cx.const_evs


def emit_load_x(cx):
    P, nc = cx.P, cx.nc
    xin = nc.dram_tensor("xT_in", [D, N], F32, kind="ExternalInput").ap()
    cx.xin = xin
    src = xin.rearrange("(c p) n -> p c n", p=128)
    dst = cx.xT[:, :].rearrange("p (c w) -> p c w", w=CW)[:, :, 1:]
    cx.x_ready = P.step("sync", lambda eng: eng.dma_start(out=dst, in_=src),
                        waits=cx.init_evs, inc="DS", amt=16)


def emit_spill_x(cx, waits):
    P, nc = cx.P, cx.nc
    if not hasattr(cx, "xspill"):
        cx.xspill = nc.dram_tensor("xspill", [D, N], F32, kind="Internal").ap()
    dst = cx.xspill.rearrange("(c p) n -> p c n", p=128)
    srcv = cx.xT[:, :].rearrange("p (c w) -> p c w", w=CW)[:, :, 1:]
    ev = P.step("sync", lambda eng: eng.dma_start(out=dst, in_=srcv),
                waits=list(waits), inc="DS", amt=16)
    cx.xgate = ev
    return cx.xspill, ev


def emit_store_x(cx, wait_evs):
    P, nc = cx.P, cx.nc
    xout = nc.dram_tensor("xT_out", [D, N], F32, kind="ExternalOutput").ap()
    dst = xout.rearrange("(c p) n -> p c n", p=128)
    src = cx.xT[:, :].rearrange("p (c w) -> p c w", w=CW)[:, :, 1:]
    ev = P.step("sync", lambda eng: eng.dma_start(out=dst, in_=src),
                waits=wait_evs, inc="DS", amt=16)
    P.step("sync", lambda eng: eng.nop(), waits=[ev])
    return ev


def emit_dbg(cx, sb_ap, shape, dtype, waits):
    P, nc = cx.P, cx.nc
    dr = nc.dram_tensor("dbg", list(shape), dtype, kind="ExternalOutput").ap()
    ev = P.step("sync", lambda eng: eng.dma_start(out=dr[:, :], in_=sb_ap), waits=waits,
                inc="DS", amt=16)
    P.step("sync", lambda eng: eng.nop(), waits=[ev])
    return ev


def emit_ln(cx, kind, src_ready, shifted=False, gain_bias=None,
            resid_dram=None, out_xb=False, out_xT=False, sub_nt=N, tag="",
            scratch_free=None):
    P, C = cx.P, cx.C
    pstat = cx.psm[:4] if kind == "_unused" else None
    pstat = cx.psm
    prepI, prepM = cx.pbig[0], cx.pbig[1]

    def src_f32(c, sh):
        base = c * CW + 1 - (1 if sh else 0)
        return cx.xT[:, base: base + N]

    def src_bf(c, sh):
        assert not sh
        return cx.subout[:, c * N: c * N + N]

    mm_evs = [None] * NIT
    last_mm_per_buf = [None, None]
    xs_ev = [None, None]
    sq_ev = [None, None]
    stat_done = [None] * NIT
    for c in range(NCH):
        sh = shifted and c >= 4
        b = 0
        if kind in ("pre", "postx"):
            xsb = cx.ln_xs[b]
            xs_ev[b] = P.step("scalar",
                              lambda eng, o=xsb, i=src_f32(c, sh): eng.activation(o[:, :], i, AF.Copy),
                              waits=[cx.x_ready, last_mm_per_buf[b]] + list(src_ready), inc="S")
            xsrc = xsb
            xsl = lambda it, t=xsb: t[:, it * 512:(it + 1) * 512]
            sqi = src_f32(c, sh)
        else:
            xsl = lambda it, c=c, sh=sh: src_bf(c, sh)[:, it * 512:(it + 1) * 512]
            sqi = src_bf(c, sh)
        sqb = cx.ln_sq[b]
        sq_ev[b] = P.step("scalar",
                          lambda eng, o=sqb, i=sqi: eng.activation(o[:, :], i, AF.Square),
                          waits=[last_mm_per_buf[b]] + list(src_ready), inc="S")
        for it in range(NIT):
            w = [xs_ev[b], sq_ev[b]]
            if c == 0:
                w += [("V", P.cnt.get("V", 0))] if False else []
            P.step("tensor",
                   lambda eng, ps=pstat[it], x=xsl(it), c=c: eng.matmul(
                       ps[:, :], C["statA"][:, :], x, start=(c == 0), stop=False),
                   waits=w)
            ev = P.step("tensor",
                        lambda eng, ps=pstat[it], s=sqb, it=it, c=c: eng.matmul(
                            ps[:, :], C["statB"][:, :], s[:, it * 512:(it + 1) * 512],
                            start=False, stop=(c == NCH - 1)),
                        inc="T")
            mm_evs[it] = ev
        last_mm_per_buf[b] = mm_evs[NIT - 1]
    for it in range(NIT):
        stat_done[it] = mm_evs[it]

    out_evs = []
    eI = [None] * NIT
    eM = [None] * NIT
    prev_eI = None
    prev_eM = None
    for it in range(NIT):
        pst = pstat[it]
        e1 = P.step("scalar",
                    lambda eng, pst=pst: eng.activation(cx.f1ap, pst[0:64, :],
                                                        AF.Square, scale=C["immc"][0:64, 2:3]),
                    waits=[stat_done[it], prev_eM], inc="S")
        e2 = P.step("vector",
                    lambda eng, pst=pst: eng.scalar_tensor_tensor(
                        cx.f2ap, pst[64:128, :], 1.0 / D, cx.f1ap,
                        OP.mult, OP.subtract),
                    waits=[e1, prev_eI], inc="V")
        e3 = P.step("scalar",
                    lambda eng: eng.activation(cx.f1ap, cx.f2ap,
                                               AF.Ln, bias=C["immc"][0:64, 0:1]),
                    waits=[e2], inc="S")
        e4 = P.step("scalar",
                    lambda eng: eng.activation(cx.f2ap, cx.f1ap,
                                               AF.Exp, scale=C["immc"][0:64, 1:2]),
                    inc="S")
        e5 = P.step("vector",
                    lambda eng, pst=pst: eng.scalar_tensor_tensor(
                        cx.f1ap, pst[0:64, :], 1.0 / D, cx.f2ap,
                        OP.mult, OP.mult),
                    waits=[e4], inc="V")
        rI = cx.pbig[it // 2][:, (it % 2) * 512:(it % 2) * 512 + 512]
        eI[it] = P.step("tensor",
                        lambda eng, rI=rI: eng.matmul(rI, C["rep64f"][64:128, :],
                                                      cx.f2ap, start=True, stop=True),
                        waits=[e4], inc="T")
        eM[it] = P.step("tensor",
                        lambda eng, pst=pst: eng.matmul(pst[:, :], C["rep64f"][0:64, :],
                                                        cx.f1ap, start=True, stop=True),
                        waits=[e5, e2, e1], inc="T")
        prev_eI, prev_eM = eI[it], eM[it]

    c_order = list(range(NCH))
    it_order = list(range(NIT))
    if kind == "post":
        c_order = list(reversed(c_order))
        it_order = list(reversed(it_order))
    tmp_ev = [None, None]
    cix = 0
    for c in c_order:
        for it in it_order:
            sh = shifted and c >= 4
            tb = cx.ln_tmp[cix % 2]
            rI = cx.pbig[it // 2][:, (it % 2) * 512:(it % 2) * 512 + 512]
            rM = pstat[it]
            if kind == "pre":
                x_sl = src_f32(c, sh)[:, it * 512:(it + 1) * 512]
                a1 = P.step("vector",
                            lambda eng, tb=tb, x=x_sl, rI=rI: eng.tensor_tensor(
                                tb[:, :], x, rI, OP.mult),
                            waits=[eI[it], tmp_ev[cix % 2]], inc="V")
                dst = cx.xb[:, c * CW + 1 + it * 512: c * CW + 1 + (it + 1) * 512]
                d2 = cx.ln_gbt[:, :] if gain_bias is not None else dst
                a2 = P.step("vector",
                            lambda eng, tb=tb, d=d2, rM=rM: eng.tensor_tensor(
                                d, tb[:, :], rM[:, :], OP.subtract),
                            waits=[eM[it]] + ([out_evs[-1]] if gain_bias is not None and out_evs else []),
                            inc="V")
                tmp_ev[cix % 2] = a2
                out_evs.append(a2)
                if gain_bias is not None:
                    g_sb, b_sb = gain_bias
                    a3 = P.step("scalar",
                                lambda eng, d=dst, c=c: eng.activation(
                                    d, cx.ln_gbt[:, :], AF.Identity, bias=b_sb[:, c:c + 1],
                                    scale=g_sb[:, c:c + 1]),
                                waits=[a2], inc="S")
                    out_evs.append(a3)
                if out_xT:
                    xd = src_f32(c, False)[:, it * 512:(it + 1) * 512]
                    a4 = P.step("vector",
                                lambda eng, tb=tb, xd=xd, rM=rM: eng.tensor_tensor(
                                    xd, tb[:, :], rM[:, :], OP.subtract),
                                waits=[eM[it], getattr(cx, "xgate", None)], inc="V")
                    tmp_ev[cix % 2] = a4
                    out_evs.append(a4)
            else:
                if kind == "postx":
                    s_sl = src_f32(c, False)[:, it * 512:(it + 1) * 512]
                else:
                    s_sl = src_bf(c, False)[:, it * 512:(it + 1) * 512]
                a1 = P.step("vector",
                            lambda eng, tb=tb, s=s_sl, rI=rI: eng.tensor_tensor(
                                tb[:, :], s, rI, OP.mult),
                            waits=[eI[it], tmp_ev[cix % 2]], inc="V")
                a2 = P.step("vector",
                            lambda eng, tb=tb, rM=rM: eng.tensor_tensor(
                                tb[:, :], tb[:, :], rM[:, :], OP.subtract),
                            waits=[eM[it]], inc="V")
                xo = cx.xo_t[cix % 2]
                src = resid_dram.rearrange("(c p) n -> p c n", p=128)[:, c, it * 512:(it + 1) * 512]
                d1 = P.step("sync",
                            lambda eng, xo=xo, src=src: eng.dma_start(out=xo[:, :], in_=src),
                            waits=[tmp_ev[cix % 2] or scratch_free], inc="DS", amt=16)
                dst = src_f32(c, False)[:, it * 512:(it + 1) * 512]
                a3 = P.step("vector",
                            lambda eng, tb=tb, xo=xo, d=dst: eng.tensor_tensor(
                                d, tb[:, :], xo[:, :], OP.add),
                            waits=[d1], inc="V")
                tmp_ev[cix % 2] = a3
                out_evs.append(a3)
            cix += 1
    return out_evs


def emit_b0(cx, weights_ready, resid_dram):
    P, nc, C, ar = cx.P, cx.nc, cx.C, cx.ar

    wq = nc.dram_tensor("b0_wqT", [D, D], F32, kind="ExternalInput").ap()
    wk = nc.dram_tensor("b0_wkT", [D, D], F32, kind="ExternalInput").ap()
    wv = nc.dram_tensor("b0_wvT", [D, D], F32, kind="ExternalInput").ap()
    wo = nc.dram_tensor("b0_woT", [D, D], F32, kind="ExternalInput").ap()
    bqs = nc.dram_tensor("b0_bqs", [128, NCH], F32, kind="ExternalInput").ap()
    bks = nc.dram_tensor("b0_bks", [128, NCH], F32, kind="ExternalInput").ap()
    bos = nc.dram_tensor("b0_bos", [128, NCH], F32, kind="ExternalInput").ap()
    bvs = nc.dram_tensor("b0_bvs", [128, NCH], F32, kind="ExternalInput").ap()

    qaug = cx.ln_xs
    kaug = cx.ln_sq
    vR = cx.vR
    Vc = cx.Vc
    sqb = cx.sqb
    trec = cx.trec
    ws = cx.ws
    ST = ar.alloc_at("b0_ST", [128, NSC * N], BF16, cx.R_OFF)
    ncacc, nc16, cln16, c16 = cx.b0_ncacc, cx.b0_nc16, cx.b0_cln16, cx.b0_c16
    cx.b0_r1_prev = None
    bq32, bkc, boc = cx.b0_bq32, cx.b0_bkc, cx.b0_boc
    bvc, boc2 = cx.b0_bvc, cx.b0_boc2

    pevs = []
    pevs.append(P.step("sync", lambda eng: eng.dma_start(out=bq32[:, :], in_=bqs[:, :]), inc="DS", amt=16))
    pevs.append(P.step("sync", lambda eng: eng.dma_start(out=bkc[:, :], in_=bks[:, :]), inc="DS", amt=16))
    pevs.append(P.step("sync", lambda eng: eng.dma_start(out=boc[:, :], in_=bos[:, :]), inc="DS", amt=16))
    bvcb = ar.alloc_at("b0_bvcb", [128, NCH], BF16, ar.off("b0_bvc") + 32)
    pevs.append(P.step("gpsimd", lambda eng: eng.dma_start(out=bvcb[:, :], in_=bvs[:, :]), inc="DG", amt=16))
    bq_ev = P.step("vector", lambda eng: eng.tensor_scalar_mul(bq32[:, :], bq32[:, :], 1.0 / 32),
                   waits=pevs[:1], inc="V")
    pevs.append(bq_ev)

    ln1_evs = emit_ln(cx, "pre", src_ready=weights_ready, tag="b0ln1")

    ln1_done = [ln1_evs[-1], ln1_evs[-2]]
    xb_full = ln1_evs

    wev = cx.wev
    strip_consumed = cx.strip_consumed

    for m, wdr in (("q", wq), ("k", wk), ("v", wv)):
        load_strip(cx, m, wdr, 0)


    pb_free = cx.pb_free
    psm_free = cx.psm_free
    OT_evs = []
    proj_read_xb = []
    Vc_free = None
    sq_free = None
    q2_parity = 0

    ST_T_budget = None

    for c in range(NCH):
        b = 0
        evac_evs = []
        for (mat, augA, augB, bias, scale) in (
                ("q", qaug[0], qaug[1], bq32, "r32"),
                ("k", kaug[0], kaug[1], bkc, None)):
            for it in range(NIT):
                mm_w = [wev[mat], psm_free[it]] + (xb_full if c == 0 and it == 0 else [])
                for kc in range(NCH):
                    ev = P.step("tensor",
                                lambda eng, it=it, kc=kc, mat=mat, b=b, c=c: eng.matmul(
                                    cx.psm[it][:, :],
                                    ws[mat][:, kc * 128:(kc + 1) * 128],
                                    cx.xb[:, kc * CW + 1 + it * 512: kc * CW + 1 + (it + 1) * 512],
                                    start=(kc == 0), stop=(kc == NCH - 1)),
                                waits=mm_w if kc == 0 else (),
                                inc="T" if kc == NCH - 1 else None)
                    if kc == NCH - 1:
                        mm_done = ev
                proj_read_xb.append(mm_done)
                for par, aug in ((0, augA), (1, augB)):
                    eev = P.step("scalar",
                                 lambda eng, it=it, par=par, aug=aug, bias=bias, c=c, scale=scale: eng.activation(
                                     aug[0:64, it * 512:(it + 1) * 512],
                                     cx.psm[it][par * 64:(par + 1) * 64, :],
                                     AF.Identity, bias=bias[par * 64:(par + 1) * 64, c:c + 1],
                                     scale=(C["immc"][par * 64:(par + 1) * 64, 3:4]
                                            if scale == "r32" else 1.0)),
                                 waits=[mm_done] + pevs, inc="S")
                    evac_evs.append(eev)
                psm_free[it] = evac_evs[-1]
        strip_consumed["q"] = evac_evs[7]
        strip_consumed["k"] = evac_evs[-1]
        v_evs = []
        for g in range(4):
            it = g
            grp_w = [wev["v"], psm_free[it], pevs[3]]
            for sci in range(4):
                sc = g * 4 + sci
                for kc in range(NCH):
                    ev = P.step("tensor",
                                lambda eng, it=it, sci=sci, sc=sc, kc=kc, b=b: eng.matmul(
                                    cx.psm[it][:, sci * 128:(sci + 1) * 128],
                                    cx.xb[:, kc * CW + 1 + sc * 128: kc * CW + 1 + (sc + 1) * 128],
                                    ws["v"][:, kc * 128:(kc + 1) * 128],
                                    start=(kc == 0 and sci == 0),
                                    stop=(kc == NCH - 1 and sci == 3),
                                    skip_group_check=True),
                                waits=grp_w if (sci == 0 and kc == 0) else (),
                                inc="T" if (kc == NCH - 1 and sci == 3) else None)
            grp_done = ev
            for sci in range(4):
                sc = g * 4 + sci
                dst = vR[:, sc * 128:(sc + 1) * 128]
                vev = P.step("scalar",
                             lambda eng, it=it, sci=sci, d=dst: eng.activation(
                                 d, cx.psm[it][:, sci * 128:(sci + 1) * 128], AF.Copy),
                             waits=[grp_done], inc="S")
                v_evs.append(vev)
            psm_free[it] = v_evs[-1]
        strip_consumed["v"] = v_evs[-1]
        v_done = v_evs[-1]
        if c + 1 < NCH:
            for m, wdr in (("q", wq), ("k", wk), ("v", wv)):
                load_strip(cx, m, wdr, c + 1)

        for par in range(2):
            h = 2 * c + par
            qa, ka = qaug[par], kaug[par]
            sq1 = P.step("scalar",
                         lambda eng, qa=qa: eng.activation(sqb[:, :], qa[0:64, :], AF.Square, scale=C["immc"][0:64, 4:5]),
                         waits=[evac_evs[-1], sq_free], inc="S")
            q2evs = []
            for half in range(2):
                P.step("tensor",
                       lambda eng, half=half: eng.matmul(
                           cx.pbig[0][0:2, half * 512:(half + 1) * 512],
                           C["c1a"][:, :], C["ones1r"][0:1, 0:512],
                           start=True, stop=False),
                       waits=[sq1, pb_free[0]] if half == 0 else ())
                ev = P.step("tensor",
                            lambda eng, half=half: eng.matmul(
                                cx.pbig[0][0:2, half * 512:(half + 1) * 512],
                                C["c64b"][:, :], sqb[:, half * 512: (half + 1) * 512],
                                start=False, stop=True),
                            inc="T")
                q2evs.append(ev)
            q2evs2 = []
            for half in range(2):
                P.step("tensor",
                       lambda eng, half=half: eng.matmul(
                           cx.pbig[1][0:2, half * 512:(half + 1) * 512],
                           C["c1a"][:, :], C["ones1r"][0:1, 0:512],
                           start=True, stop=False),
                       waits=[sq1, pb_free[1]] if half == 0 else ())
                ev = P.step("tensor",
                            lambda eng, half=half: eng.matmul(
                                cx.pbig[1][0:2, half * 512:(half + 1) * 512],
                                C["c64b"][:, :], sqb[:, 1024 + half * 512: 1024 + (half + 1) * 512],
                                start=False, stop=True),
                            inc="T")
                q2evs2.append(ev)
            eq = P.step("scalar",
                        lambda eng, qa=qa: eng.activation(
                            qa[64:66, 0:1024], cx.pbig[0][0:2, :], AF.Copy,
                            scale=C["sc_q"][:, 0:1]),
                        waits=q2evs, inc="S")
            eq2 = P.step("scalar",
                         lambda eng, qa=qa: eng.activation(
                             qa[64:66, 1024:2048], cx.pbig[1][0:2, :], AF.Copy,
                             scale=C["sc_q"][:, 0:1]),
                         waits=q2evs2, inc="S")
            sq2 = P.step("scalar",
                         lambda eng, ka=ka: eng.activation(sqb[:, :], ka[0:64, :], AF.Square),
                         waits=[eq, eq2] + q2evs + q2evs2, inc="S")
            k2evs = []
            for (pb, base) in ((0, 0), (1, 1024)):
                for half in range(2):
                    P.step("tensor",
                           lambda eng, pb=pb, half=half: eng.matmul(
                               cx.pbig[pb][0:2, half * 512:(half + 1) * 512],
                               C["c1b"][:, :], C["ones1r"][0:1, 0:512],
                               start=True, stop=False),
                           waits=[sq2] if half == 0 else ())
                    ev = P.step("tensor",
                                lambda eng, pb=pb, base=base, half=half: eng.matmul(
                                    cx.pbig[pb][0:2, half * 512:(half + 1) * 512],
                                    C["c64a"][:, :], sqb[:, base + half * 512: base + (half + 1) * 512],
                                    start=False, stop=True),
                                inc="T")
                    k2evs.append(ev)
            ek = P.step("scalar",
                        lambda eng, ka=ka: eng.activation(
                            ka[64:66, 0:1024], cx.pbig[0][0:2, :], AF.Copy,
                            scale=C["sc_k"][:, 0:1]),
                        waits=k2evs[:2], inc="S")
            ek2 = P.step("scalar",
                         lambda eng, ka=ka: eng.activation(
                             ka[64:66, 1024:2048], cx.pbig[1][0:2, :], AF.Copy,
                             scale=C["sc_k"][:, 0:1]),
                         waits=k2evs[2:], inc="S")
            sq_free = ek2
            pb_free[0] = ek
            pb_free[1] = ek2

            exp_evs = []
            stq_waits = [eq, eq2, ek, ek2]
            for jc in range(NSC):
                for tp in range(2):
                    pbi = (jc * 2 + tp) % 2
                    w = [pb_free[pbi]] + (stq_waits if jc == 0 and tp == 0 else [])
                    P.step("tensor",
                           lambda eng, pbi=pbi, jc=jc, tp=tp, ka=ka, qa=qa: eng.matmul(
                               cx.pbig[pbi][:, 0:512],
                               ka[0:66, jc * 128:(jc + 1) * 128],
                               qa[0:66, tp * 1024: tp * 1024 + 512],
                               start=True, stop=True),
                           waits=w)
                    mev = P.step("tensor",
                                 lambda eng, pbi=pbi, jc=jc, tp=tp, ka=ka, qa=qa: eng.matmul(
                                     cx.pbig[pbi][:, 512:1024],
                                     ka[0:66, jc * 128:(jc + 1) * 128],
                                     qa[0:66, tp * 1024 + 512: tp * 1024 + 1024],
                                     start=True, stop=True),
                                 inc="T")
                    eev = P.step("scalar",
                                 lambda eng, pbi=pbi, jc=jc, tp=tp: eng.activation(
                                     ST[:, jc * N + tp * 1024: jc * N + (tp + 1) * 1024],
                                     cx.pbig[pbi][:, :], AF.Exp,
                                     accum_out=ncacc[:, jc * 2 + tp: jc * 2 + tp + 1]),
                                 waits=[mev] + ([cx.b0_r1_prev] if jc == 0 and tp == 0 else []),
                                 inc="S")
                    exp_evs.append(eev)
                    pb_free[pbi] = eev
            r1 = P.step("vector",
                        lambda eng: eng.tensor_reduce(
                            nc16[:, :], ncacc[:, :].rearrange("p (j t) -> p j t", t=2),
                            AX.X, OP.add),
                        waits=[exp_evs[-1], exp_evs[-2]], inc="V")
            cx.b0_r1_prev = r1
            r2 = P.step("scalar",
                        lambda eng: eng.activation(cln16[:, :], nc16[:, :], AF.Ln),
                        waits=[r1], inc="S")
            r3 = P.step("scalar",
                        lambda eng: eng.activation(c16[:, :], cln16[:, :], AF.Exp, scale=C["immc"][:, 1:2]),
                        inc="S")
            vc_evs = []
            for sc in range(NSC):
                vsrc = vR[:, sc * 128 + 64 * par: sc * 128 + 64 * par + 64]
                vev = P.step("vector",
                             lambda eng, sc=sc, vsrc=vsrc: eng.tensor_scalar(
                                 Vc[:, sc * 128:sc * 128 + 64], vsrc,
                                 c16[:, sc:sc + 1], None, OP.mult),
                             waits=[r3, v_done, Vc_free], inc="V")
                vev2 = P.step("vector",
                              lambda eng, sc=sc: eng.tensor_scalar(
                                  Vc[:, sc * 128 + 64:(sc + 1) * 128], C["ones64"][:, :],
                                  c16[:, sc:sc + 1], None, OP.mult),
                              inc="V")
                vc_evs.append(vev2)
            fin_evs = []
            for it in range(NIT):
                for jc in range(NSC):
                    ev = P.step("tensor",
                                lambda eng, it=it, jc=jc: eng.matmul(
                                    cx.psm[it][:, :],
                                    Vc[:, jc * 128:(jc + 1) * 128],
                                    ST[:, jc * N + it * 512: jc * N + (it + 1) * 512],
                                    start=(jc == 0), stop=(jc == NSC - 1)),
                                waits=([vc_evs[-1], psm_free[it]] + exp_evs[-1:]) if jc == 0 else (),
                                inc="T" if jc == NSC - 1 else None)
                pv_done = ev
                rc = P.step("vector",
                            lambda eng, it=it: eng.reciprocal(trec[:, :], cx.psm[it][64:128, :]),
                            waits=[pv_done], inc="V")
                fo = P.step("vector",
                            lambda eng, it=it, h=h: eng.tensor_tensor(
                                cx.OT[(h % 2) * 64:(h % 2) * 64 + 64,
                                      (h // 2) * N + it * 512:(h // 2) * N + (it + 1) * 512],
                                cx.psm[it][0:64, :], trec[:, :], OP.mult),
                            waits=[rc, getattr(cx, "xgate", None)], inc="V")
                psm_free[it] = fo
                fin_evs.append(fo)
                OT_evs.append(fo)
            Vc_free = fin_evs[-1]

    sub_evs = emit_outproj(cx, wo, boc, OT_evs[-1],
                           [strip_consumed["q"], strip_consumed["k"]], pevs,
                           bv_col=bvcb, bias_out=boc2)

    out_evs = emit_ln(cx, "post", src_ready=sub_evs[-8:], resid_dram=resid_dram, tag="b0ln2",
                      scratch_free=OT_evs[-1])
    return out_evs


def emit_b2(cx, resid_dram, src_ready=()):
    P, nc, C, ar = cx.P, cx.nc, cx.C, cx.ar
    wqkv = nc.dram_tensor("b2_wqkv", [D, 3 * D], F32, kind="ExternalInput").ap()
    wout = nc.dram_tensor("b2_wout", [D, D], F32, kind="ExternalInput").ap()
    glogd = nc.dram_tensor("b2_glog", [16, N], F32, kind="ExternalInput").ap()

    qaug = cx.ln_xs
    kaug = cx.ln_sq
    vR, Vc, ws = cx.vR, cx.Vc, cx.ws
    wev, strip_consumed = cx.wev, cx.strip_consumed
    psm_free, pb_free = cx.psm_free, cx.pb_free
    gsig = getattr(cx, "b2_gsig", None)
    if gsig is None:
        gsig = ar.alloc_at("b2_gsig", [16, N], BF16, cx.R_OFF + 8192)
        glog_sb = ar.alloc_at("b2_glog", [16, N], BF16, cx.R_OFF + 12288)
        cx.b2_gsig, cx.b2_glog = gsig, glog_sb
    glog_sb = cx.b2_glog
    PCt, pc_evs = load_pc(cx, ["l1m", "hrep"], waits=[cx.pc_free_ev])

    gev = P.step("gpsimd", lambda eng: eng.dma_start(out=glog_sb[:, :], in_=glogd[:, :]),
                 waits=[cx.bs_free_ev], inc="DG", amt=16)
    sev = P.step("scalar", lambda eng: eng.activation(gsig[:, :], glog_sb[:, :], AF.Sigmoid),
                 waits=[gev], inc="S")

    ln1_evs = emit_ln(cx, "pre", src_ready=list(src_ready), tag="b2ln1")
    xb_full = ln1_evs

    vones = cx.Vc[:, :].rearrange("p (s b) -> p s b", b=128)[:, :, 64:]
    ones_ev = P.step("vector", lambda eng: eng.memset(vones, 1.0), inc="V")

    for m, off in (("q", 0), ("k", 8), ("v", 16)):
        load_strip(cx, m, wqkv, off)

    OT_evs = []
    Vc_free = None
    eb_free = [None, None]
    for c in range(NCH):
        evac_evs = []
        for (mat, augA, augB) in (("q", qaug[0], qaug[1]), ("k", kaug[0], kaug[1])):
            for it in range(NIT):
                mm_w = [wev[mat], psm_free[it]] + (xb_full if c == 0 and it == 0 else [])
                for kc in range(NCH):
                    ev = P.step("tensor",
                                lambda eng, it=it, kc=kc, mat=mat: eng.matmul(
                                    cx.psm[it][:, :],
                                    ws[mat][:, kc * 128:(kc + 1) * 128],
                                    cx.xb[:, kc * CW + 1 + it * 512: kc * CW + 1 + (it + 1) * 512],
                                    start=(kc == 0), stop=(kc == NCH - 1)),
                                waits=mm_w if kc == 0 else (),
                                inc="T" if kc == NCH - 1 else None)
                    if kc == NCH - 1:
                        mm_done = ev
                for par, aug in ((0, augA), (1, augB)):
                    eev = P.step("scalar",
                                 lambda eng, it=it, par=par, aug=aug: eng.activation(
                                     aug[0:64, it * 512:(it + 1) * 512],
                                     cx.psm[it][par * 64:(par + 1) * 64, :], AF.Copy),
                                 waits=[mm_done], inc="S")
                    evac_evs.append(eev)
                psm_free[it] = evac_evs[-1]
        strip_consumed["q"] = evac_evs[7]
        strip_consumed["k"] = evac_evs[-1]
        v_evs = []
        for g in range(4):
            it = g
            grp_w = [wev["v"], psm_free[it]]
            for sci in range(4):
                sc = g * 4 + sci
                for kc in range(NCH):
                    ev = P.step("tensor",
                                lambda eng, it=it, sci=sci, sc=sc, kc=kc: eng.matmul(
                                    cx.psm[it][:, sci * 128:(sci + 1) * 128],
                                    cx.xb[:, kc * CW + 1 + sc * 128: kc * CW + 1 + (sc + 1) * 128],
                                    ws["v"][:, kc * 128:(kc + 1) * 128],
                                    start=(kc == 0 and sci == 0), stop=(kc == NCH - 1 and sci == 3),
                                    skip_group_check=True),
                                waits=grp_w if (sci == 0 and kc == 0) else (),
                                inc="T" if (kc == NCH - 1) else None)
            grp_done = ev
            for sci in range(4):
                sc = g * 4 + sci
                vev = P.step("scalar",
                             lambda eng, it=it, sci=sci, sc=sc: eng.activation(
                                 vR[:, sc * 128:(sc + 1) * 128],
                                 cx.psm[it][:, sci * 128:(sci + 1) * 128], AF.Copy),
                             waits=[grp_done], inc="S")
                v_evs.append(vev)
            psm_free[it] = v_evs[-1]
        strip_consumed["v"] = v_evs[-1]
        v_done = v_evs[-1]
        if c + 1 < NCH:
            for m, off in (("q", 0), ("k", 8), ("v", 16)):
                load_strip(cx, m, wqkv, off + c + 1)

        for par in range(2):
            h = 2 * c + par
            qa, ka = qaug[par], kaug[par]
            gq_evs = []
            for it in range(NIT):
                gm = P.step("tensor",
                            lambda eng, it=it, h=h: eng.matmul(
                                cx.pbig[0][0:64, (it % 2) * 512:(it % 2) * 512 + 512],
                                PCt["hrep"][:, h * 64:(h + 1) * 64],
                                gsig[:, it * 512:(it + 1) * 512],
                                start=True, stop=True),
                            waits=[sev, evac_evs[-1], pc_evs[1],
                                   pb_free[0] if it < 2 else gq_evs[it - 2]],
                            inc="T")
                gq = P.step("vector",
                            lambda eng, it=it, qa=qa: eng.scalar_tensor_tensor(
                                qa[0:64, it * 512:(it + 1) * 512],
                                qa[0:64, it * 512:(it + 1) * 512], 0.125,
                                cx.pbig[0][0:64, (it % 2) * 512:(it % 2) * 512 + 512],
                                OP.mult, OP.mult),
                            waits=[gm], inc="V")
                gq_evs.append(gq)
            pb_free[0] = gq_evs[-1]
            vc_evs = []
            for sc in range(NSC):
                vev = P.step("vector",
                             lambda eng, sc=sc, par=par: eng.tensor_copy(
                                 Vc[:, sc * 128:sc * 128 + 64],
                                 vR[:, sc * 128 + 64 * par: sc * 128 + 64 * par + 64]),
                             waits=[ones_ev, v_done, Vc_free], inc="V")
                vc_evs.append(vev)
            fin_evs = []
            pv_last = {}
            glist = []
            for jc in range(NSC):
                its = list(range(jc // 4, NIT))
                glist += [(jc, gi, grp) for gi, grp in
                          enumerate(its[i:i + 2] for i in range(0, len(its), 2))]
            g_stash = {}

            def b2_st(gx):
                jc, gi, grp = glist[gx]
                pbi = gx % 2
                ebi = gx % 2
                stw = [pb_free[pbi]] + ([gq_evs[-1], vc_evs[-1]] if gx == 0 else [])
                ev = None
                for k2, it in enumerate(grp):
                    ev = P.step("tensor",
                                lambda eng, pbi=pbi, k2=k2, it=it, jc=jc, qa=qa, ka=ka: eng.matmul(
                                    cx.pbig[pbi][:, k2 * 512:(k2 + 1) * 512],
                                    ka[0:64, jc * 128:(jc + 1) * 128],
                                    qa[0:64, it * 512:(it + 1) * 512],
                                    start=True, stop=True),
                                waits=stw if k2 == 0 else (), inc="T")
                st_done = ev
                wexp = [st_done, eb_free[ebi]]
                eev = P.step("scalar",
                             lambda eng, pbi=pbi, ebi=ebi, w=len(grp): eng.activation(
                                 cx.Ebuf[ebi][:, 0:w * 512],
                                 cx.pbig[pbi][:, 0:w * 512], AF.Exp),
                             waits=wexp, inc="S")
                pb_free[pbi] = eev
                mev = eev
                if grp[0] == jc // 4:
                    mev = P.step("vector",
                                 lambda eng, ebi=ebi, jc=jc: eng.tensor_tensor(
                                     cx.Ebuf[ebi][:, 0:512], cx.Ebuf[ebi][:, 0:512],
                                     PCt["l1m"][:, (jc % 4) * 512:(jc % 4 + 1) * 512],
                                     OP.mult),
                                 waits=[eev, pc_evs[0]], inc="V")
                g_stash[gx] = mev

            def b2_pv(gx):
                jc, gi, grp = glist[gx]
                ebi = gx % 2
                mev = g_stash.pop(gx)
                pv_evs = []
                for k2, it in enumerate(grp):
                    ev = P.step("tensor",
                                lambda eng, ebi=ebi, k2=k2, it=it, jc=jc: eng.matmul(
                                    cx.psm[it][:, :],
                                    Vc[:, jc * 128:(jc + 1) * 128],
                                    cx.Ebuf[ebi][:, k2 * 512:(k2 + 1) * 512],
                                    start=(jc == 0), stop=(jc == 4 * it + 3)),
                                waits=[mev] + ([psm_free[it]] if jc == 0 else []),
                                inc="T")
                    pv_evs.append(ev)
                    pv_last[it] = ev
                eb_free[ebi] = pv_evs[-1]

            for gx in range(len(glist)):
                b2_st(gx)
                if gx > 0:
                    b2_pv(gx - 1)
            b2_pv(len(glist) - 1)
            for it in range(NIT):
                rc = P.step("vector",
                            lambda eng, it=it: eng.reciprocal(cx.trec[:, :], cx.psm[it][64:128, :]),
                            waits=[pv_last[it]], inc="V")
                fo = P.step("vector",
                            lambda eng, it=it, h=h: eng.tensor_tensor(
                                cx.OT[(h % 2) * 64:(h % 2) * 64 + 64,
                                      (h // 2) * N + it * 512:(h // 2) * N + (it + 1) * 512],
                                cx.psm[it][0:64, :], cx.trec[:, :], OP.mult),
                            waits=[rc, getattr(cx, "xgate", None)], inc="V")
                psm_free[it] = fo
                fin_evs.append(fo)
                OT_evs.append(fo)
            Vc_free = fin_evs[-1]

    sub_evs = emit_outproj(cx, wout, None, OT_evs[-1],
                           [strip_consumed["q"], strip_consumed["k"]], [])
    out_evs = emit_ln(cx, "post", src_ready=sub_evs[-8:], resid_dram=resid_dram, tag="b2ln2",
                      scratch_free=OT_evs[-1])
    return out_evs


def emit_b3(cx, resid_dram, src_ready=()):
    P, nc, C, ar = cx.P, cx.nc, cx.C, cx.ar
    AR = nc.dram_tensor("b3_AR", [4, 512, 512], F32, kind="ExternalInput").ap()
    BR = nc.dram_tensor("b3_BR", [4, 512, 512], F32, kind="ExternalInput").ap()
    AI = nc.dram_tensor("b3_AI", [4, 512, 512], F32, kind="ExternalInput").ap()
    BI = nc.dram_tensor("b3_BI", [4, 512, 512], F32, kind="ExternalInput").ap()
    brs = nc.dram_tensor("b3_brs", [128, 4], F32, kind="ExternalInput").ap()
    bis = nc.dram_tensor("b3_bis", [128, 4], F32, kind="ExternalInput").ap()

    if not hasattr(cx, "b3_wr"):
        o = cx.R_OFF
        cx.b3_wr = ar.alloc_at("b3_wr", [128, 4 * 512], BF16, o)
        cx.b3_wi = ar.alloc_at("b3_wi", [128, 4 * 512], BF16, o + 4096)
        cx.b3_wineg = ar.alloc_at("b3_wineg", [128, 4 * 512], BF16, o + 8192)
        cx.b3_a = [ar.alloc_at(f"b3_a{i}", [128, 4 * 512], BF16, o + 12288 + i * 4096) for i in range(2)]
        cx.b3_b = [ar.alloc_at(f"b3_b{i}", [128, 4 * 512], BF16, o + 20480 + i * 4096) for i in range(2)]
    wr, wi, wineg = cx.b3_wr, cx.b3_wi, cx.b3_wineg
    abuf, bbuf = cx.b3_a, cx.b3_b
    brc, bic = cx.b3_bb, cx.b3_bb2

    pevs = []
    pevs.append(P.step("sync", lambda eng: eng.dma_start(out=brc[:, :], in_=brs[:, :]), inc="DS", amt=16))
    pevs.append(P.step("sync", lambda eng: eng.dma_start(out=bic[:, :], in_=bis[:, :]), inc="DS", amt=16))

    psm_free = cx.psm_free
    wr_evs = []
    ld_consumed = [None, None]
    for half, (Adr, Bdr, wdst) in enumerate(((AR, BR, wr), (AI, BI, wi))):
        mm_last = None
        for cc in range(4):
            bi = cc % 2
            src_a = Adr[cc].rearrange("(kc p) m -> p kc m", p=128)
            src_b = Bdr[cc].rearrange("(kc p) m -> p kc m", p=128)
            ea = P.step("gpsimd",
                        lambda eng, d=abuf[bi], s=src_a: eng.dma_start(
                            out=d[:, :].rearrange("p (kc m) -> p kc m", m=512), in_=s),
                        waits=[ld_consumed[bi], cx.bs_free_ev], inc="DG", amt=16)
            eb = P.step("gpsimd",
                        lambda eng, d=bbuf[bi], s=src_b: eng.dma_start(
                            out=d[:, :].rearrange("p (kc m) -> p kc m", m=512), in_=s),
                        inc="DG", amt=16)
            for mc in range(4):
                for kc in range(4):
                    w0 = [ea, eb, psm_free[mc]] if kc == 0 else ()
                    ev = P.step("tensor",
                                lambda eng, mc=mc, kc=kc, bi=bi, cc=cc, half=half: eng.matmul(
                                    cx.psm[mc][:, :],
                                    abuf[bi][:, kc * 512 + mc * 128: kc * 512 + (mc + 1) * 128],
                                    bbuf[bi][:, kc * 512:(kc + 1) * 512],
                                    start=(cc == 0 and kc == 0), stop=(cc == 3 and kc == 3)),
                                waits=w0, inc="T" if kc == 3 else None)
            ld_consumed[bi] = ev
            mm_last = ev
        for mc in range(4):
            e1 = P.step("scalar",
                        lambda eng, mc=mc, wdst=wdst: eng.activation(
                            wdst[:, mc * 512:(mc + 1) * 512], cx.psm[mc][:, :], AF.Copy),
                        waits=[mm_last] + cx.init_evs, inc="S")
            if half == 1:
                e2 = P.step("scalar",
                            lambda eng, mc=mc: eng.activation(
                                wineg[:, mc * 512:(mc + 1) * 512], cx.psm[mc][:, :],
                                AF.Identity, scale=C["immc"][:, 6:7]),
                            inc="S")
                psm_free[mc] = e2
            else:
                psm_free[mc] = e1
            wr_evs.append(e1)
    w_ready = wr_evs[-1]

    ln1_evs = emit_ln(cx, "pre", src_ready=list(src_ready), tag="b3ln1")
    xb_full = ln1_evs

    sub_evs = []
    for half in range(2):
        terms = ((wr, 0), (wineg, 4)) if half == 0 else ((wi, 0), (wr, 4))
        bias = brc if half == 0 else bic
        for oc in range(4):
            for it in range(NIT):
                w0 = [w_ready, psm_free[it]] + (xb_full if half == 0 and oc == 0 and it == 0 else [])
                first = True
                for wt, choff in terms:
                    for kc in range(4):
                        ev = P.step("tensor",
                                    lambda eng, wt=wt, choff=choff, kc=kc, oc=oc, it=it,
                                    first=first, last=(choff == terms[1][1] and kc == 3): eng.matmul(
                                        cx.psm[it][:, :],
                                        wt[:, kc * 512 + oc * 128: kc * 512 + (oc + 1) * 128],
                                        cx.xb[:, (choff + kc) * CW + 1 + it * 512:
                                              (choff + kc) * CW + 1 + (it + 1) * 512],
                                        start=first, stop=last),
                                    waits=w0 if first else (), inc="T" if (choff == terms[1][1] and kc == 3) else None)
                        first = False
                mm_done = ev
                sev = P.step("scalar",
                             lambda eng, half=half, oc=oc, it=it, bias=bias: eng.activation(
                                 cx.subout[:, (half * 4 + oc) * N + it * 512:
                                           (half * 4 + oc) * N + (it + 1) * 512],
                                 cx.psm[it][:, :], AF.Identity, bias=bias[:, oc:oc + 1]),
                             waits=[mm_done, getattr(cx, "xgate", None)] + pevs, inc="S")
                psm_free[it] = sev
                sub_evs.append(sev)

    out_evs = emit_ln(cx, "post", src_ready=sub_evs[-8:], resid_dram=resid_dram, tag="b3ln2",
                      scratch_free=sub_evs[-1])
    return out_evs


def host_b3_inputs(inp):
    f = np.float32
    d = {}
    G1R = np.asarray(inp["L1_ffn_G1_R"], f); G2R = np.asarray(inp["L1_ffn_G2_R"], f)
    G1I = np.asarray(inp["L1_ffn_G1_I"], f); G2I = np.asarray(inp["L1_ffn_G2_I"], f)

    def build_AB(G1, G2):
        A = np.zeros((4, 512, 512), f)
        B = np.zeros((4, 512, 512), f)
        for c in range(4):
            G1c = G1[0, :, c, :]
            G2c = G2[c, :, 0, :]
            for g in range(16):
                A[c, np.arange(32)[:, None] * 16 + g, np.arange(32)[None, :] * 16 + g] = G1c
            for b in range(32):
                B[c, b * 16 + np.arange(16)[:, None], b * 16 + np.arange(16)[None, :]] = G2c.T
        return np.ascontiguousarray(A), np.ascontiguousarray(B)

    d["b3_AR"], d["b3_BR"] = build_AB(G1R, G2R)
    d["b3_AI"], d["b3_BI"] = build_AB(G1I, G2I)
    d["b3_brs"] = np.ascontiguousarray(np.asarray(inp["L1_ffn_bias_real"], f).reshape(4, 128).T)
    d["b3_bis"] = np.ascontiguousarray(np.asarray(inp["L1_ffn_bias_imag"], f).reshape(4, 128).T)
    return d


def host_b2_inputs(inp):
    f = np.float32
    d = {}
    d["b2_wqkv"] = np.ascontiguousarray(np.asarray(inp["L1_attn_w_qkv"], f))
    d["b2_wout"] = np.ascontiguousarray(np.asarray(inp["L1_attn_w_out"], f))
    bg = np.asarray(inp["L1_attn_bandha_gate"], f)
    cycles = np.array([[5, 6, 7, 8][i % 4] for i in range(16)])
    phases = np.arange(N)[None, :] % cycles[:, None]
    d["b2_glog"] = np.ascontiguousarray(np.take_along_axis(bg, phases, axis=1).astype(f))
    return d


def host_b0_inputs(inp):
    f = np.float32
    d = {}
    d["b0_wqT"] = np.ascontiguousarray(np.asarray(inp["L0_attn_wq"], f).T)
    d["b0_wkT"] = np.ascontiguousarray(np.asarray(inp["L0_attn_wk"], f).T)
    d["b0_wvT"] = np.ascontiguousarray(np.asarray(inp["L0_attn_wv"], f).T)
    d["b0_woT"] = np.ascontiguousarray(np.asarray(inp["L0_attn_wo"], f).T)
    d["b0_bqs"] = np.ascontiguousarray(np.asarray(inp["L0_attn_bq"], f).reshape(8, 128).T)
    d["b0_bks"] = np.ascontiguousarray(np.asarray(inp["L0_attn_bk"], f).reshape(8, 128).T)
    d["b0_bos"] = np.ascontiguousarray(np.asarray(inp["L0_attn_bo"], f).reshape(8, 128).T)
    d["b0_bvs"] = np.ascontiguousarray(np.asarray(inp["L0_attn_bv"], f).reshape(8, 128).T)
    return d


def host_const_inputs(cx):
    d = {"c_" + k: v for k, v in cx.host_const_arrays.items()}
    d.update({"pc_" + k: v for k, v in cx.host_pc_arrays.items()})
    return d


def build_b2_program():
    nc = bass.Bass("TRN2", target_bir_lowering=False, debug=False)
    cx = build_base(nc)
    emit_init(cx)
    emit_load_x(cx)
    out_evs = emit_b2(cx, resid_dram=cx.xin)
    emit_store_x(cx, out_evs)
    with nc.Block() as block:
        cx.P.emit(block)
    return nc, cx


def build_fused_program():
    nc = bass.Bass("TRN2", target_bir_lowering=False, debug=False)
    cx = build_base(nc)
    cx.b1_eb_free = [None, None]
    emit_init(cx)
    emit_load_x(cx)
    ev0 = emit_b0(cx, weights_ready=[], resid_dram=cx.xin)
    sc1, sp1 = emit_spill_x(cx, waits=ev0[-4:])
    xTz = cx.xT[:, :].rearrange("p (c w) -> p c w", w=CW)[:, :, 0:1]
    rz = cx.P.step("vector", lambda eng: eng.memset(xTz, 0.0), waits=[ev0[-1]], inc="V")
    cx.pc_free_ev = ev0[-1]
    cx.bs_free_ev = ev0[-1]
    ev1 = emit_b1(cx, resid_dram=sc1, src_ready=ev0[-4:] + [rz])
    cx.pc_free_ev = ev1[-1]
    cx.bs_free_ev = ev1[-1]
    sc2, sp2 = emit_spill_x(cx, waits=ev1[-4:])
    ev2 = emit_b2(cx, resid_dram=sc2, src_ready=ev1[-4:])
    cx.pc_free_ev = ev2[-1]
    cx.bs_free_ev = ev2[-1]
    sc3, sp3 = emit_spill_x(cx, waits=ev2[-4:])
    ev3 = emit_b3(cx, resid_dram=sc3, src_ready=ev2[-4:])
    emit_store_x(cx, ev3)
    with nc.Block() as block:
        cx.P.emit(block)
    return nc, cx


def host_fused_inputs(inp, cx):
    d = {}
    d.update(host_b0_inputs(inp))
    d.update(host_b1_inputs(inp))
    d.update(host_b2_inputs(inp))
    d.update(host_b3_inputs(inp))
    d.update(host_const_inputs(cx))
    return d


def build_b3_program():
    nc = bass.Bass("TRN2", target_bir_lowering=False, debug=False)
    cx = build_base(nc)
    emit_init(cx)
    emit_load_x(cx)
    out_evs = emit_b3(cx, resid_dram=cx.xin)
    emit_store_x(cx, out_evs)
    with nc.Block() as block:
        cx.P.emit(block)
    return nc, cx


def build_b0_program():
    nc = bass.Bass("TRN2", target_bir_lowering=False, debug=False)
    cx = build_base(nc)
    emit_init(cx)
    emit_load_x(cx)
    out_evs = emit_b0(cx, weights_ready=[], resid_dram=cx.xin)
    emit_store_x(cx, out_evs)
    with nc.Block() as block:
        cx.P.emit(block)
    return nc, cx


def run_b0(x_full, inp):
    nc, cx = build_b0_program()
    wd = host_b0_inputs(inp)
    cd = host_const_inputs(cx)
    in_maps = []
    for corei in range(8):
        m = dict(wd)
        m.update(cd)
        m["xT_in"] = np.ascontiguousarray(x_full[corei].T)
        in_maps.append(m)
    res = run_bass_kernel_spmd(nc, in_maps, core_ids=list(range(8)))
    return np.stack([r["xT_out"].T for r in res.results])



B1_DEPTH = 6
B1_ATTN_ONLY = False
B1_SKIP_LN2 = False


def emit_b1(cx, resid_dram, src_ready=()):
    P, nc, C, ar = cx.P, cx.nc, cx.C, cx.ar
    DEPTH = 6
    DI2 = 2816
    Wqkv = nc.dram_tensor("b1_Wqkv", [DEPTH, D, 3 * D], F32, kind="ExternalInput").ap()
    Wo = nc.dram_tensor("b1_Wo", [DEPTH, D, D], F32, kind="ExternalInput").ap()
    W1p = nc.dram_tensor("b1_W1p", [DEPTH, D, 2 * DI2], F32, kind="ExternalInput").ap()
    W2p = nc.dram_tensor("b1_W2p", [DEPTH, DI2, D], F32, kind="ExternalInput").ap()
    qsc = nc.dram_tensor("b1_qsc", [128, DEPTH], F32, kind="ExternalInput").ap()
    ksc = nc.dram_tensor("b1_ksc", [128, DEPTH], F32, kind="ExternalInput").ap()
    lng = nc.dram_tensor("b1_lng", [128, DEPTH * NCH], F32, kind="ExternalInput").ap()
    lnb = nc.dram_tensor("b1_lnb", [128, DEPTH * NCH], F32, kind="ExternalInput").ap()
    b1a = nc.dram_tensor("b1_b1as", [128, DEPTH * 22], F32, kind="ExternalInput").ap()
    b1g = nc.dram_tensor("b1_b1gs", [128, DEPTH * 22], F32, kind="ExternalInput").ap()
    b2s = nc.dram_tensor("b1_b2s", [128, DEPTH * NCH], F32, kind="ExternalInput").ap()

    if not hasattr(cx, "b1_qq"):
        o = cx.R_OFF + 20992
        cx.b1_qq = ar.alloc_at("b1_qq", [128, NCH * 512], BF16, o)
        cx.b1_kq = ar.alloc_at("b1_kq", [128, NCH * 768], BF16, o + 8192)
        cx.b1_vq = ar.alloc_at("b1_vq", [128, 6 * 1024], BF16, o + 20480)
        cx.b1_OTq = ar.alloc_at("b1_OTq", [128, NCH * 512], BF16, o + 32768)
        cx.b1_wv = ar.alloc_at("b1_wv", [128, NCH * 256], BF16, o + 40960)
        cx.b1_Mst = ar.alloc_at("b1_Mst", [128, 22 * 1024], BF16, cx.R_OFF)
        cx.b1_w1s = [ar.alloc_at(f"b1_w1s{i}", [128, 8 * 128], BF16, cx.R_OFF + 45056 + i * 2048)
                     for i in range(4)]
        cx.b1_w2s = [ar.alloc_at(f"b1_w2s{i}", [128, 22 * 128], BF16, cx.R_OFF + 53248 + i * 5632)
                     for i in range(2)]
    qq, kq, vq, OTq = cx.b1_qq, cx.b1_kq, cx.b1_vq, cx.b1_OTq
    wvf, sqq = cx.b1_wv, cx.b1_sqq
    t0b, t1b, t2b = cx.b1_t0, cx.b1_t1, cx.b1_t2
    q2c, rfq, rfq2 = cx.b1_q2c, cx.b1_rfq, cx.b1_rfq2
    Mst, w1s, w2s = cx.b1_Mst, cx.b1_w1s, cx.b1_w2s

    stash = cx.b1_stash
    pevs = []
    for name, dr_ in (("qsc", qsc), ("ksc", ksc), ("lng", lng), ("lnb", lnb),
                      ("b1a", b1a), ("b1g", b1g), ("b2s", b2s)):
        sb = stash[name]
        pevs.append(P.step("sync", lambda eng, sb=sb, dr_=dr_: eng.dma_start(out=sb[:, :], in_=dr_[:, :]),
                           inc="DS", amt=16))

    psm_free = cx.psm_free
    pb_free = cx.pb_free

    ln1_evs = emit_ln(cx, "pre", src_ready=list(src_ready), out_xT=True, tag="b1ln1")
    xb_full = ln1_evs

    xb_ready = ln1_evs[-1]

    vq_tail_ev = None
    kq_tail_ev = None

    pc_names = ["hsel", "hselT", "l0m", "rtAq", "rtAk", "rtBq", "rtBk"]
    for l in range(B1_DEPTH):
        PCt, pc_evs = load_pc(cx, pc_names,
                              waits=[cx.pc_free_ev if l == 0 else None,
                                     getattr(cx, "b1_mst_war", None)])
        lpevs = pevs + pc_evs
        wql = Wqkv[l]
        wo_l = Wo[l]
        OT_evs = []
        resid_evs = []
        prep_done_prev = None
        for qt in range(4):
            tok0 = qt * 512
            pevq = {}
            for c in range(NCH):
                for mi, (mat, moff2) in enumerate((("q", 0), ("k", 8))):
                    lev = load_strip(cx, mat, wql, moff2 + c)
                    pb = (2 * c + mi) % 2
                    w0 = [lev, psm_free[pb], xb_ready] + ([prep_done_prev] if c == 0 else [])
                    for kc in range(NCH):
                        sh = 1 if kc >= 4 else 0
                        ev = P.step("tensor",
                                    lambda eng, c=c, kc=kc, mat=mat, sh=sh, tok0=tok0, pb=pb: eng.matmul(
                                        cx.psm[pb][:, :],
                                        cx.ws[mat][:, kc * 128:(kc + 1) * 128],
                                        cx.xb[:, kc * CW + 1 - sh + tok0: kc * CW + 1 - sh + tok0 + 512],
                                        start=(kc == 0), stop=(kc == NCH - 1)),
                                    waits=w0 if kc == 0 else (),
                                    inc="T" if kc == NCH - 1 else None)
                    if mat == "q":
                        eev = P.step("scalar",
                                     lambda eng, c=c, pb=pb: eng.activation(
                                         qq[:, c * 512:(c + 1) * 512], cx.psm[pb][:, :], AF.Copy),
                                     waits=[ev], inc="S")
                    else:
                        eev = P.step("scalar",
                                     lambda eng, c=c, pb=pb: eng.activation(
                                         kq[:, c * 768 + 256: c * 768 + 768], cx.psm[pb][:, :], AF.Copy),
                                     waits=[ev, kq_tail_ev], inc="S")
                    pevq[(mat, c)] = eev
                    psm_free[pb] = eev
                    cx.strip_consumed[mat] = ev
            for dt_ in range(4):
                wvsrc = wql[:, 2 * D + dt_ * 256: 2 * D + (dt_ + 1) * 256].rearrange(
                    "(kc p) m -> p kc m", p=128)
                wv_ev = P.step("gpsimd",
                               lambda eng, s=wvsrc: eng.dma_start(
                                   out=wvf[:, :].rearrange("p (kc m) -> p kc m", m=256), in_=s),
                               waits=[getattr(cx, "b1_wv_consumed", None),
                                      getattr(cx, "b1_mst_war", None)], inc="DG", amt=16)
                for sci in range(4):
                    scg = qt * 4 + sci
                    slot = 2 + sci
                    pi = 2 + (sci % 2)
                    w0 = [wv_ev, psm_free[pi], vq_tail_ev]
                    for kc in range(NCH):
                        ev = P.step("tensor",
                                    lambda eng, sci=sci, scg=scg, kc=kc, dt_=dt_, pi=pi: eng.matmul(
                                        cx.psm[pi][:, 0:256],
                                        cx.xb[:, kc * CW + 1 - (1 if kc >= 4 else 0) + scg * 128:
                                              kc * CW + 1 - (1 if kc >= 4 else 0) + (scg + 1) * 128],
                                        wvf[:, kc * 256:(kc + 1) * 256],
                                        start=(kc == 0), stop=(kc == NCH - 1)),
                                    waits=w0 if kc == 0 else (),
                                    inc="T" if kc == NCH - 1 else None)
                    vev = P.step("scalar",
                                 lambda eng, slot=slot, dt_=dt_, pi=pi: eng.activation(
                                     vq[:, slot * 1024 + dt_ * 256: slot * 1024 + (dt_ + 1) * 256],
                                     cx.psm[pi][0:128, 0:256], AF.Copy),
                                 waits=[ev], inc="S")
                    psm_free[pi] = vev
                cx.b1_wv_consumed = ev
            v_done = vev
            matspec = {}
            m3s = {}
            last_sq = prep_done_prev
            for mi, mat in enumerate(("q", "k")):
                buf = qq if mat == "q" else kq
                boff = 0 if mat == "q" else 256
                bw = 512 if mat == "q" else 768
                scol = stash["qsc" if mat == "q" else "ksc"][:, l:l + 1]
                tA = PCt["rtAq"] if mat == "q" else PCt["rtAk"]
                tB = PCt["rtBq"] if mat == "q" else PCt["rtBk"]
                rf = rfq if mat == "q" else rfq2
                bank = 1 if mat == "q" else 3
                pbq = 1 - mi
                matspec[mat] = (buf, boff, bw, scol, tA, tB, rf, bank)
                q2mm = []
                for c in range(NCH):
                    sev = P.step("scalar",
                                 lambda eng, buf=buf, c=c, boff=boff, bw=bw: eng.activation(
                                     sqq[:, :], buf[:, c * bw + boff: c * bw + boff + 512], AF.Square),
                                 waits=[pevq[(mat, c)], last_sq], inc="S")
                    ev = P.step("tensor",
                                lambda eng, c=c, hs=PCt["hsel"], pbq=pbq: eng.matmul(
                                    cx.pbig[pbq][0:16, 0:512], hs[:, c * 16:(c + 1) * 16],
                                    sqq[:, :], start=(c == 0), stop=(c == NCH - 1)),
                                waits=[sev, pb_free[pbq] if c == 0 else None] + (lpevs if c == 0 else []),
                                inc="T")
                    q2mm.append(ev)
                    last_sq = ev
                m1 = P.step("vector",
                            lambda eng, pbq=pbq: eng.tensor_scalar_max(
                                q2c[:, :], cx.pbig[pbq][0:16, 0:512], 1e-24),
                            waits=[q2mm[-1]] + ([m3s["q"]] if mat == "k" else []), inc="V")
                m2 = P.step("scalar",
                            lambda eng: eng.activation(q2c[:, :], q2c[:, :], AF.Ln),
                            waits=[m1], inc="S")
                m3 = P.step("scalar",
                            lambda eng, rf=rf: eng.activation(rf[:, :], q2c[:, :], AF.Exp,
                                                              scale=C["immc"][0:16, 1:2]),
                            inc="S")
                pb_free[pbq] = m1
                m3s[mat] = m3
            prep_ev = {}

            def chunk_prep(c):
                for mat in ("q", "k"):
                    buf, boff, bw, scol, tA, tB, rf, bank = matspec[mat]
                    rm = P.step("tensor",
                                lambda eng, c=c, ht=PCt["hselT"], rf=rf, bank=bank: eng.matmul(
                                    cx.psm[bank][:, :], ht[:, c * 128:(c + 1) * 128],
                                    rf[:, :], start=True, stop=True),
                                waits=[m3s[mat] if c == 0 else None, psm_free[bank]], inc="T")
                    e1 = P.step("vector",
                                lambda eng, buf=buf, c=c, boff=boff, bw=bw, scol=scol, bank=bank:
                                eng.scalar_tensor_tensor(
                                    t0b[:, :], buf[:, c * bw + boff: c * bw + boff + 512], scol,
                                    cx.psm[bank][:, :], OP.mult, OP.mult),
                                waits=[rm] + lpevs, inc="V")
                    psm_free[bank] = e1
                    e2 = P.step("vector",
                                lambda eng, tA=tA, tok0=tok0: eng.tensor_tensor(
                                    t1b[0:64, :], t0b[0:64, :], tA[0:64, tok0: tok0 + 512], OP.mult),
                                inc="V")
                    e3 = P.step("vector",
                                lambda eng, tA=tA, tok0=tok0: eng.tensor_tensor(
                                    t1b[64:128, :], t0b[64:128, :], tA[64:128, tok0: tok0 + 512], OP.mult),
                                inc="V")
                    e4 = P.step("vector",
                                lambda eng, tB=tB, tok0=tok0: eng.tensor_tensor(
                                    t2b[0:32, :], t0b[32:64, :], tB[32:64, tok0: tok0 + 512], OP.mult),
                                inc="V")
                    e5 = P.step("vector",
                                lambda eng, tB=tB, tok0=tok0: eng.tensor_tensor(
                                    t2b[32:64, :], t0b[0:32, :], tB[0:32, tok0: tok0 + 512], OP.mult),
                                inc="V")
                    e6 = P.step("vector",
                                lambda eng, tB=tB, tok0=tok0: eng.tensor_tensor(
                                    t2b[64:96, :], t0b[96:128, :], tB[96:128, tok0: tok0 + 512], OP.mult),
                                inc="V")
                    e7 = P.step("vector",
                                lambda eng, tB=tB, tok0=tok0: eng.tensor_tensor(
                                    t2b[96:128, :], t0b[64:96, :], tB[64:96, tok0: tok0 + 512], OP.mult),
                                inc="V")
                    e8 = P.step("vector",
                                lambda eng, buf=buf, c=c, boff=boff, bw=bw: eng.tensor_tensor(
                                    buf[:, c * bw + boff: c * bw + boff + 512], t1b[:, :], t2b[:, :],
                                    OP.add),
                                inc="V")
                    prep_ev[(mat, c)] = e8
            la_fin = []
            items = [(bq, 2 * hc + par) for hc in range(NCH) for par in (0, 1) for bq in (0, 1)]
            la_stash = {}

            def st_phase(idx):
                bq, h = items[idx]
                B = 2 * qt + bq
                first_blk = (B == 0)
                njc = 2 if first_blk else 4
                jc0 = 2 if first_blk else 0
                par = h % 2
                hc = h // 2
                pbi = idx % 2
                ebi = pbi
                stw = [pb_free[pbi], prep_ev[("q", h // 2)], prep_ev[("k", h // 2)]]
                ev = None
                for j2 in range(njc):
                    jc = jc0 + j2
                    ev = P.step("tensor",
                                lambda eng, pbi=pbi, j2=j2, jc=jc, par=par, hc=hc, bq=bq: eng.matmul(
                                    cx.pbig[pbi][:, j2 * 256:(j2 + 1) * 256],
                                    kq[par * 64:(par + 1) * 64,
                                       hc * 768 + bq * 256 + jc * 128: hc * 768 + bq * 256 + (jc + 1) * 128],
                                    qq[par * 64:(par + 1) * 64, hc * 512 + bq * 256: hc * 512 + (bq + 1) * 256],
                                    start=True, stop=True),
                                waits=stw if j2 == 0 else (), inc="T")
                st_done = ev
                eev = P.step("scalar",
                             lambda eng, pbi=pbi, ebi=ebi, njc=njc: eng.activation(
                                 cx.Ebuf[ebi][:, 0:njc * 256], cx.pbig[pbi][:, 0:njc * 256], AF.Exp),
                             waits=[st_done, cx.b1_eb_free[ebi]], inc="S")
                pb_free[pbi] = eev
                mev = P.step("vector",
                             lambda eng, ebi=ebi, njc=njc, jc0=jc0, lm=PCt["l0m"]: eng.tensor_tensor(
                                 cx.Ebuf[ebi][:, 0:njc * 256], cx.Ebuf[ebi][:, 0:njc * 256],
                                 lm[:, jc0 * 256:(jc0 + njc) * 256], OP.mult),
                             waits=[eev], inc="V")
                la_stash[idx] = (mev, njc, jc0)

            def pv_phase(idx):
                bq, h = items[idx]
                mev, njc, jc0 = la_stash.pop(idx)
                pbi = idx % 2
                ebi = pbi
                pvw = [mev, psm_free[pbi * 2]]
                ev = None
                for j2 in range(njc):
                    jc = jc0 + j2
                    slot = 2 * bq + jc
                    P.step("tensor",
                           lambda eng, j2=j2, slot=slot, h=h, pbi=pbi, ebi=ebi, njc=njc: eng.matmul(
                               cx.psm[pbi * 2][0:64, 0:256],
                               vq[:, slot * 1024 + h * 64: slot * 1024 + (h + 1) * 64],
                               cx.Ebuf[ebi][:, j2 * 256:(j2 + 1) * 256],
                               start=(j2 == 0), stop=(j2 == njc - 1),
                               skip_group_check=True),
                           waits=pvw if j2 == 0 else ())
                    ev = P.step("tensor",
                                lambda eng, j2=j2, pbi=pbi, ebi=ebi, njc=njc: eng.matmul(
                                    cx.psm[pbi * 2][64:128, 0:256],
                                    C["ones64"][:, :],
                                    cx.Ebuf[ebi][:, j2 * 256:(j2 + 1) * 256],
                                    start=(j2 == 0), stop=(j2 == njc - 1),
                                    skip_group_check=True),
                                inc="T" if j2 == njc - 1 else None)
                pv_done = ev
                cx.b1_eb_free[ebi] = pv_done
                rc = P.step("vector",
                            lambda eng, pbi=pbi: eng.reciprocal(
                                cx.trec[:, 0:256], cx.psm[pbi * 2][64:128, 0:256]),
                            waits=[pv_done], inc="V")
                fo = P.step("vector",
                            lambda eng, pbi=pbi, h=h, bq=bq: eng.tensor_tensor(
                                OTq[(h % 2) * 64:(h % 2) * 64 + 64,
                                    (h // 2) * 512 + bq * 256:(h // 2) * 512 + (bq + 1) * 256],
                                cx.psm[pbi * 2][0:64, 0:256], cx.trec[:, 0:256], OP.mult),
                            waits=[rc], inc="V")
                psm_free[pbi * 2] = fo
                la_fin.append(fo)
                OT_evs.append(fo)

            next_st = 0
            chunk_prep(0)
            for c in range(1, NCH):
                chunk_prep(c)
                while next_st < 4 * c:
                    st_phase(next_st)
                    if next_st > 0:
                        pv_phase(next_st - 1)
                    next_st += 1
            while next_st < len(items):
                st_phase(next_st)
                if next_st > 0:
                    pv_phase(next_st - 1)
                next_st += 1
            pv_phase(len(items) - 1)
            prep_done = prep_ev[("k", NCH - 1)]
            if qt < 3:
                bc1 = P.step("vector",
                             lambda eng: eng.tensor_copy(
                                 vq[:, 0: 2 * 1024], vq[:, 4 * 1024: 6 * 1024]),
                             waits=[la_fin[-1], la_fin[-2]], inc="V")
                bc2 = P.step("vector",
                             lambda eng: eng.tensor_copy(
                                 kq[:, :].rearrange("p (c w) -> p c w", w=768)[:, :, 0:256],
                                 kq[:, :].rearrange("p (c w) -> p c w", w=768)[:, :, 512:768]),
                             inc="V")
                vq_tail_ev = bc1
                kq_tail_ev = bc2
            else:
                vq_tail_ev = None
                kq_tail_ev = None
            wo_evs = {}
            wo_consumed = [getattr(cx, "b1_woc0", None), getattr(cx, "b1_woc1", None)]

            def load_wo(ec, qt=qt):
                dst = cx.wos[ec % 2][:, :].rearrange("p (kc q) -> p kc q", q=128)
                war = [wo_consumed[ec % 2]]
                if qt == 0 and ec < 2:
                    war += [cx.strip_consumed["q"], cx.strip_consumed["k"]]
                wo_evs[ec] = P.step("gpsimd",
                                    lambda eng, d=dst, s=strip_src(wo_l, ec): eng.dma_start(
                                        out=d, in_=s),
                                    waits=war, inc="DG", amt=16)

            load_wo(0)
            load_wo(1)
            for ec in range(8):
                pi = ec % 4
                w0 = [wo_evs[ec], psm_free[pi]] + ([OT_evs[-1]] if ec == 0 else [])
                for dc in range(NCH):
                    ev = P.step("tensor",
                                lambda eng, ec=ec, dc=dc, pi=pi: eng.matmul(
                                    cx.psm[pi][:, :],
                                    cx.wos[ec % 2][:, dc * 128:(dc + 1) * 128],
                                    OTq[:, dc * 512:(dc + 1) * 512],
                                    start=(dc == 0), stop=(dc == NCH - 1)),
                                waits=w0 if dc == 0 else (),
                                inc="T" if dc == NCH - 1 else None)
                mm_done = ev
                wo_consumed[ec % 2] = mm_done
                rev = P.step("vector",
                             lambda eng, ec=ec, tok0=tok0, pi=pi: eng.tensor_tensor(
                                 cx.xT[:, ec * CW + 1 + tok0: ec * CW + 1 + tok0 + 512],
                                 cx.xT[:, ec * CW + 1 + tok0: ec * CW + 1 + tok0 + 512],
                                 cx.psm[pi][:, :], OP.add),
                             waits=[mm_done], inc="V")
                psm_free[pi] = rev
                resid_evs.append(rev)
                if ec + 2 < 8:
                    load_wo(ec + 2)
            cx.b1_woc0, cx.b1_woc1 = wo_consumed
            cx.strip_consumed["q"] = wo_consumed[0]
            cx.strip_consumed["k"] = wo_consumed[1]
            prep_done_prev = prep_done
        attn_done = resid_evs[-8:]
        if B1_ATTN_ONLY and l == B1_DEPTH - 1:
            break

        gb = (stash["lng"][:, l * NCH:(l + 1) * NCH], stash["lnb"][:, l * NCH:(l + 1) * NCH])
        ln_evs = emit_ln(cx, "pre", src_ready=attn_done, shifted=True, gain_bias=gb,
                         tag=f"b1l{l}ffnln")
        xbf = ln_evs
        mst_war = getattr(cx, "b1_mst_war", None)
        for hf in range(2):
            t0 = hf * 1024
            m_evs = []
            w1_consumed = list(getattr(cx, "b1_w1c", [None] * 4))
            for mc in range(22):
                sa = W1p[l][:, mc * 128:(mc + 1) * 128].rearrange("(kc p) m -> p kc m", p=128)
                sg = W1p[l][:, DI2 + mc * 128: DI2 + (mc + 1) * 128].rearrange("(kc p) m -> p kc m", p=128)
                bi_ = mc % 2
                ea = P.step("gpsimd",
                            lambda eng, s=sa, bi_=bi_: eng.dma_start(
                                out=w1s[bi_ * 2][:, :].rearrange("p (kc m) -> p kc m", m=128), in_=s),
                            waits=[w1_consumed[bi_ * 2]], inc="DG", amt=16)
                eg = P.step("gpsimd",
                            lambda eng, s=sg, bi_=bi_: eng.dma_start(
                                out=w1s[bi_ * 2 + 1][:, :].rearrange("p (kc m) -> p kc m", m=128), in_=s),
                            waits=[w1_consumed[bi_ * 2 + 1]], inc="DG", amt=16)
                for it2 in range(2):
                    pa, pg = (0, 1) if (mc + it2) % 2 == 0 else (2, 3)
                    w0a = [ea, psm_free[pa]] + (xbf if mc == 0 and it2 == 0 and hf == 0 else [])
                    w0g = [eg, psm_free[pg]]
                    for kc in range(NCH):
                        eva = P.step("tensor",
                                     lambda eng, mc=mc, it2=it2, kc=kc, pa=pa, t0=t0, bi_=bi_: eng.matmul(
                                         cx.psm[pa][:, :],
                                         w1s[bi_ * 2][:, kc * 128:(kc + 1) * 128],
                                         cx.xb[:, kc * CW + 1 + t0 + it2 * 512: kc * CW + 1 + t0 + (it2 + 1) * 512],
                                         start=(kc == 0), stop=(kc == NCH - 1)),
                                     waits=w0a if kc == 0 else (),
                                     inc="T" if kc == NCH - 1 else None)
                    for kc in range(NCH):
                        evg = P.step("tensor",
                                     lambda eng, mc=mc, it2=it2, kc=kc, pg=pg, t0=t0, bi_=bi_: eng.matmul(
                                         cx.psm[pg][:, :],
                                         w1s[bi_ * 2 + 1][:, kc * 128:(kc + 1) * 128],
                                         cx.xb[:, kc * CW + 1 + t0 + it2 * 512: kc * CW + 1 + t0 + (it2 + 1) * 512],
                                         start=(kc == 0), stop=(kc == NCH - 1)),
                                     waits=w0g if kc == 0 else (),
                                     inc="T" if kc == NCH - 1 else None)
                    w1_consumed[bi_ * 2] = eva
                    w1_consumed[bi_ * 2 + 1] = evg
                    gel = P.step("scalar",
                                 lambda eng, mc=mc, pg=pg, l=l: eng.activation(
                                     t0b[:, :], cx.psm[pg][:, :], AF.Gelu,
                                     bias=stash["b1g"][:, l * 22 + mc: l * 22 + mc + 1]),
                                 waits=[evg] + pevs + (m_evs[-1:] if m_evs else [])
                                 + ([mst_war] if mc == 0 and it2 == 0 and hf == 0 else []),
                                 inc="S")
                    mev = P.step("vector",
                                 lambda eng, mc=mc, it2=it2, pa=pa, l=l: eng.scalar_tensor_tensor(
                                     Mst[:, mc * 1024 + it2 * 512: mc * 1024 + (it2 + 1) * 512],
                                     cx.psm[pa][:, :], stash["b1a"][:, l * 22 + mc: l * 22 + mc + 1],
                                     t0b[:, :], OP.add, OP.mult),
                                 waits=[eva, gel], inc="V")
                    psm_free[pa] = mev
                    psm_free[pg] = mev
                    m_evs.append(mev)
            w2_consumed = list(getattr(cx, "b1_w2c", [None, None]))
            for ec in range(8):
                s2 = W2p[l][:, ec * 128:(ec + 1) * 128].rearrange("(kc p) m -> p kc m", p=128)
                e2 = P.step("gpsimd",
                            lambda eng, s=s2, ec=ec: eng.dma_start(
                                out=w2s[ec % 2][:, :].rearrange("p (kc m) -> p kc m", m=128), in_=s),
                            waits=[w2_consumed[ec % 2]]
                            + ([w1_consumed[2], w1_consumed[3]] if ec <= 1 else []),
                            inc="DG", amt=16)
                for it2 in range(2):
                    pi = (ec * 2 + it2) % 4
                    w0 = [e2, psm_free[pi]] + (m_evs[-4:] if ec == 0 and it2 == 0 else [])
                    for kc in range(22):
                        ev = P.step("tensor",
                                    lambda eng, ec=ec, it2=it2, kc=kc, pi=pi: eng.matmul(
                                        cx.psm[pi][:, :],
                                        w2s[ec % 2][:, kc * 128:(kc + 1) * 128],
                                        Mst[:, kc * 1024 + it2 * 512: kc * 1024 + (it2 + 1) * 512],
                                        start=(kc == 0), stop=(kc == 21)),
                                    waits=w0 if kc == 0 else (),
                                    inc="T" if kc == 21 else None)
                    mm_done = ev
                    w2_consumed[ec % 2] = mm_done
                    rev = P.step("vector",
                                 lambda eng, ec=ec, it2=it2, t0=t0, pi=pi, l=l: eng.scalar_tensor_tensor(
                                     cx.xT[:, ec * CW + 1 + t0 + it2 * 512: ec * CW + 1 + t0 + (it2 + 1) * 512],
                                     cx.psm[pi][:, :], stash["b2s"][:, l * NCH + ec: l * NCH + ec + 1],
                                     cx.xT[:, ec * CW + 1 + t0 + it2 * 512: ec * CW + 1 + t0 + (it2 + 1) * 512],
                                     OP.add, OP.add),
                                 waits=[mm_done] + pevs, inc="V")
                    psm_free[pi] = rev
                    resid_evs.append(rev)
            cx.b1_w2c = w2_consumed
            cx.b1_w2c_alias = w2_consumed[0]
            cx.b1_w2c_alias2 = w2_consumed[1]
            cx.b1_w1c = w1_consumed
        cx.b1_mst_war = ev
        ffn_done = resid_evs[-8:]
        if l + 1 < B1_DEPTH:
            for c in range(NCH):
                xev = P.step("scalar",
                             lambda eng, c=c: eng.activation(
                                 cx.xb[:, c * CW + 1: c * CW + 1 + N],
                                 cx.xT[:, c * CW + 1: c * CW + 1 + N], AF.Copy),
                             waits=ffn_done if c == 0 else [], inc="S")
            xb_ready = xev
        prep_done_prev = None

    if B1_SKIP_LN2:
        return resid_evs[-8:]
    out_evs = emit_ln(cx, "postx", src_ready=resid_evs[-8:], resid_dram=resid_dram,
                      tag="b1ln2", scratch_free=resid_evs[-1])
    return out_evs


def host_b1_inputs(inp):
    f = np.float32
    d = {}
    d["b1_Wqkv"] = np.ascontiguousarray(np.asarray(inp["L0_ffn_Wqkv"], f))
    d["b1_Wo"] = np.ascontiguousarray(np.asarray(inp["L0_ffn_Wo"], f))
    W1 = np.asarray(inp["L0_ffn_W1"], f)
    W2 = np.asarray(inp["L0_ffn_W2"], f)
    DI, DI2 = 2730, 2816
    W1p = np.zeros((6, 1024, 2 * DI2), f)
    W1p[:, :, :DI] = W1[:, :, :DI]
    W1p[:, :, DI2:DI2 + DI] = W1[:, :, DI:]
    W2p = np.zeros((6, DI2, 1024), f)
    W2p[:, :DI, :] = W2
    d["b1_W1p"] = W1p
    d["b1_W2p"] = W2p
    qs = np.asarray(inp["L0_ffn_q_scale"], f)
    ks = np.asarray(inp["L0_ffn_k_scale"], f)
    d["b1_qsc"] = np.ascontiguousarray(np.tile(qs, (1, 2)).T)
    d["b1_ksc"] = np.ascontiguousarray(np.tile(ks, (1, 2)).T)
    d["b1_lng"] = np.ascontiguousarray(np.asarray(inp["L0_ffn_ln_g"], f).reshape(6, 8, 128)
                                       .transpose(2, 0, 1).reshape(128, 48))
    d["b1_lnb"] = np.ascontiguousarray(np.asarray(inp["L0_ffn_ln_b"], f).reshape(6, 8, 128)
                                       .transpose(2, 0, 1).reshape(128, 48))
    b1 = np.asarray(inp["L0_ffn_b1"], f)
    b1ap = np.zeros((6, DI2), f); b1ap[:, :DI] = b1[:, :DI]
    b1gp = np.zeros((6, DI2), f); b1gp[:, :DI] = b1[:, DI:]
    d["b1_b1as"] = np.ascontiguousarray(b1ap.reshape(6, 22, 128).transpose(2, 0, 1).reshape(128, 132))
    d["b1_b1gs"] = np.ascontiguousarray(b1gp.reshape(6, 22, 128).transpose(2, 0, 1).reshape(128, 132))
    d["b1_b2s"] = np.ascontiguousarray(np.asarray(inp["L0_ffn_b2"], f).reshape(6, 8, 128)
                                       .transpose(2, 0, 1).reshape(128, 48))
    return d


def build_b1_program():
    nc = bass.Bass("TRN2", target_bir_lowering=False, debug=False)
    cx = build_base(nc)
    cx.b1_eb_free = [None, None]
    emit_init(cx)
    emit_load_x(cx)
    out_evs = emit_b1(cx, resid_dram=cx.xin)
    emit_store_x(cx, out_evs)
    with nc.Block() as block:
        cx.P.emit(block)
    return nc, cx



def kernel(**inputs):
    x = np.ascontiguousarray(np.asarray(inputs["L0_attn_x"], np.float32))
    nc, cx = build_fused_program()
    wd = host_fused_inputs(inputs, cx)
    in_maps = []
    for c in range(8):
        m = dict(wd)
        m["xT_in"] = np.ascontiguousarray(x[c].T)
        in_maps.append(m)
    res = run_bass_kernel_spmd(nc, in_maps, core_ids=list(range(8)))
    return np.stack([np.asarray(r["xT_out"]).T for r in res.results]).astype(np.float32)


# version 06
# speedup vs baseline: 1.0511x; 1.0511x over previous
import numpy as np
import concourse.bass as bass
import concourse.mybir as mybir
from concourse.bass_utils import run_bass_kernel_spmd

AF = mybir.ActivationFunctionType
OP = mybir.AluOpType
AX = mybir.AxisListType
F32 = mybir.dt.float32
BF16 = mybir.dt.bfloat16

N = 2048
D = 1024
NCH = 8
CW = N + 1
H = 16
DH = 64
NSC = 16
NIT = 4

ENGS = ("tensor", "vector", "scalar", "sync", "gpsimd")


class Prog:

    NLANE = 4

    def __init__(self, nc):
        self.nc = nc
        self.steps = {e: [] for e in ENGS}
        self.sems = {}
        self.cnt = {}
        self._sem_ctx = []
        self.lane_rr = {}

    def sem(self, name):
        if name not in self.sems:
            cm = self.nc.semaphore(name)
            h = cm.__enter__()
            self._sem_ctx.append(cm)
            self.sems[name] = h
            self.cnt[name] = 0
        return name

    ENG_SEM = {"vector": "V", "scalar": "S", "gpsimd": "GC"}

    def step(self, eng, fn, waits=(), inc=None, amt=1):
        w = tuple(ev for ev in waits if ev is not None)
        ev = None
        if inc is not None and amt == 16:
            li = self.lane_rr.get(eng, 0)
            self.lane_rr[eng] = (li + 1) % self.NLANE
            lane = f"{inc}{li}"
            self.sem(lane)
            if self.cnt[lane] > 0:
                w = w + ((lane, self.cnt[lane]),)
            self.cnt[lane] += 16
            ev = (lane, self.cnt[lane])
            self.steps[eng].append((fn, w, (lane, 16)))
            return ev
        es = self.ENG_SEM.get(eng)
        if es is not None:
            inc = es
            amt = 1
        if inc is not None:
            self.sem(inc)
            if es is not None and self.cnt[inc] > 0:
                w = w + ((inc, self.cnt[inc]),)
            self.cnt[inc] += amt
            ev = (inc, self.cnt[inc])
        self.steps[eng].append((fn, w, (inc, amt) if inc else None))
        return ev

    def emit(self, block):
        for e in ENGS:
            if self.steps[e]:
                self._emit_engine(block, e)

    def _emit_engine(self, block, e):
        prog = self

        def runner(eng):
            seen = {}
            for fn, waits, inc in prog.steps[e]:
                for (s, v) in waits:
                    if seen.get(s, 0) < v:
                        eng.wait_ge(prog.sems[s], v)
                        seen[s] = v
                inst = fn(eng)
                if inc is not None:
                    if isinstance(inst, (list, tuple)):
                        inst = inst[-1]
                    inst.then_inc(prog.sems[inc[0]], inc[1])

        getattr(block, e)(runner)


class Arena:
    def __init__(self, nc):
        self.nc = nc
        self.top = (nc.sbuf_base + 63) // 64 * 64
        self.limit = nc.sbuf_top
        self.tensors = {}

    @staticmethod
    def _size(shape, dtype):
        es = 2 if dtype == BF16 else 4
        n = 1
        for s in shape[1:]:
            n *= s
        return n * es

    def alloc(self, name, shape, dtype, align=64):
        off = (self.top + align - 1) // align * align
        t = self.alloc_at(name, shape, dtype, off)
        self.top = off + self._size(shape, dtype)
        return t

    def alloc_at(self, name, shape, dtype, off):
        t = self.nc.alloc_sbuf_tensor_at(name, shape, dtype, offset=off)
        end = off + self._size(shape, dtype)
        assert end <= self.limit, f"SBUF overflow at {name}: {end}"
        self.tensors[name] = (t, off)
        return t

    def off(self, name):
        return self.tensors[name][1]


def host_consts():
    import ml_dtypes
    bf = ml_dtypes.bfloat16
    c = {}
    a = np.zeros((128, 128), np.float32); a[:, :64] = 1
    c["statA"] = a.astype(bf)
    a = np.zeros((128, 128), np.float32); a[:, 64:] = 1
    c["statB"] = a.astype(bf)
    c["rep64f"] = np.full((128, 128), 1.0 / 64, np.float32)
    c["ones1r"] = np.ones((1, 512), np.float32).astype(bf)
    a = np.zeros((64, 2), np.float32); a[:, 0] = 1
    c["c64a"] = a.astype(bf)
    a = np.zeros((64, 2), np.float32); a[:, 1] = 1
    c["c64b"] = a.astype(bf)
    a = np.zeros((1, 2), np.float32); a[0, 0] = 1
    c["c1a"] = a.astype(bf)
    a = np.zeros((1, 2), np.float32); a[0, 1] = 1
    c["c1b"] = a.astype(bf)
    c["ones64"] = np.ones((128, 64), np.float32).astype(bf)
    c["sc_k"] = np.array([[-1.0 / 64], [1.0]], np.float32)
    c["sc_q"] = np.array([[1.0], [-1.0 / 64]], np.float32)
    imm = np.array([1e-5, -0.5, 1.0 / 1024, 1.0 / 32, 32.0, 8.0, -1.0, 0.0],
                   np.float32)
    c["immc"] = np.tile(imm[None, :], (128, 1))
    rot = np.zeros((128, 128), np.float32)
    for b in (0, 64):
        for i in range(32):
            rot[b + 32 + i, b + i] = -1.0
            rot[b + i, b + 32 + i] = 1.0
    c["rot"] = rot.astype(bf)
    return c


def host_phase_consts():
    import ml_dtypes
    bf = ml_dtypes.bfloat16
    c = {}
    hr = np.zeros((16, 16 * 64), np.float32)
    for h in range(16):
        hr[h, h * 64:(h + 1) * 64] = 1
    c["hrep"] = hr.astype(bf)
    lm = np.zeros((128, 4 * 512), np.float32)
    for v in range(4):
        for p in range(128):
            lm[p, v * 512 + 128 * v + p:(v + 1) * 512] = 1
    c["l1m"] = lm.astype(bf)
    hs = np.zeros((128, 8 * 16), np.float32)
    ht = np.zeros((16, 8 * 128), np.float32)
    for cc in range(8):
        for p in range(128):
            h = 2 * cc + (1 if p >= 64 else 0)
            hs[p, cc * 16 + h] = 1
            ht[h, cc * 128 + p] = 1
    c["hsel"] = hs.astype(bf)
    c["hselT"] = ht.astype(bf)
    lm0 = np.zeros((128, 4 * 256), np.float32)
    for jc in range(4):
        for p in range(128):
            jl = jc * 128 + p
            lo, hi = max(0, jl - 256), min(255, jl)
            if lo <= hi:
                lm0[p, jc * 256 + lo: jc * 256 + hi + 1] = 1
    c["l0m"] = lm0.astype(bf)
    dh, wnd, nn = 64, 256, 2048
    half = np.arange(0, dh, 2, dtype=np.float64)
    inv_freq = 1.0 / 10000.0 ** (half / dh)
    t = np.arange(nn, dtype=np.float64)
    fr = t[:, None] * inv_freq[None, :]
    base = (half + 0.4 * dh) / (1.4 * dh)
    power = (t - nn // 2) / (wnd // 2)
    sc = base[None, :] ** power[:, None]
    cosf = np.cos(fr); sinf = np.sin(fr)
    qc = np.concatenate([cosf * sc, cosf * sc], 1).T * 8
    kc_ = np.concatenate([cosf / sc, cosf / sc], 1).T
    qsn = np.concatenate([-sinf * sc, sinf * sc], 1).T * 8
    ksn = np.concatenate([-sinf / sc, sinf / sc], 1).T
    qsp = np.concatenate([sinf * sc, sinf * sc], 1).T * 8
    ksp = np.concatenate([sinf / sc, sinf / sc], 1).T
    c["rtAq"] = np.ascontiguousarray(np.vstack([qc, qc]).astype(np.float32)).astype(bf)
    c["rtAk"] = np.ascontiguousarray(np.vstack([kc_, kc_]).astype(np.float32)).astype(bf)
    c["rtBq"] = np.ascontiguousarray(np.vstack([qsp, qsp]).astype(np.float32)).astype(bf)
    c["rtBk"] = np.ascontiguousarray(np.vstack([ksp, ksp]).astype(np.float32)).astype(bf)
    return c


class Ctx:
    pass


def build_base(nc):
    P = Prog(nc)
    ar = Arena(nc)
    cx = Ctx()
    cx.nc, cx.P, cx.ar = nc, P, ar
    cx.pbig = [nc.alloc_psum_tensor(f"pbig{i}", [128, 1024], F32) for i in range(2)]
    cx.psm = [nc.alloc_psum_tensor(f"psm{i}", [128, 512], F32) for i in range(4)]

    cx.xT = ar.alloc("xT", [128, NCH * CW], F32)
    cx.xb = ar.alloc("xb", [128, NCH * CW], BF16)

    xoff = ar.off("xT")
    cx.subout = ar.alloc_at("subout", [128, NCH * N], BF16, xoff)
    cx.OT = ar.alloc_at("OT", [128, NCH * N], BF16, xoff + NCH * N * 2)

    hc = host_consts()
    cx.C = {}
    cx.const_evs = []
    for name, arr in hc.items():
        dt = F32 if arr.dtype == np.float32 else BF16
        dr = nc.dram_tensor("c_" + name, list(arr.shape), dt, kind="ExternalInput").ap()
        sb = ar.alloc("s_" + name, list(arr.shape), dt)
        ev = P.step("sync", lambda eng, dr=dr, sb=sb: eng.dma_start(out=sb[:, :], in_=dr[:, :]),
                    inc="DS", amt=16)
        cx.C[name] = sb
        cx.const_evs.append(ev)
    cx.host_const_arrays = hc

    cx.ln_xs = [ar.alloc(f"ln_xs{i}", [128, N], BF16) for i in range(2)]
    cx.ln_sq = [ar.alloc(f"ln_sq{i}", [128, N], BF16) for i in range(2)]
    f12 = ar.alloc("ln_f12", [128, 512], F32)
    cx.ln_f1 = f12
    cx.ln_f2 = f12
    cx.f1ap = f12[0:64, :]
    cx.f2ap = f12[64:128, :]

    cx.vR = ar.alloc("vR", [128, NSC * 128], BF16)
    cx.Vc = ar.alloc("Vc", [128, NSC * 128], BF16)
    cx.ln_tmp = [ar.alloc_at(f"ln_tmp{i}", [128, 512], F32, ar.off("Vc") + i * 2048) for i in range(2)]
    cx.xo_t = [ar.alloc_at(f"xo_t{i}", [128, 512], F32, ar.off("vR") + i * 2048) for i in range(2)]
    cx.trec = ar.alloc_at("trec", [64, 512], F32, ar.off("ln_f12"))
    wso = ar.limit - 6144
    cx.ws = {m: ar.alloc_at(f"ws_{m}", [128, 8 * 128], BF16, wso + i * 2048)
             for i, m in enumerate(("q", "k", "v"))}
    cx.wos = [ar.alloc_at("wos0", [128, 8 * 128], BF16, wso),
              ar.alloc_at("wos1", [128, 8 * 128], BF16, wso + 2048)]
    cx.Ebuf = [ar.alloc(f"Ebuf{i}", [128, 1024], BF16) for i in range(2)]
    cx.wev = {m: None for m in ("q", "k", "v")}
    cx.strip_consumed = {m: None for m in ("q", "k", "v")}
    cx.psm_free = [None] * 4
    cx.pb_free = [None, None]

    cx.b0_bq32 = ar.alloc("b0_bq32", [128, NCH], F32)
    cx.b0_bkc = ar.alloc("b0_bkc", [128, NCH], F32)
    cx.b0_boc = ar.alloc("b0_boc", [128, NCH], F32)
    cx.b0_bvc = ar.alloc("b0_bvc", [128, NCH], F32)
    cx.b0_boc2 = ar.alloc("b0_boc2", [128, NCH], F32)
    cx.b0_ncacc = ar.alloc("b0_ncacc", [128, 32], F32)
    cx.b0_nc16 = ar.alloc("b0_nc16", [128, 16], F32)
    cx.b0_cln16 = ar.alloc("b0_cln16", [128, 16], F32)
    cx.b0_c16 = ar.alloc("b0_c16", [128, 16], F32)
    cx.b1_stash = {}
    for nm, wdt in (("qsc", 6), ("ksc", 6), ("lng", 48), ("lnb", 48),
                    ("b1a", 132), ("b1g", 132), ("b2s", 48)):
        cx.b1_stash[nm] = ar.alloc("b1s_" + nm, [128, wdt], F32)
    cx.b3_bb = ar.alloc("b3_bc", [128, 4], F32)
    cx.b3_bb2 = ar.alloc("b3_bc2", [128, 4], F32)
    cx.ln_gbt = ar.alloc_at("ln_gbt", [128, 512], F32, ar.off("ln_sq1"))
    cx.b1_t0 = ar.alloc_at("b1_t0", [128, 512], BF16, ar.off("ln_xs1"))
    cx.b1_t1 = ar.alloc_at("b1_t1", [128, 512], BF16, ar.off("ln_xs1") + 1024)
    cx.b1_t2 = ar.alloc_at("b1_t2", [128, 512], BF16, ar.off("ln_xs1") + 2048)

    pc = host_phase_consts()
    cx.host_pc_arrays = pc
    cx.PCdr = {}
    for name, arr in pc.items():
        cx.PCdr[name] = nc.dram_tensor("pc_" + name, list(arr.shape), BF16,
                                       kind="ExternalInput").ap()
    cx.R_OFF = (ar.top + 63) // 64 * 64
    cx.R_END = ar.limit - 6144
    cx.sqb = ar.alloc_at("sqb", [64, N], BF16, cx.R_OFF + 65536)
    cx.b1_sqq = ar.alloc_at("b1_sqq", [128, 512], BF16, cx.R_OFF + 66048)
    cx.b1_q2c = ar.alloc_at("b1_q2c", [16, 512], F32, cx.R_OFF + 67072)
    cx.b1_rfq = ar.alloc_at("b1_rfq", [16, 512], BF16, cx.R_OFF + 69120)
    cx.b1_rfq2 = ar.alloc_at("b1_rfq2", [16, 512], BF16, cx.R_OFF + 70144)
    cx.pc_free_ev = None
    cx.bs_free_ev = None
    return cx


def load_pc(cx, names, waits=()):
    off = cx.R_OFF
    out = {}
    evs = []
    for name in names:
        arr = cx.host_pc_arrays[name]
        sb = cx.ar.alloc_at(f"pc_{name}_{len(cx.ar.tensors)}", list(arr.shape), BF16, off)
        off += Arena._size(list(arr.shape), BF16)
        off = (off + 63) // 64 * 64
        ev = cx.P.step("sync",
                       lambda eng, sb=sb, dr=cx.PCdr[name]: eng.dma_start(out=sb[:, :], in_=dr[:, :]),
                       waits=list(waits), inc="DS", amt=16)
        out[name] = sb
        evs.append(ev)
    return out, evs


def strip_src(w, c):
    return w.rearrange("(kc p) (mc m) -> p kc mc m", p=128, m=128)[:, :, c, :]


def load_strip(cx, m, wdr, c):
    dst = cx.ws[m][:, :].rearrange("p (kc q) -> p kc q", q=128)
    ev = cx.P.step("gpsimd",
                   lambda eng, t=dst, s=strip_src(wdr, c): eng.dma_start(out=t, in_=s),
                   waits=[cx.strip_consumed[m]], inc="DG", amt=16)
    cx.wev[m] = ev
    return ev


def emit_outproj(cx, wo, bias_col, OT_last_ev, strip_war, pevs, bv_col=None,
                 bias_out=None):
    P, C = cx.P, cx.C
    wo_evs = {}
    wo_consumed = [None, None]

    def load_wo_strip(ec):
        dst = cx.wos[ec % 2][:, :].rearrange("p (kc q) -> p kc q", q=128)
        war = [wo_consumed[ec % 2]]
        if ec < 2:
            war += strip_war
        wo_evs[ec] = P.step("gpsimd",
                            lambda eng, d=dst, ec=ec: eng.dma_start(out=d, in_=strip_src(wo, ec)),
                            waits=war, inc="DG", amt=16)

    load_wo_strip(0)
    load_wo_strip(1)
    sub_evs = []
    bo_evs = {}
    for ec in range(8):
        if bv_col is not None:
            for dc in range(NCH):
                bmm = P.step("tensor",
                             lambda eng, ec=ec, dc=dc: eng.matmul(
                                 cx.pbig[0][:, 0:1],
                                 cx.wos[ec % 2][:, dc * 128:(dc + 1) * 128],
                                 bv_col[:, dc:dc + 1],
                                 start=(dc == 0), stop=(dc == NCH - 1)),
                             waits=[wo_evs[ec], cx.pb_free[0]] + pevs if dc == 0 else (),
                             inc="T" if dc == NCH - 1 else None)
            bo_evs[ec] = P.step("vector",
                                lambda eng, ec=ec: eng.tensor_tensor(
                                    bias_out[:, ec:ec + 1], bias_col[:, ec:ec + 1],
                                    cx.pbig[0][:, 0:1], OP.add),
                                waits=[bmm], inc="V")
            cx.pb_free[0] = bo_evs[ec]
        for it in range(NIT):
            w0 = [wo_evs[ec], cx.psm_free[it]] + ([OT_last_ev] if ec == 0 and it == 0 else [])
            for dc in range(NCH):
                ev = P.step("tensor",
                            lambda eng, ec=ec, it=it, dc=dc: eng.matmul(
                                cx.psm[it][:, :],
                                cx.wos[ec % 2][:, dc * 128:(dc + 1) * 128],
                                cx.OT[:, dc * N + it * 512: dc * N + (it + 1) * 512],
                                start=(dc == 0), stop=(dc == NCH - 1)),
                            waits=w0 if dc == 0 else (),
                            inc="T" if dc == NCH - 1 else None)
            mm_done = ev
            if bv_col is not None:
                sev = P.step("scalar",
                             lambda eng, ec=ec, it=it: eng.activation(
                                 cx.subout[:, ec * N + it * 512: ec * N + (it + 1) * 512],
                                 cx.psm[it][:, :], AF.Identity, bias=bias_out[:, ec:ec + 1]),
                             waits=[mm_done, bo_evs[ec]] + pevs, inc="S")
            elif bias_col is not None:
                sev = P.step("scalar",
                             lambda eng, ec=ec, it=it: eng.activation(
                                 cx.subout[:, ec * N + it * 512: ec * N + (it + 1) * 512],
                                 cx.psm[it][:, :], AF.Identity, bias=bias_col[:, ec:ec + 1]),
                             waits=[mm_done] + pevs, inc="S")
            else:
                sev = P.step("scalar",
                             lambda eng, ec=ec, it=it: eng.activation(
                                 cx.subout[:, ec * N + it * 512: ec * N + (it + 1) * 512],
                                 cx.psm[it][:, :], AF.Copy),
                             waits=[mm_done] + pevs, inc="S")
            cx.psm_free[it] = sev
            sub_evs.append(sev)
        wo_consumed[ec % 2] = mm_done
        if ec + 2 < 8:
            load_wo_strip(ec + 2)
    cx.strip_consumed["q"] = wo_consumed[0]
    cx.strip_consumed["k"] = wo_consumed[1]
    return sub_evs


def emit_init(cx):
    P = cx.P
    xTz = cx.xT[:, :].rearrange("p (c w) -> p c w", w=CW)[:, :, 0:1]
    xbz = cx.xb[:, :].rearrange("p (c w) -> p c w", w=CW)[:, :, 0:1]
    e1 = P.step("vector", lambda eng: eng.memset(xTz, 0.0), inc="V")
    e2 = P.step("vector", lambda eng: eng.memset(xbz, 0.0), inc="V")
    cx.init_evs = [e1, e2] + cx.const_evs


def emit_load_x(cx):
    P, nc = cx.P, cx.nc
    xin = nc.dram_tensor("xT_in", [D, N], F32, kind="ExternalInput").ap()
    cx.xin = xin
    src = xin.rearrange("(c p) n -> p c n", p=128)
    dst = cx.xT[:, :].rearrange("p (c w) -> p c w", w=CW)[:, :, 1:]
    cx.x_ready = P.step("sync", lambda eng: eng.dma_start(out=dst, in_=src),
                        waits=cx.init_evs, inc="DS", amt=16)


def emit_spill_x(cx, waits):
    P, nc = cx.P, cx.nc
    if not hasattr(cx, "xspill"):
        cx.xspill = nc.dram_tensor("xspill", [D, N], F32, kind="Internal").ap()
    dst = cx.xspill.rearrange("(c p) n -> p c n", p=128)
    srcv = cx.xT[:, :].rearrange("p (c w) -> p c w", w=CW)[:, :, 1:]
    ev = P.step("sync", lambda eng: eng.dma_start(out=dst, in_=srcv),
                waits=list(waits), inc="DS", amt=16)
    cx.xgate = ev
    return cx.xspill, ev


def emit_store_x(cx, wait_evs):
    P, nc = cx.P, cx.nc
    xout = nc.dram_tensor("xT_out", [D, N], F32, kind="ExternalOutput").ap()
    dst = xout.rearrange("(c p) n -> p c n", p=128)
    src = cx.xT[:, :].rearrange("p (c w) -> p c w", w=CW)[:, :, 1:]
    ev = P.step("sync", lambda eng: eng.dma_start(out=dst, in_=src),
                waits=wait_evs, inc="DS", amt=16)
    P.step("sync", lambda eng: eng.nop(), waits=[ev])
    return ev


def emit_dbg(cx, sb_ap, shape, dtype, waits):
    P, nc = cx.P, cx.nc
    dr = nc.dram_tensor("dbg", list(shape), dtype, kind="ExternalOutput").ap()
    ev = P.step("sync", lambda eng: eng.dma_start(out=dr[:, :], in_=sb_ap), waits=waits,
                inc="DS", amt=16)
    P.step("sync", lambda eng: eng.nop(), waits=[ev])
    return ev


def emit_ln(cx, kind, src_ready, shifted=False, gain_bias=None,
            resid_dram=None, out_xb=False, out_xT=False, sub_nt=N, tag="",
            scratch_free=None, src_chunks=None):
    P, C = cx.P, cx.C
    pstat = cx.psm[:4] if kind == "_unused" else None
    pstat = cx.psm
    prepI, prepM = cx.pbig[0], cx.pbig[1]

    def src_f32(c, sh):
        base = c * CW + 1 - (1 if sh else 0)
        return cx.xT[:, base: base + N]

    def src_bf(c, sh):
        assert not sh
        return cx.subout[:, c * N: c * N + N]

    mm_evs = [None] * NIT
    last_mm_per_buf = [None, None]
    xs_ev = [None, None]
    sq_ev = [None, None]
    stat_done = [None] * NIT
    for c in range(NCH):
        sh = shifted and c >= 4
        b = 0
        if kind in ("pre", "postx"):
            xsb = cx.ln_xs[b]
            xs_ev[b] = P.step("scalar",
                              lambda eng, o=xsb, i=src_f32(c, sh): eng.activation(o[:, :], i, AF.Copy),
                              waits=[cx.x_ready, last_mm_per_buf[b]]
                              + ([src_chunks[c]] if src_chunks else list(src_ready)), inc="S")
            xsrc = xsb
            xsl = lambda it, t=xsb: t[:, it * 512:(it + 1) * 512]
            sqi = src_f32(c, sh)
        else:
            xsl = lambda it, c=c, sh=sh: src_bf(c, sh)[:, it * 512:(it + 1) * 512]
            sqi = src_bf(c, sh)
        sqb = cx.ln_sq[b]
        sq_ev[b] = P.step("scalar",
                          lambda eng, o=sqb, i=sqi: eng.activation(o[:, :], i, AF.Square),
                          waits=[last_mm_per_buf[b]]
                          + ([src_chunks[c]] if src_chunks else list(src_ready)), inc="S")
        for it in range(NIT):
            w = [xs_ev[b], sq_ev[b]]
            if c == 0:
                w += [cx.psm_free[it]]
            P.step("tensor",
                   lambda eng, ps=pstat[it], x=xsl(it), c=c: eng.matmul(
                       ps[:, :], C["statA"][:, :], x, start=(c == 0), stop=False),
                   waits=w)
            ev = P.step("tensor",
                        lambda eng, ps=pstat[it], s=sqb, it=it, c=c: eng.matmul(
                            ps[:, :], C["statB"][:, :], s[:, it * 512:(it + 1) * 512],
                            start=False, stop=(c == NCH - 1)),
                        inc="T")
            mm_evs[it] = ev
        last_mm_per_buf[b] = mm_evs[NIT - 1]
    for it in range(NIT):
        stat_done[it] = mm_evs[it]

    out_evs = []
    eI = [None] * NIT
    eM = [None] * NIT
    prev_eI = None
    prev_eM = None
    for it in range(NIT):
        pst = pstat[it]
        e1 = P.step("scalar",
                    lambda eng, pst=pst: eng.activation(cx.f1ap, pst[0:64, :],
                                                        AF.Square, scale=C["immc"][0:64, 2:3]),
                    waits=[stat_done[it], prev_eM], inc="S")
        e2 = P.step("vector",
                    lambda eng, pst=pst: eng.scalar_tensor_tensor(
                        cx.f2ap, pst[64:128, :], 1.0 / D, cx.f1ap,
                        OP.mult, OP.subtract),
                    waits=[e1, prev_eI], inc="V")
        e3 = P.step("scalar",
                    lambda eng: eng.activation(cx.f1ap, cx.f2ap,
                                               AF.Ln, bias=C["immc"][0:64, 0:1]),
                    waits=[e2], inc="S")
        e4 = P.step("scalar",
                    lambda eng: eng.activation(cx.f2ap, cx.f1ap,
                                               AF.Exp, scale=C["immc"][0:64, 1:2]),
                    inc="S")
        e5 = P.step("vector",
                    lambda eng, pst=pst: eng.scalar_tensor_tensor(
                        cx.f1ap, pst[0:64, :], 1.0 / D, cx.f2ap,
                        OP.mult, OP.mult),
                    waits=[e4], inc="V")
        rI = cx.pbig[it // 2][:, (it % 2) * 512:(it % 2) * 512 + 512]
        eI[it] = P.step("tensor",
                        lambda eng, rI=rI: eng.matmul(rI, C["rep64f"][64:128, :],
                                                      cx.f2ap, start=True, stop=True),
                        waits=[e4], inc="T")
        eM[it] = P.step("tensor",
                        lambda eng, pst=pst: eng.matmul(pst[:, :], C["rep64f"][0:64, :],
                                                        cx.f1ap, start=True, stop=True),
                        waits=[e5, e2, e1], inc="T")
        prev_eI, prev_eM = eI[it], eM[it]

    c_order = list(range(NCH))
    it_order = list(range(NIT))
    if kind == "post":
        c_order = list(reversed(c_order))
        it_order = list(reversed(it_order))
    tmp_ev = [None, None]
    cix = 0
    for c in c_order:
        for it in it_order:
            sh = shifted and c >= 4
            tb = cx.ln_tmp[cix % 2]
            rI = cx.pbig[it // 2][:, (it % 2) * 512:(it % 2) * 512 + 512]
            rM = pstat[it]
            if kind == "pre":
                x_sl = src_f32(c, sh)[:, it * 512:(it + 1) * 512]
                a1 = P.step("vector",
                            lambda eng, tb=tb, x=x_sl, rI=rI: eng.tensor_tensor(
                                tb[:, :], x, rI, OP.mult),
                            waits=[eI[it], tmp_ev[cix % 2]], inc="V")
                dst = cx.xb[:, c * CW + 1 + it * 512: c * CW + 1 + (it + 1) * 512]
                d2 = cx.ln_gbt[:, :] if gain_bias is not None else dst
                a2 = P.step("vector",
                            lambda eng, tb=tb, d=d2, rM=rM: eng.tensor_tensor(
                                d, tb[:, :], rM[:, :], OP.subtract),
                            waits=[eM[it]] + ([out_evs[-1]] if gain_bias is not None and out_evs else []),
                            inc="V")
                tmp_ev[cix % 2] = a2
                out_evs.append(a2)
                if gain_bias is not None:
                    g_sb, b_sb = gain_bias
                    a3 = P.step("scalar",
                                lambda eng, d=dst, c=c: eng.activation(
                                    d, cx.ln_gbt[:, :], AF.Identity, bias=b_sb[:, c:c + 1],
                                    scale=g_sb[:, c:c + 1]),
                                waits=[a2], inc="S")
                    out_evs.append(a3)
                if out_xT:
                    xd = src_f32(c, False)[:, it * 512:(it + 1) * 512]
                    a4 = P.step("vector",
                                lambda eng, tb=tb, xd=xd, rM=rM: eng.tensor_tensor(
                                    xd, tb[:, :], rM[:, :], OP.subtract),
                                waits=[eM[it], getattr(cx, "xgate", None)], inc="V")
                    tmp_ev[cix % 2] = a4
                    out_evs.append(a4)
            else:
                if kind == "postx":
                    s_sl = src_f32(c, False)[:, it * 512:(it + 1) * 512]
                else:
                    s_sl = src_bf(c, False)[:, it * 512:(it + 1) * 512]
                a1 = P.step("vector",
                            lambda eng, tb=tb, s=s_sl, rI=rI: eng.tensor_tensor(
                                tb[:, :], s, rI, OP.mult),
                            waits=[eI[it], tmp_ev[cix % 2]], inc="V")
                a2 = P.step("vector",
                            lambda eng, tb=tb, rM=rM: eng.tensor_tensor(
                                tb[:, :], tb[:, :], rM[:, :], OP.subtract),
                            waits=[eM[it]], inc="V")
                xo = cx.xo_t[cix % 2]
                src = resid_dram.rearrange("(c p) n -> p c n", p=128)[:, c, it * 512:(it + 1) * 512]
                d1 = P.step("sync",
                            lambda eng, xo=xo, src=src: eng.dma_start(out=xo[:, :], in_=src),
                            waits=[tmp_ev[cix % 2] or scratch_free], inc="DS", amt=16)
                dst = src_f32(c, False)[:, it * 512:(it + 1) * 512]
                a3 = P.step("vector",
                            lambda eng, tb=tb, xo=xo, d=dst: eng.tensor_tensor(
                                d, tb[:, :], xo[:, :], OP.add),
                            waits=[d1], inc="V")
                tmp_ev[cix % 2] = a3
                out_evs.append(a3)
            cix += 1
    return out_evs


def emit_b0(cx, weights_ready, resid_dram):
    P, nc, C, ar = cx.P, cx.nc, cx.C, cx.ar

    wq = nc.dram_tensor("b0_wqT", [D, D], F32, kind="ExternalInput").ap()
    wk = nc.dram_tensor("b0_wkT", [D, D], F32, kind="ExternalInput").ap()
    wv = nc.dram_tensor("b0_wvT", [D, D], F32, kind="ExternalInput").ap()
    wo = nc.dram_tensor("b0_woT", [D, D], F32, kind="ExternalInput").ap()
    bqs = nc.dram_tensor("b0_bqs", [128, NCH], F32, kind="ExternalInput").ap()
    bks = nc.dram_tensor("b0_bks", [128, NCH], F32, kind="ExternalInput").ap()
    bos = nc.dram_tensor("b0_bos", [128, NCH], F32, kind="ExternalInput").ap()
    bvs = nc.dram_tensor("b0_bvs", [128, NCH], F32, kind="ExternalInput").ap()

    qaug = cx.ln_xs
    kaug = cx.ln_sq
    vR = cx.vR
    Vc = cx.Vc
    sqb = cx.sqb
    trec = cx.trec
    ws = cx.ws
    ST = ar.alloc_at("b0_ST", [128, NSC * N], BF16, cx.R_OFF)
    ncacc, nc16, cln16, c16 = cx.b0_ncacc, cx.b0_nc16, cx.b0_cln16, cx.b0_c16
    cx.b0_r1_prev = None
    bq32, bkc, boc = cx.b0_bq32, cx.b0_bkc, cx.b0_boc
    bvc, boc2 = cx.b0_bvc, cx.b0_boc2

    pevs = []
    pevs.append(P.step("sync", lambda eng: eng.dma_start(out=bq32[:, :], in_=bqs[:, :]), inc="DS", amt=16))
    pevs.append(P.step("sync", lambda eng: eng.dma_start(out=bkc[:, :], in_=bks[:, :]), inc="DS", amt=16))
    pevs.append(P.step("sync", lambda eng: eng.dma_start(out=boc[:, :], in_=bos[:, :]), inc="DS", amt=16))
    bvcb = ar.alloc_at("b0_bvcb", [128, NCH], BF16, ar.off("b0_bvc") + 32)
    pevs.append(P.step("gpsimd", lambda eng: eng.dma_start(out=bvcb[:, :], in_=bvs[:, :]), inc="DG", amt=16))
    bq_ev = P.step("vector", lambda eng: eng.tensor_scalar_mul(bq32[:, :], bq32[:, :], 1.0 / 32),
                   waits=pevs[:1], inc="V")
    pevs.append(bq_ev)

    ln1_evs = emit_ln(cx, "pre", src_ready=weights_ready, tag="b0ln1")

    ln1_done = [ln1_evs[-1], ln1_evs[-2]]
    xb_full = ln1_evs

    wev = cx.wev
    strip_consumed = cx.strip_consumed

    for m, wdr in (("q", wq), ("k", wk), ("v", wv)):
        load_strip(cx, m, wdr, 0)


    pb_free = cx.pb_free
    psm_free = cx.psm_free
    OT_evs = []
    proj_read_xb = []
    Vc_free = None
    sq_free = None
    q2_parity = 0

    ST_T_budget = None

    for c in range(NCH):
        b = 0
        evac_evs = []
        for (mat, augA, augB, bias, scale) in (
                ("q", qaug[0], qaug[1], bq32, "r32"),
                ("k", kaug[0], kaug[1], bkc, None)):
            for it in range(NIT):
                mm_w = [wev[mat], psm_free[it]] + (xb_full if c == 0 and it == 0 else [])
                for kc in range(NCH):
                    ev = P.step("tensor",
                                lambda eng, it=it, kc=kc, mat=mat, b=b, c=c: eng.matmul(
                                    cx.psm[it][:, :],
                                    ws[mat][:, kc * 128:(kc + 1) * 128],
                                    cx.xb[:, kc * CW + 1 + it * 512: kc * CW + 1 + (it + 1) * 512],
                                    start=(kc == 0), stop=(kc == NCH - 1)),
                                waits=mm_w if kc == 0 else (),
                                inc="T" if kc == NCH - 1 else None)
                    if kc == NCH - 1:
                        mm_done = ev
                proj_read_xb.append(mm_done)
                for par, aug in ((0, augA), (1, augB)):
                    eev = P.step("scalar",
                                 lambda eng, it=it, par=par, aug=aug, bias=bias, c=c, scale=scale: eng.activation(
                                     aug[0:64, it * 512:(it + 1) * 512],
                                     cx.psm[it][par * 64:(par + 1) * 64, :],
                                     AF.Identity, bias=bias[par * 64:(par + 1) * 64, c:c + 1],
                                     scale=(C["immc"][par * 64:(par + 1) * 64, 3:4]
                                            if scale == "r32" else 1.0)),
                                 waits=[mm_done] + pevs, inc="S")
                    evac_evs.append(eev)
                psm_free[it] = evac_evs[-1]
        strip_consumed["q"] = evac_evs[7]
        strip_consumed["k"] = evac_evs[-1]
        v_evs = []
        for g in range(4):
            it = g
            grp_w = [wev["v"], psm_free[it], pevs[3]]
            for sci in range(4):
                sc = g * 4 + sci
                for kc in range(NCH):
                    ev = P.step("tensor",
                                lambda eng, it=it, sci=sci, sc=sc, kc=kc, b=b: eng.matmul(
                                    cx.psm[it][:, sci * 128:(sci + 1) * 128],
                                    cx.xb[:, kc * CW + 1 + sc * 128: kc * CW + 1 + (sc + 1) * 128],
                                    ws["v"][:, kc * 128:(kc + 1) * 128],
                                    start=(kc == 0 and sci == 0),
                                    stop=(kc == NCH - 1 and sci == 3),
                                    skip_group_check=True),
                                waits=grp_w if (sci == 0 and kc == 0) else (),
                                inc="T" if (kc == NCH - 1 and sci == 3) else None)
            grp_done = ev
            for sci in range(4):
                sc = g * 4 + sci
                dst = vR[:, sc * 128:(sc + 1) * 128]
                vev = P.step("scalar",
                             lambda eng, it=it, sci=sci, d=dst: eng.activation(
                                 d, cx.psm[it][:, sci * 128:(sci + 1) * 128], AF.Copy),
                             waits=[grp_done], inc="S")
                v_evs.append(vev)
            psm_free[it] = v_evs[-1]
        strip_consumed["v"] = v_evs[-1]
        v_done = v_evs[-1]
        if c + 1 < NCH:
            for m, wdr in (("q", wq), ("k", wk), ("v", wv)):
                load_strip(cx, m, wdr, c + 1)

        for par in range(2):
            h = 2 * c + par
            qa, ka = qaug[par], kaug[par]
            sq1 = P.step("scalar",
                         lambda eng, qa=qa: eng.activation(sqb[:, :], qa[0:64, :], AF.Square, scale=C["immc"][0:64, 4:5]),
                         waits=[evac_evs[-1], sq_free], inc="S")
            q2evs = []
            for half in range(2):
                P.step("tensor",
                       lambda eng, half=half: eng.matmul(
                           cx.pbig[0][0:2, half * 512:(half + 1) * 512],
                           C["c1a"][:, :], C["ones1r"][0:1, 0:512],
                           start=True, stop=False),
                       waits=[sq1, pb_free[0]] if half == 0 else ())
                ev = P.step("tensor",
                            lambda eng, half=half: eng.matmul(
                                cx.pbig[0][0:2, half * 512:(half + 1) * 512],
                                C["c64b"][:, :], sqb[:, half * 512: (half + 1) * 512],
                                start=False, stop=True),
                            inc="T")
                q2evs.append(ev)
            q2evs2 = []
            for half in range(2):
                P.step("tensor",
                       lambda eng, half=half: eng.matmul(
                           cx.pbig[1][0:2, half * 512:(half + 1) * 512],
                           C["c1a"][:, :], C["ones1r"][0:1, 0:512],
                           start=True, stop=False),
                       waits=[sq1, pb_free[1]] if half == 0 else ())
                ev = P.step("tensor",
                            lambda eng, half=half: eng.matmul(
                                cx.pbig[1][0:2, half * 512:(half + 1) * 512],
                                C["c64b"][:, :], sqb[:, 1024 + half * 512: 1024 + (half + 1) * 512],
                                start=False, stop=True),
                            inc="T")
                q2evs2.append(ev)
            eq = P.step("scalar",
                        lambda eng, qa=qa: eng.activation(
                            qa[64:66, 0:1024], cx.pbig[0][0:2, :], AF.Copy,
                            scale=C["sc_q"][:, 0:1]),
                        waits=q2evs, inc="S")
            eq2 = P.step("scalar",
                         lambda eng, qa=qa: eng.activation(
                             qa[64:66, 1024:2048], cx.pbig[1][0:2, :], AF.Copy,
                             scale=C["sc_q"][:, 0:1]),
                         waits=q2evs2, inc="S")
            sq2 = P.step("scalar",
                         lambda eng, ka=ka: eng.activation(sqb[:, :], ka[0:64, :], AF.Square),
                         waits=[eq, eq2] + q2evs + q2evs2, inc="S")
            k2evs = []
            for (pb, base) in ((0, 0), (1, 1024)):
                for half in range(2):
                    P.step("tensor",
                           lambda eng, pb=pb, half=half: eng.matmul(
                               cx.pbig[pb][0:2, half * 512:(half + 1) * 512],
                               C["c1b"][:, :], C["ones1r"][0:1, 0:512],
                               start=True, stop=False),
                           waits=[sq2] if half == 0 else ())
                    ev = P.step("tensor",
                                lambda eng, pb=pb, base=base, half=half: eng.matmul(
                                    cx.pbig[pb][0:2, half * 512:(half + 1) * 512],
                                    C["c64a"][:, :], sqb[:, base + half * 512: base + (half + 1) * 512],
                                    start=False, stop=True),
                                inc="T")
                    k2evs.append(ev)
            ek = P.step("scalar",
                        lambda eng, ka=ka: eng.activation(
                            ka[64:66, 0:1024], cx.pbig[0][0:2, :], AF.Copy,
                            scale=C["sc_k"][:, 0:1]),
                        waits=k2evs[:2], inc="S")
            ek2 = P.step("scalar",
                         lambda eng, ka=ka: eng.activation(
                             ka[64:66, 1024:2048], cx.pbig[1][0:2, :], AF.Copy,
                             scale=C["sc_k"][:, 0:1]),
                         waits=k2evs[2:], inc="S")
            sq_free = ek2
            pb_free[0] = ek
            pb_free[1] = ek2

            exp_evs = []
            stq_waits = [eq, eq2, ek, ek2]
            for jc in range(NSC):
                for tp in range(2):
                    pbi = (jc * 2 + tp) % 2
                    w = [pb_free[pbi]] + (stq_waits if jc == 0 and tp == 0 else [])
                    P.step("tensor",
                           lambda eng, pbi=pbi, jc=jc, tp=tp, ka=ka, qa=qa: eng.matmul(
                               cx.pbig[pbi][:, 0:512],
                               ka[0:66, jc * 128:(jc + 1) * 128],
                               qa[0:66, tp * 1024: tp * 1024 + 512],
                               start=True, stop=True),
                           waits=w)
                    mev = P.step("tensor",
                                 lambda eng, pbi=pbi, jc=jc, tp=tp, ka=ka, qa=qa: eng.matmul(
                                     cx.pbig[pbi][:, 512:1024],
                                     ka[0:66, jc * 128:(jc + 1) * 128],
                                     qa[0:66, tp * 1024 + 512: tp * 1024 + 1024],
                                     start=True, stop=True),
                                 inc="T")
                    eev = P.step("scalar",
                                 lambda eng, pbi=pbi, jc=jc, tp=tp: eng.activation(
                                     ST[:, jc * N + tp * 1024: jc * N + (tp + 1) * 1024],
                                     cx.pbig[pbi][:, :], AF.Exp,
                                     accum_out=ncacc[:, jc * 2 + tp: jc * 2 + tp + 1]),
                                 waits=[mev] + ([cx.b0_r1_prev] if jc == 0 and tp == 0 else []),
                                 inc="S")
                    exp_evs.append(eev)
                    pb_free[pbi] = eev
            r1 = P.step("vector",
                        lambda eng: eng.tensor_reduce(
                            nc16[:, :], ncacc[:, :].rearrange("p (j t) -> p j t", t=2),
                            AX.X, OP.add),
                        waits=[exp_evs[-1], exp_evs[-2]], inc="V")
            cx.b0_r1_prev = r1
            r2 = P.step("scalar",
                        lambda eng: eng.activation(cln16[:, :], nc16[:, :], AF.Ln),
                        waits=[r1], inc="S")
            r3 = P.step("scalar",
                        lambda eng: eng.activation(c16[:, :], cln16[:, :], AF.Exp, scale=C["immc"][:, 1:2]),
                        inc="S")
            vc_evs = []
            for sc in range(NSC):
                vsrc = vR[:, sc * 128 + 64 * par: sc * 128 + 64 * par + 64]
                vev = P.step("vector",
                             lambda eng, sc=sc, vsrc=vsrc: eng.tensor_scalar(
                                 Vc[:, sc * 128:sc * 128 + 64], vsrc,
                                 c16[:, sc:sc + 1], None, OP.mult),
                             waits=[r3, v_done, Vc_free], inc="V")
                vev2 = P.step("vector",
                              lambda eng, sc=sc: eng.tensor_scalar(
                                  Vc[:, sc * 128 + 64:(sc + 1) * 128], C["ones64"][:, :],
                                  c16[:, sc:sc + 1], None, OP.mult),
                              inc="V")
                vc_evs.append(vev2)
            fin_evs = []
            for it in range(NIT):
                for jc in range(NSC):
                    ev = P.step("tensor",
                                lambda eng, it=it, jc=jc: eng.matmul(
                                    cx.psm[it][:, :],
                                    Vc[:, jc * 128:(jc + 1) * 128],
                                    ST[:, jc * N + it * 512: jc * N + (it + 1) * 512],
                                    start=(jc == 0), stop=(jc == NSC - 1)),
                                waits=([vc_evs[-1], psm_free[it]] + exp_evs[-1:]) if jc == 0 else (),
                                inc="T" if jc == NSC - 1 else None)
                pv_done = ev
                rc = P.step("vector",
                            lambda eng, it=it: eng.reciprocal(trec[:, :], cx.psm[it][64:128, :]),
                            waits=[pv_done], inc="V")
                fo = P.step("vector",
                            lambda eng, it=it, h=h: eng.tensor_tensor(
                                cx.OT[(h % 2) * 64:(h % 2) * 64 + 64,
                                      (h // 2) * N + it * 512:(h // 2) * N + (it + 1) * 512],
                                cx.psm[it][0:64, :], trec[:, :], OP.mult),
                            waits=[rc, getattr(cx, "xgate", None)], inc="V")
                psm_free[it] = fo
                fin_evs.append(fo)
                OT_evs.append(fo)
            Vc_free = fin_evs[-1]

    sub_evs = emit_outproj(cx, wo, boc, OT_evs[-1],
                           [strip_consumed["q"], strip_consumed["k"]], pevs,
                           bv_col=bvcb, bias_out=boc2)

    out_evs = emit_ln(cx, "post", src_ready=sub_evs[-8:], resid_dram=resid_dram, tag="b0ln2",
                      src_chunks=[sub_evs[4 * c_ + 3] for c_ in range(NCH)],
                      scratch_free=OT_evs[-1])
    return out_evs


def emit_b2(cx, resid_dram, src_ready=()):
    P, nc, C, ar = cx.P, cx.nc, cx.C, cx.ar
    wqkv = nc.dram_tensor("b2_wqkv", [D, 3 * D], F32, kind="ExternalInput").ap()
    wout = nc.dram_tensor("b2_wout", [D, D], F32, kind="ExternalInput").ap()
    glogd = nc.dram_tensor("b2_glog", [16, N], F32, kind="ExternalInput").ap()

    qaug = cx.ln_xs
    kaug = cx.ln_sq
    vR, Vc, ws = cx.vR, cx.Vc, cx.ws
    wev, strip_consumed = cx.wev, cx.strip_consumed
    psm_free, pb_free = cx.psm_free, cx.pb_free
    gsig = getattr(cx, "b2_gsig", None)
    if gsig is None:
        gsig = ar.alloc_at("b2_gsig", [16, N], BF16, cx.R_OFF + 8192)
        glog_sb = ar.alloc_at("b2_glog", [16, N], BF16, cx.R_OFF + 12288)
        cx.b2_gsig, cx.b2_glog = gsig, glog_sb
    glog_sb = cx.b2_glog
    PCt, pc_evs = load_pc(cx, ["l1m", "hrep"], waits=[cx.pc_free_ev])

    gev = P.step("gpsimd", lambda eng: eng.dma_start(out=glog_sb[:, :], in_=glogd[:, :]),
                 waits=[cx.bs_free_ev], inc="DG", amt=16)
    sev = P.step("scalar", lambda eng: eng.activation(gsig[:, :], glog_sb[:, :], AF.Sigmoid),
                 waits=[gev], inc="S")

    ln1_evs = emit_ln(cx, "pre", src_ready=list(src_ready), tag="b2ln1")
    xb_full = ln1_evs

    vones = cx.Vc[:, :].rearrange("p (s b) -> p s b", b=128)[:, :, 64:]
    ones_ev = P.step("vector", lambda eng: eng.memset(vones, 1.0), inc="V")

    for m, off in (("q", 0), ("k", 8), ("v", 16)):
        load_strip(cx, m, wqkv, off)

    OT_evs = []
    Vc_free = None
    eb_free = [None, None]
    for c in range(NCH):
        evac_evs = []
        for (mat, augA, augB) in (("q", qaug[0], qaug[1]), ("k", kaug[0], kaug[1])):
            for it in range(NIT):
                mm_w = [wev[mat], psm_free[it]] + (xb_full if c == 0 and it == 0 else [])
                for kc in range(NCH):
                    ev = P.step("tensor",
                                lambda eng, it=it, kc=kc, mat=mat: eng.matmul(
                                    cx.psm[it][:, :],
                                    ws[mat][:, kc * 128:(kc + 1) * 128],
                                    cx.xb[:, kc * CW + 1 + it * 512: kc * CW + 1 + (it + 1) * 512],
                                    start=(kc == 0), stop=(kc == NCH - 1)),
                                waits=mm_w if kc == 0 else (),
                                inc="T" if kc == NCH - 1 else None)
                    if kc == NCH - 1:
                        mm_done = ev
                for par, aug in ((0, augA), (1, augB)):
                    eev = P.step("scalar",
                                 lambda eng, it=it, par=par, aug=aug: eng.activation(
                                     aug[0:64, it * 512:(it + 1) * 512],
                                     cx.psm[it][par * 64:(par + 1) * 64, :], AF.Copy),
                                 waits=[mm_done], inc="S")
                    evac_evs.append(eev)
                psm_free[it] = evac_evs[-1]
        strip_consumed["q"] = evac_evs[7]
        strip_consumed["k"] = evac_evs[-1]
        v_evs = []
        for g in range(4):
            it = g
            grp_w = [wev["v"], psm_free[it]]
            for sci in range(4):
                sc = g * 4 + sci
                for kc in range(NCH):
                    ev = P.step("tensor",
                                lambda eng, it=it, sci=sci, sc=sc, kc=kc: eng.matmul(
                                    cx.psm[it][:, sci * 128:(sci + 1) * 128],
                                    cx.xb[:, kc * CW + 1 + sc * 128: kc * CW + 1 + (sc + 1) * 128],
                                    ws["v"][:, kc * 128:(kc + 1) * 128],
                                    start=(kc == 0 and sci == 0), stop=(kc == NCH - 1 and sci == 3),
                                    skip_group_check=True),
                                waits=grp_w if (sci == 0 and kc == 0) else (),
                                inc="T" if (kc == NCH - 1) else None)
            grp_done = ev
            for sci in range(4):
                sc = g * 4 + sci
                vev = P.step("scalar",
                             lambda eng, it=it, sci=sci, sc=sc: eng.activation(
                                 vR[:, sc * 128:(sc + 1) * 128],
                                 cx.psm[it][:, sci * 128:(sci + 1) * 128], AF.Copy),
                             waits=[grp_done], inc="S")
                v_evs.append(vev)
            psm_free[it] = v_evs[-1]
        strip_consumed["v"] = v_evs[-1]
        v_done = v_evs[-1]
        if c + 1 < NCH:
            for m, off in (("q", 0), ("k", 8), ("v", 16)):
                load_strip(cx, m, wqkv, off + c + 1)

        for par in range(2):
            h = 2 * c + par
            qa, ka = qaug[par], kaug[par]
            gq_evs = []
            for it in range(NIT):
                gm = P.step("tensor",
                            lambda eng, it=it, h=h: eng.matmul(
                                cx.pbig[0][0:64, (it % 2) * 512:(it % 2) * 512 + 512],
                                PCt["hrep"][:, h * 64:(h + 1) * 64],
                                gsig[:, it * 512:(it + 1) * 512],
                                start=True, stop=True),
                            waits=[sev, evac_evs[7], pc_evs[1],
                                   pb_free[0] if it < 2 else gq_evs[it - 2]],
                            inc="T")
                gq = P.step("vector",
                            lambda eng, it=it, qa=qa: eng.scalar_tensor_tensor(
                                qa[0:64, it * 512:(it + 1) * 512],
                                qa[0:64, it * 512:(it + 1) * 512], 0.125,
                                cx.pbig[0][0:64, (it % 2) * 512:(it % 2) * 512 + 512],
                                OP.mult, OP.mult),
                            waits=[gm], inc="V")
                gq_evs.append(gq)
            pb_free[0] = gq_evs[-1]
            vc_evs = []
            for sc in range(NSC):
                vev = P.step("vector",
                             lambda eng, sc=sc, par=par: eng.tensor_copy(
                                 Vc[:, sc * 128:sc * 128 + 64],
                                 vR[:, sc * 128 + 64 * par: sc * 128 + 64 * par + 64]),
                             waits=[ones_ev, v_done, Vc_free], inc="V")
                vc_evs.append(vev)
            fin_evs = []
            pv_last = {}
            glist = []
            for jc in range(NSC):
                its = list(range(jc // 4, NIT))
                glist += [(jc, gi, grp) for gi, grp in
                          enumerate(its[i:i + 2] for i in range(0, len(its), 2))]
            g_stash = {}

            def b2_st(gx):
                jc, gi, grp = glist[gx]
                pbi = gx % 2
                ebi = gx % 2
                stw = [pb_free[pbi]] + ([gq_evs[-1], vc_evs[-1], evac_evs[-1]]
                                        if gx == 0 else [])
                ev = None
                for k2, it in enumerate(grp):
                    ev = P.step("tensor",
                                lambda eng, pbi=pbi, k2=k2, it=it, jc=jc, qa=qa, ka=ka: eng.matmul(
                                    cx.pbig[pbi][:, k2 * 512:(k2 + 1) * 512],
                                    ka[0:64, jc * 128:(jc + 1) * 128],
                                    qa[0:64, it * 512:(it + 1) * 512],
                                    start=True, stop=True),
                                waits=stw if k2 == 0 else (), inc="T")
                st_done = ev
                wexp = [st_done, eb_free[ebi]]
                eev = P.step("scalar",
                             lambda eng, pbi=pbi, ebi=ebi, w=len(grp): eng.activation(
                                 cx.Ebuf[ebi][:, 0:w * 512],
                                 cx.pbig[pbi][:, 0:w * 512], AF.Exp),
                             waits=wexp, inc="S")
                pb_free[pbi] = eev
                mev = eev
                if grp[0] == jc // 4:
                    mev = P.step("vector",
                                 lambda eng, ebi=ebi, jc=jc: eng.tensor_tensor(
                                     cx.Ebuf[ebi][:, 0:512], cx.Ebuf[ebi][:, 0:512],
                                     PCt["l1m"][:, (jc % 4) * 512:(jc % 4 + 1) * 512],
                                     OP.mult),
                                 waits=[eev, pc_evs[0]], inc="V")
                g_stash[gx] = mev

            def b2_pv(gx):
                jc, gi, grp = glist[gx]
                ebi = gx % 2
                mev = g_stash.pop(gx)
                pv_evs = []
                for k2, it in enumerate(grp):
                    ev = P.step("tensor",
                                lambda eng, ebi=ebi, k2=k2, it=it, jc=jc: eng.matmul(
                                    cx.psm[it][:, :],
                                    Vc[:, jc * 128:(jc + 1) * 128],
                                    cx.Ebuf[ebi][:, k2 * 512:(k2 + 1) * 512],
                                    start=(jc == 0), stop=(jc == 4 * it + 3)),
                                waits=[mev] + ([psm_free[it]] if jc == 0 else []),
                                inc="T")
                    pv_evs.append(ev)
                    pv_last[it] = ev
                    if jc == 4 * it + 3:
                        rc = P.step("vector",
                                    lambda eng, it=it: eng.reciprocal(
                                        cx.trec[:, :], cx.psm[it][64:128, :]),
                                    waits=[ev], inc="V")
                        fo = P.step("vector",
                                    lambda eng, it=it, h=h: eng.tensor_tensor(
                                        cx.OT[(h % 2) * 64:(h % 2) * 64 + 64,
                                              (h // 2) * N + it * 512:(h // 2) * N + (it + 1) * 512],
                                        cx.psm[it][0:64, :], cx.trec[:, :], OP.mult),
                                    waits=[rc, getattr(cx, "xgate", None)], inc="V")
                        psm_free[it] = fo
                        fin_evs.append(fo)
                        OT_evs.append(fo)
                eb_free[ebi] = pv_evs[-1]

            for gx in range(len(glist)):
                b2_st(gx)
                if gx > 0:
                    b2_pv(gx - 1)
            b2_pv(len(glist) - 1)
            Vc_free = fin_evs[-1]

    sub_evs = emit_outproj(cx, wout, None, OT_evs[-1],
                           [strip_consumed["q"], strip_consumed["k"]], [])
    out_evs = emit_ln(cx, "post", src_ready=sub_evs[-8:], resid_dram=resid_dram, tag="b2ln2",
                      src_chunks=[sub_evs[4 * c_ + 3] for c_ in range(NCH)],
                      scratch_free=OT_evs[-1])
    return out_evs


def emit_b3(cx, resid_dram, src_ready=()):
    P, nc, C, ar = cx.P, cx.nc, cx.C, cx.ar
    AR = nc.dram_tensor("b3_AR", [4, 512, 512], F32, kind="ExternalInput").ap()
    BR = nc.dram_tensor("b3_BR", [4, 512, 512], F32, kind="ExternalInput").ap()
    AI = nc.dram_tensor("b3_AI", [4, 512, 512], F32, kind="ExternalInput").ap()
    BI = nc.dram_tensor("b3_BI", [4, 512, 512], F32, kind="ExternalInput").ap()
    brs = nc.dram_tensor("b3_brs", [128, 4], F32, kind="ExternalInput").ap()
    bis = nc.dram_tensor("b3_bis", [128, 4], F32, kind="ExternalInput").ap()

    if not hasattr(cx, "b3_wr"):
        o = cx.R_OFF
        cx.b3_wr = ar.alloc_at("b3_wr", [128, 4 * 512], BF16, o)
        cx.b3_wi = ar.alloc_at("b3_wi", [128, 4 * 512], BF16, o + 4096)
        cx.b3_wineg = ar.alloc_at("b3_wineg", [128, 4 * 512], BF16, o + 8192)
        cx.b3_a = [ar.alloc_at(f"b3_a{i}", [128, 4 * 512], BF16, o + 12288 + i * 4096) for i in range(2)]
        cx.b3_b = [ar.alloc_at(f"b3_b{i}", [128, 4 * 512], BF16, o + 20480 + i * 4096) for i in range(2)]
    wr, wi, wineg = cx.b3_wr, cx.b3_wi, cx.b3_wineg
    abuf, bbuf = cx.b3_a, cx.b3_b
    brc, bic = cx.b3_bb, cx.b3_bb2

    pevs = []
    pevs.append(P.step("sync", lambda eng: eng.dma_start(out=brc[:, :], in_=brs[:, :]), inc="DS", amt=16))
    pevs.append(P.step("sync", lambda eng: eng.dma_start(out=bic[:, :], in_=bis[:, :]), inc="DS", amt=16))

    psm_free = cx.psm_free
    wr_evs = []
    ld_consumed = [None, None]
    for half, (Adr, Bdr, wdst) in enumerate(((AR, BR, wr), (AI, BI, wi))):
        mm_last = None
        for cc in range(4):
            bi = cc % 2
            src_a = Adr[cc].rearrange("(kc p) m -> p kc m", p=128)
            src_b = Bdr[cc].rearrange("(kc p) m -> p kc m", p=128)
            ea = P.step("gpsimd",
                        lambda eng, d=abuf[bi], s=src_a: eng.dma_start(
                            out=d[:, :].rearrange("p (kc m) -> p kc m", m=512), in_=s),
                        waits=[ld_consumed[bi], cx.bs_free_ev], inc="DG", amt=16)
            eb = P.step("gpsimd",
                        lambda eng, d=bbuf[bi], s=src_b: eng.dma_start(
                            out=d[:, :].rearrange("p (kc m) -> p kc m", m=512), in_=s),
                        inc="DG", amt=16)
            for mc in range(4):
                for kc in range(4):
                    w0 = [ea, eb, psm_free[mc]] if kc == 0 else ()
                    ev = P.step("tensor",
                                lambda eng, mc=mc, kc=kc, bi=bi, cc=cc, half=half: eng.matmul(
                                    cx.psm[mc][:, :],
                                    abuf[bi][:, kc * 512 + mc * 128: kc * 512 + (mc + 1) * 128],
                                    bbuf[bi][:, kc * 512:(kc + 1) * 512],
                                    start=(cc == 0 and kc == 0), stop=(cc == 3 and kc == 3)),
                                waits=w0, inc="T" if kc == 3 else None)
            ld_consumed[bi] = ev
            mm_last = ev
        for mc in range(4):
            e1 = P.step("scalar",
                        lambda eng, mc=mc, wdst=wdst: eng.activation(
                            wdst[:, mc * 512:(mc + 1) * 512], cx.psm[mc][:, :], AF.Copy),
                        waits=[mm_last] + cx.init_evs, inc="S")
            if half == 1:
                e2 = P.step("scalar",
                            lambda eng, mc=mc: eng.activation(
                                wineg[:, mc * 512:(mc + 1) * 512], cx.psm[mc][:, :],
                                AF.Identity, scale=C["immc"][:, 6:7]),
                            inc="S")
                psm_free[mc] = e2
            else:
                psm_free[mc] = e1
            wr_evs.append(e1)
    w_ready = wr_evs[-1]

    ln1_evs = emit_ln(cx, "pre", src_ready=list(src_ready), tag="b3ln1")
    xb_full = ln1_evs

    sub_evs = []
    for half in range(2):
        terms = ((wr, 0), (wineg, 4)) if half == 0 else ((wi, 0), (wr, 4))
        bias = brc if half == 0 else bic
        for oc in range(4):
            for it in range(NIT):
                w0 = [w_ready, psm_free[it]] + (xb_full if half == 0 and oc == 0 and it == 0 else [])
                first = True
                for wt, choff in terms:
                    for kc in range(4):
                        ev = P.step("tensor",
                                    lambda eng, wt=wt, choff=choff, kc=kc, oc=oc, it=it,
                                    first=first, last=(choff == terms[1][1] and kc == 3): eng.matmul(
                                        cx.psm[it][:, :],
                                        wt[:, kc * 512 + oc * 128: kc * 512 + (oc + 1) * 128],
                                        cx.xb[:, (choff + kc) * CW + 1 + it * 512:
                                              (choff + kc) * CW + 1 + (it + 1) * 512],
                                        start=first, stop=last),
                                    waits=w0 if first else (), inc="T" if (choff == terms[1][1] and kc == 3) else None)
                        first = False
                mm_done = ev
                sev = P.step("scalar",
                             lambda eng, half=half, oc=oc, it=it, bias=bias: eng.activation(
                                 cx.subout[:, (half * 4 + oc) * N + it * 512:
                                           (half * 4 + oc) * N + (it + 1) * 512],
                                 cx.psm[it][:, :], AF.Identity, bias=bias[:, oc:oc + 1]),
                             waits=[mm_done, getattr(cx, "xgate", None)] + pevs, inc="S")
                psm_free[it] = sev
                sub_evs.append(sev)

    out_evs = emit_ln(cx, "post", src_ready=sub_evs[-8:], resid_dram=resid_dram, tag="b3ln2",
                      src_chunks=[sub_evs[4 * c_ + 3] for c_ in range(NCH)],
                      scratch_free=sub_evs[-1])
    return out_evs


def host_b3_inputs(inp):
    f = np.float32
    d = {}
    G1R = np.asarray(inp["L1_ffn_G1_R"], f); G2R = np.asarray(inp["L1_ffn_G2_R"], f)
    G1I = np.asarray(inp["L1_ffn_G1_I"], f); G2I = np.asarray(inp["L1_ffn_G2_I"], f)

    def build_AB(G1, G2):
        A = np.zeros((4, 512, 512), f)
        B = np.zeros((4, 512, 512), f)
        for c in range(4):
            G1c = G1[0, :, c, :]
            G2c = G2[c, :, 0, :]
            for g in range(16):
                A[c, np.arange(32)[:, None] * 16 + g, np.arange(32)[None, :] * 16 + g] = G1c
            for b in range(32):
                B[c, b * 16 + np.arange(16)[:, None], b * 16 + np.arange(16)[None, :]] = G2c.T
        return np.ascontiguousarray(A), np.ascontiguousarray(B)

    d["b3_AR"], d["b3_BR"] = build_AB(G1R, G2R)
    d["b3_AI"], d["b3_BI"] = build_AB(G1I, G2I)
    d["b3_brs"] = np.ascontiguousarray(np.asarray(inp["L1_ffn_bias_real"], f).reshape(4, 128).T)
    d["b3_bis"] = np.ascontiguousarray(np.asarray(inp["L1_ffn_bias_imag"], f).reshape(4, 128).T)
    return d


def host_b2_inputs(inp):
    f = np.float32
    d = {}
    d["b2_wqkv"] = np.ascontiguousarray(np.asarray(inp["L1_attn_w_qkv"], f))
    d["b2_wout"] = np.ascontiguousarray(np.asarray(inp["L1_attn_w_out"], f))
    bg = np.asarray(inp["L1_attn_bandha_gate"], f)
    cycles = np.array([[5, 6, 7, 8][i % 4] for i in range(16)])
    phases = np.arange(N)[None, :] % cycles[:, None]
    d["b2_glog"] = np.ascontiguousarray(np.take_along_axis(bg, phases, axis=1).astype(f))
    return d


def host_b0_inputs(inp):
    f = np.float32
    d = {}
    d["b0_wqT"] = np.ascontiguousarray(np.asarray(inp["L0_attn_wq"], f).T)
    d["b0_wkT"] = np.ascontiguousarray(np.asarray(inp["L0_attn_wk"], f).T)
    d["b0_wvT"] = np.ascontiguousarray(np.asarray(inp["L0_attn_wv"], f).T)
    d["b0_woT"] = np.ascontiguousarray(np.asarray(inp["L0_attn_wo"], f).T)
    d["b0_bqs"] = np.ascontiguousarray(np.asarray(inp["L0_attn_bq"], f).reshape(8, 128).T)
    d["b0_bks"] = np.ascontiguousarray(np.asarray(inp["L0_attn_bk"], f).reshape(8, 128).T)
    d["b0_bos"] = np.ascontiguousarray(np.asarray(inp["L0_attn_bo"], f).reshape(8, 128).T)
    d["b0_bvs"] = np.ascontiguousarray(np.asarray(inp["L0_attn_bv"], f).reshape(8, 128).T)
    return d


def host_const_inputs(cx):
    d = {"c_" + k: v for k, v in cx.host_const_arrays.items()}
    d.update({"pc_" + k: v for k, v in cx.host_pc_arrays.items()})
    return d


def build_b2_program():
    nc = bass.Bass("TRN2", target_bir_lowering=False, debug=False)
    cx = build_base(nc)
    emit_init(cx)
    emit_load_x(cx)
    out_evs = emit_b2(cx, resid_dram=cx.xin)
    emit_store_x(cx, out_evs)
    with nc.Block() as block:
        cx.P.emit(block)
    return nc, cx


def build_fused_program():
    nc = bass.Bass("TRN2", target_bir_lowering=False, debug=False)
    cx = build_base(nc)
    cx.b1_eb_free = [None, None]
    emit_init(cx)
    emit_load_x(cx)
    ev0 = emit_b0(cx, weights_ready=[], resid_dram=cx.xin)
    sc1, sp1 = emit_spill_x(cx, waits=ev0[-4:])
    xTz = cx.xT[:, :].rearrange("p (c w) -> p c w", w=CW)[:, :, 0:1]
    rz = cx.P.step("vector", lambda eng: eng.memset(xTz, 0.0), waits=[ev0[-1]], inc="V")
    cx.pc_free_ev = ev0[-1]
    cx.bs_free_ev = ev0[-1]
    ev1 = emit_b1(cx, resid_dram=sc1, src_ready=ev0[-4:] + [rz])
    cx.pc_free_ev = ev1[-1]
    cx.bs_free_ev = ev1[-1]
    sc2, sp2 = emit_spill_x(cx, waits=ev1[-4:])
    ev2 = emit_b2(cx, resid_dram=sc2, src_ready=ev1[-4:])
    cx.pc_free_ev = ev2[-1]
    cx.bs_free_ev = ev2[-1]
    sc3, sp3 = emit_spill_x(cx, waits=ev2[-4:])
    ev3 = emit_b3(cx, resid_dram=sc3, src_ready=ev2[-4:])
    emit_store_x(cx, ev3)
    with nc.Block() as block:
        cx.P.emit(block)
    return nc, cx


def host_fused_inputs(inp, cx):
    d = {}
    d.update(host_b0_inputs(inp))
    d.update(host_b1_inputs(inp))
    d.update(host_b2_inputs(inp))
    d.update(host_b3_inputs(inp))
    d.update(host_const_inputs(cx))
    return d


def build_b3_program():
    nc = bass.Bass("TRN2", target_bir_lowering=False, debug=False)
    cx = build_base(nc)
    emit_init(cx)
    emit_load_x(cx)
    out_evs = emit_b3(cx, resid_dram=cx.xin)
    emit_store_x(cx, out_evs)
    with nc.Block() as block:
        cx.P.emit(block)
    return nc, cx


def build_b0_program():
    nc = bass.Bass("TRN2", target_bir_lowering=False, debug=False)
    cx = build_base(nc)
    emit_init(cx)
    emit_load_x(cx)
    out_evs = emit_b0(cx, weights_ready=[], resid_dram=cx.xin)
    emit_store_x(cx, out_evs)
    with nc.Block() as block:
        cx.P.emit(block)
    return nc, cx


def run_b0(x_full, inp):
    nc, cx = build_b0_program()
    wd = host_b0_inputs(inp)
    cd = host_const_inputs(cx)
    in_maps = []
    for corei in range(8):
        m = dict(wd)
        m.update(cd)
        m["xT_in"] = np.ascontiguousarray(x_full[corei].T)
        in_maps.append(m)
    res = run_bass_kernel_spmd(nc, in_maps, core_ids=list(range(8)))
    return np.stack([r["xT_out"].T for r in res.results])



B1_DEPTH = 6
B1_ATTN_ONLY = False
B1_SKIP_LN2 = False


def emit_b1(cx, resid_dram, src_ready=()):
    P, nc, C, ar = cx.P, cx.nc, cx.C, cx.ar
    DEPTH = 6
    DI2 = 2816
    Wqkv = nc.dram_tensor("b1_Wqkv", [DEPTH, D, 3 * D], F32, kind="ExternalInput").ap()
    Wo = nc.dram_tensor("b1_Wo", [DEPTH, D, D], F32, kind="ExternalInput").ap()
    W1p = nc.dram_tensor("b1_W1p", [DEPTH, D, 2 * DI2], F32, kind="ExternalInput").ap()
    W2p = nc.dram_tensor("b1_W2p", [DEPTH, DI2, D], F32, kind="ExternalInput").ap()
    qsc = nc.dram_tensor("b1_qsc", [128, DEPTH], F32, kind="ExternalInput").ap()
    ksc = nc.dram_tensor("b1_ksc", [128, DEPTH], F32, kind="ExternalInput").ap()
    lng = nc.dram_tensor("b1_lng", [128, DEPTH * NCH], F32, kind="ExternalInput").ap()
    lnb = nc.dram_tensor("b1_lnb", [128, DEPTH * NCH], F32, kind="ExternalInput").ap()
    b1a = nc.dram_tensor("b1_b1as", [128, DEPTH * 22], F32, kind="ExternalInput").ap()
    b1g = nc.dram_tensor("b1_b1gs", [128, DEPTH * 22], F32, kind="ExternalInput").ap()
    b2s = nc.dram_tensor("b1_b2s", [128, DEPTH * NCH], F32, kind="ExternalInput").ap()

    if not hasattr(cx, "b1_qq"):
        o = cx.R_OFF + 20992
        cx.b1_qq = ar.alloc_at("b1_qq", [128, NCH * 512], BF16, o)
        cx.b1_kq = ar.alloc_at("b1_kq", [128, NCH * 768], BF16, o + 8192)
        cx.b1_vq = ar.alloc_at("b1_vq", [128, 6 * 1024], BF16, o + 20480)
        cx.b1_OTq = ar.alloc_at("b1_OTq", [128, NCH * 512], BF16, o + 32768)
        cx.b1_wv = ar.alloc_at("b1_wv", [128, NCH * 256], BF16, o + 40960)
        cx.b1_Mst = ar.alloc_at("b1_Mst", [128, 22 * 1024], BF16, cx.R_OFF)
        cx.b1_w1s = [ar.alloc_at(f"b1_w1s{i}", [128, 8 * 128], BF16, cx.R_OFF + 45056 + i * 2048)
                     for i in range(4)]
        cx.b1_w2s = [ar.alloc_at(f"b1_w2s{i}", [128, 22 * 128], BF16, cx.R_OFF + 53248 + i * 5632)
                     for i in range(2)]
    qq, kq, vq, OTq = cx.b1_qq, cx.b1_kq, cx.b1_vq, cx.b1_OTq
    wvf, sqq = cx.b1_wv, cx.b1_sqq
    t0b, t1b, t2b = cx.b1_t0, cx.b1_t1, cx.b1_t2
    q2c, rfq, rfq2 = cx.b1_q2c, cx.b1_rfq, cx.b1_rfq2
    Mst, w1s, w2s = cx.b1_Mst, cx.b1_w1s, cx.b1_w2s

    stash = cx.b1_stash
    pevs = []
    for name, dr_ in (("qsc", qsc), ("ksc", ksc), ("lng", lng), ("lnb", lnb),
                      ("b1a", b1a), ("b1g", b1g), ("b2s", b2s)):
        sb = stash[name]
        pevs.append(P.step("sync", lambda eng, sb=sb, dr_=dr_: eng.dma_start(out=sb[:, :], in_=dr_[:, :]),
                           inc="DS", amt=16))

    psm_free = cx.psm_free
    pb_free = cx.pb_free

    ln1_evs = emit_ln(cx, "pre", src_ready=list(src_ready), out_xT=True, tag="b1ln1")
    xb_full = ln1_evs

    xb_ready = ln1_evs[-1]

    vq_tail_ev = None
    kq_tail_ev = None

    pc_names = ["hsel", "hselT", "l0m", "rtAq", "rtAk", "rtBq", "rtBk"]
    for l in range(B1_DEPTH):
        PCt, pc_evs = load_pc(cx, pc_names,
                              waits=[cx.pc_free_ev if l == 0 else None,
                                     getattr(cx, "b1_mst_war", None)])
        lpevs = pevs + pc_evs
        wql = Wqkv[l]
        wo_l = Wo[l]
        OT_evs = []
        resid_evs = []
        prep_done_prev = None
        for qt in range(4):
            tok0 = qt * 512
            pevq = {}
            for c in range(NCH):
                for mi, (mat, moff2) in enumerate((("q", 0), ("k", 8))):
                    lev = load_strip(cx, mat, wql, moff2 + c)
                    pb = (2 * c + mi) % 2
                    w0 = [lev, psm_free[pb], xb_ready] + ([prep_done_prev] if c == 0 else [])
                    for kc in range(NCH):
                        sh = 1 if kc >= 4 else 0
                        ev = P.step("tensor",
                                    lambda eng, c=c, kc=kc, mat=mat, sh=sh, tok0=tok0, pb=pb: eng.matmul(
                                        cx.psm[pb][:, :],
                                        cx.ws[mat][:, kc * 128:(kc + 1) * 128],
                                        cx.xb[:, kc * CW + 1 - sh + tok0: kc * CW + 1 - sh + tok0 + 512],
                                        start=(kc == 0), stop=(kc == NCH - 1)),
                                    waits=w0 if kc == 0 else (),
                                    inc="T" if kc == NCH - 1 else None)
                    if mat == "q":
                        eev = P.step("scalar",
                                     lambda eng, c=c, pb=pb: eng.activation(
                                         qq[:, c * 512:(c + 1) * 512], cx.psm[pb][:, :], AF.Copy),
                                     waits=[ev], inc="S")
                    else:
                        eev = P.step("scalar",
                                     lambda eng, c=c, pb=pb: eng.activation(
                                         kq[:, c * 768 + 256: c * 768 + 768], cx.psm[pb][:, :], AF.Copy),
                                     waits=[ev, kq_tail_ev], inc="S")
                    pevq[(mat, c)] = eev
                    psm_free[pb] = eev
                    cx.strip_consumed[mat] = ev
            for dt_ in range(4):
                wvsrc = wql[:, 2 * D + dt_ * 256: 2 * D + (dt_ + 1) * 256].rearrange(
                    "(kc p) m -> p kc m", p=128)
                wv_ev = P.step("gpsimd",
                               lambda eng, s=wvsrc: eng.dma_start(
                                   out=wvf[:, :].rearrange("p (kc m) -> p kc m", m=256), in_=s),
                               waits=[getattr(cx, "b1_wv_consumed", None),
                                      getattr(cx, "b1_mst_war", None)], inc="DG", amt=16)
                for sci in range(4):
                    scg = qt * 4 + sci
                    slot = 2 + sci
                    pi = 2 + (sci % 2)
                    w0 = [wv_ev, psm_free[pi], vq_tail_ev]
                    for kc in range(NCH):
                        ev = P.step("tensor",
                                    lambda eng, sci=sci, scg=scg, kc=kc, dt_=dt_, pi=pi: eng.matmul(
                                        cx.psm[pi][:, 0:256],
                                        cx.xb[:, kc * CW + 1 - (1 if kc >= 4 else 0) + scg * 128:
                                              kc * CW + 1 - (1 if kc >= 4 else 0) + (scg + 1) * 128],
                                        wvf[:, kc * 256:(kc + 1) * 256],
                                        start=(kc == 0), stop=(kc == NCH - 1)),
                                    waits=w0 if kc == 0 else (),
                                    inc="T" if kc == NCH - 1 else None)
                    vev = P.step("scalar",
                                 lambda eng, slot=slot, dt_=dt_, pi=pi: eng.activation(
                                     vq[:, slot * 1024 + dt_ * 256: slot * 1024 + (dt_ + 1) * 256],
                                     cx.psm[pi][0:128, 0:256], AF.Copy),
                                 waits=[ev], inc="S")
                    psm_free[pi] = vev
                cx.b1_wv_consumed = ev
            v_done = vev
            matspec = {}
            m3s = {}
            last_sq = prep_done_prev
            for mi, mat in enumerate(("q", "k")):
                buf = qq if mat == "q" else kq
                boff = 0 if mat == "q" else 256
                bw = 512 if mat == "q" else 768
                scol = stash["qsc" if mat == "q" else "ksc"][:, l:l + 1]
                tA = PCt["rtAq"] if mat == "q" else PCt["rtAk"]
                tB = PCt["rtBq"] if mat == "q" else PCt["rtBk"]
                rf = rfq if mat == "q" else rfq2
                bank = 1 if mat == "q" else 3
                pbq = 1 - mi
                matspec[mat] = (buf, boff, bw, scol, tA, tB, rf, bank)
                q2mm = []
                for c in range(NCH):
                    sev = P.step("scalar",
                                 lambda eng, buf=buf, c=c, boff=boff, bw=bw: eng.activation(
                                     sqq[:, :], buf[:, c * bw + boff: c * bw + boff + 512], AF.Square),
                                 waits=[pevq[(mat, c)], last_sq], inc="S")
                    ev = P.step("tensor",
                                lambda eng, c=c, hs=PCt["hsel"], pbq=pbq: eng.matmul(
                                    cx.pbig[pbq][0:16, 0:512], hs[:, c * 16:(c + 1) * 16],
                                    sqq[:, :], start=(c == 0), stop=(c == NCH - 1)),
                                waits=[sev, pb_free[pbq] if c == 0 else None] + (lpevs if c == 0 else []),
                                inc="T")
                    q2mm.append(ev)
                    last_sq = ev
                m1 = P.step("vector",
                            lambda eng, pbq=pbq: eng.tensor_scalar_max(
                                q2c[:, :], cx.pbig[pbq][0:16, 0:512], 1e-24),
                            waits=[q2mm[-1]] + ([m3s["q"]] if mat == "k" else []), inc="V")
                m2 = P.step("scalar",
                            lambda eng: eng.activation(q2c[:, :], q2c[:, :], AF.Ln),
                            waits=[m1], inc="S")
                m3 = P.step("scalar",
                            lambda eng, rf=rf: eng.activation(rf[:, :], q2c[:, :], AF.Exp,
                                                              scale=C["immc"][0:16, 1:2]),
                            inc="S")
                pb_free[pbq] = m1
                m3s[mat] = m3
            prep_ev = {}

            def chunk_prep(c):
                for mat in ("q", "k"):
                    buf, boff, bw, scol, tA, tB, rf, bank = matspec[mat]
                    rm = P.step("tensor",
                                lambda eng, c=c, ht=PCt["hselT"], rf=rf, bank=bank: eng.matmul(
                                    cx.psm[bank][:, :], ht[:, c * 128:(c + 1) * 128],
                                    rf[:, :], start=True, stop=True),
                                waits=[m3s[mat] if c == 0 else None, psm_free[bank]], inc="T")
                    e1 = P.step("vector",
                                lambda eng, buf=buf, c=c, boff=boff, bw=bw, scol=scol, bank=bank:
                                eng.scalar_tensor_tensor(
                                    t0b[:, :], buf[:, c * bw + boff: c * bw + boff + 512], scol,
                                    cx.psm[bank][:, :], OP.mult, OP.mult),
                                waits=[rm] + lpevs, inc="V")
                    psm_free[bank] = e1
                    e2 = P.step("vector",
                                lambda eng, tA=tA, tok0=tok0: eng.tensor_tensor(
                                    t1b[:, :], t0b[:, :], tA[:, tok0: tok0 + 512], OP.mult),
                                inc="V")
                    rtm = P.step("tensor",
                                 lambda eng, bank=bank: eng.matmul(
                                     cx.psm[bank][:, :], C["rot"][:, :], t0b[:, :],
                                     start=True, stop=True),
                                 waits=[e1], inc="T")
                    e3 = P.step("vector",
                                lambda eng, tB=tB, tok0=tok0, bank=bank: eng.tensor_tensor(
                                    t2b[:, :], cx.psm[bank][:, :], tB[:, tok0: tok0 + 512], OP.mult),
                                waits=[rtm], inc="V")
                    psm_free[bank] = e3
                    e8 = P.step("vector",
                                lambda eng, buf=buf, c=c, boff=boff, bw=bw: eng.tensor_tensor(
                                    buf[:, c * bw + boff: c * bw + boff + 512], t1b[:, :], t2b[:, :],
                                    OP.add),
                                inc="V")
                    prep_ev[(mat, c)] = e8
            la_fin = []
            items = [(bq, 2 * hc + par) for hc in range(NCH) for par in (0, 1) for bq in (0, 1)]
            la_stash = {}

            def st_phase(idx):
                bq, h = items[idx]
                B = 2 * qt + bq
                first_blk = (B == 0)
                njc = 2 if first_blk else 4
                jc0 = 2 if first_blk else 0
                par = h % 2
                hc = h // 2
                pbi = idx % 2
                ebi = pbi
                stw = [pb_free[pbi], prep_ev[("q", h // 2)], prep_ev[("k", h // 2)]]
                ev = None
                for j2 in range(njc):
                    jc = jc0 + j2
                    ev = P.step("tensor",
                                lambda eng, pbi=pbi, j2=j2, jc=jc, par=par, hc=hc, bq=bq: eng.matmul(
                                    cx.pbig[pbi][:, j2 * 256:(j2 + 1) * 256],
                                    kq[par * 64:(par + 1) * 64,
                                       hc * 768 + bq * 256 + jc * 128: hc * 768 + bq * 256 + (jc + 1) * 128],
                                    qq[par * 64:(par + 1) * 64, hc * 512 + bq * 256: hc * 512 + (bq + 1) * 256],
                                    start=True, stop=True),
                                waits=stw if j2 == 0 else (), inc="T")
                st_done = ev
                eev = P.step("scalar",
                             lambda eng, pbi=pbi, ebi=ebi, njc=njc: eng.activation(
                                 cx.Ebuf[ebi][:, 0:njc * 256], cx.pbig[pbi][:, 0:njc * 256], AF.Exp),
                             waits=[st_done, cx.b1_eb_free[ebi]], inc="S")
                pb_free[pbi] = eev
                mev = P.step("vector",
                             lambda eng, ebi=ebi, njc=njc, jc0=jc0, lm=PCt["l0m"]: eng.tensor_tensor(
                                 cx.Ebuf[ebi][:, 0:njc * 256], cx.Ebuf[ebi][:, 0:njc * 256],
                                 lm[:, jc0 * 256:(jc0 + njc) * 256], OP.mult),
                             waits=[eev], inc="V")
                la_stash[idx] = (mev, njc, jc0)

            def pv_phase(idx):
                bq, h = items[idx]
                mev, njc, jc0 = la_stash.pop(idx)
                pbi = idx % 2
                ebi = pbi
                pvw = [mev, psm_free[pbi * 2]]
                ev = None
                for j2 in range(njc):
                    jc = jc0 + j2
                    slot = 2 * bq + jc
                    P.step("tensor",
                           lambda eng, j2=j2, slot=slot, h=h, pbi=pbi, ebi=ebi, njc=njc: eng.matmul(
                               cx.psm[pbi * 2][0:64, 0:256],
                               vq[:, slot * 1024 + h * 64: slot * 1024 + (h + 1) * 64],
                               cx.Ebuf[ebi][:, j2 * 256:(j2 + 1) * 256],
                               start=(j2 == 0), stop=(j2 == njc - 1),
                               skip_group_check=True),
                           waits=pvw if j2 == 0 else ())
                    ev = P.step("tensor",
                                lambda eng, j2=j2, pbi=pbi, ebi=ebi, njc=njc: eng.matmul(
                                    cx.psm[pbi * 2][64:128, 0:256],
                                    C["ones64"][:, :],
                                    cx.Ebuf[ebi][:, j2 * 256:(j2 + 1) * 256],
                                    start=(j2 == 0), stop=(j2 == njc - 1),
                                    skip_group_check=True),
                                inc="T" if j2 == njc - 1 else None)
                pv_done = ev
                cx.b1_eb_free[ebi] = pv_done
                rc = P.step("vector",
                            lambda eng, pbi=pbi: eng.reciprocal(
                                cx.trec[:, 0:256], cx.psm[pbi * 2][64:128, 0:256]),
                            waits=[pv_done], inc="V")
                fo = P.step("vector",
                            lambda eng, pbi=pbi, h=h, bq=bq: eng.tensor_tensor(
                                OTq[(h % 2) * 64:(h % 2) * 64 + 64,
                                    (h // 2) * 512 + bq * 256:(h // 2) * 512 + (bq + 1) * 256],
                                cx.psm[pbi * 2][0:64, 0:256], cx.trec[:, 0:256], OP.mult),
                            waits=[rc], inc="V")
                psm_free[pbi * 2] = fo
                la_fin.append(fo)
                OT_evs.append(fo)

            next_st = 0
            chunk_prep(0)
            for c in range(1, NCH):
                chunk_prep(c)
                while next_st < 4 * c:
                    st_phase(next_st)
                    if next_st > 0:
                        pv_phase(next_st - 1)
                    next_st += 1
            while next_st < len(items):
                st_phase(next_st)
                if next_st > 0:
                    pv_phase(next_st - 1)
                next_st += 1
            pv_phase(len(items) - 1)
            prep_done = prep_ev[("k", NCH - 1)]
            if qt < 3:
                bc1 = P.step("vector",
                             lambda eng: eng.tensor_copy(
                                 vq[:, 0: 2 * 1024], vq[:, 4 * 1024: 6 * 1024]),
                             waits=[la_fin[-1], la_fin[-2]], inc="V")
                bc2 = P.step("vector",
                             lambda eng: eng.tensor_copy(
                                 kq[:, :].rearrange("p (c w) -> p c w", w=768)[:, :, 0:256],
                                 kq[:, :].rearrange("p (c w) -> p c w", w=768)[:, :, 512:768]),
                             inc="V")
                vq_tail_ev = bc1
                kq_tail_ev = bc2
            else:
                vq_tail_ev = None
                kq_tail_ev = None
            wo_evs = {}
            wo_consumed = [getattr(cx, "b1_woc0", None), getattr(cx, "b1_woc1", None)]

            def load_wo(ec, qt=qt):
                dst = cx.wos[ec % 2][:, :].rearrange("p (kc q) -> p kc q", q=128)
                war = [wo_consumed[ec % 2]]
                if qt == 0 and ec < 2:
                    war += [cx.strip_consumed["q"], cx.strip_consumed["k"]]
                wo_evs[ec] = P.step("gpsimd",
                                    lambda eng, d=dst, s=strip_src(wo_l, ec): eng.dma_start(
                                        out=d, in_=s),
                                    waits=war, inc="DG", amt=16)

            load_wo(0)
            load_wo(1)
            for ec in range(8):
                pi = ec % 4
                w0 = [wo_evs[ec], psm_free[pi]] + ([OT_evs[-1]] if ec == 0 else [])
                for dc in range(NCH):
                    ev = P.step("tensor",
                                lambda eng, ec=ec, dc=dc, pi=pi: eng.matmul(
                                    cx.psm[pi][:, :],
                                    cx.wos[ec % 2][:, dc * 128:(dc + 1) * 128],
                                    OTq[:, dc * 512:(dc + 1) * 512],
                                    start=(dc == 0), stop=(dc == NCH - 1)),
                                waits=w0 if dc == 0 else (),
                                inc="T" if dc == NCH - 1 else None)
                mm_done = ev
                wo_consumed[ec % 2] = mm_done
                rev = P.step("vector",
                             lambda eng, ec=ec, tok0=tok0, pi=pi: eng.tensor_tensor(
                                 cx.xT[:, ec * CW + 1 + tok0: ec * CW + 1 + tok0 + 512],
                                 cx.xT[:, ec * CW + 1 + tok0: ec * CW + 1 + tok0 + 512],
                                 cx.psm[pi][:, :], OP.add),
                             waits=[mm_done], inc="V")
                psm_free[pi] = rev
                resid_evs.append(rev)
                if ec + 2 < 8:
                    load_wo(ec + 2)
            cx.b1_woc0, cx.b1_woc1 = wo_consumed
            cx.strip_consumed["q"] = wo_consumed[0]
            cx.strip_consumed["k"] = wo_consumed[1]
            prep_done_prev = prep_done
        attn_done = resid_evs[-8:]
        if B1_ATTN_ONLY and l == B1_DEPTH - 1:
            break

        gb = (stash["lng"][:, l * NCH:(l + 1) * NCH], stash["lnb"][:, l * NCH:(l + 1) * NCH])
        ln_evs = emit_ln(cx, "pre", src_ready=attn_done, shifted=True, gain_bias=gb,
                         tag=f"b1l{l}ffnln", src_chunks=attn_done)
        xbf = ln_evs
        mst_war = getattr(cx, "b1_mst_war", None)
        ffn_ck = [None] * NCH
        xb_evs = []
        for hf in range(2):
            t0 = hf * 1024
            m_evs = []
            w1_consumed = list(getattr(cx, "b1_w1c", [None] * 4))
            for mc in range(22):
                sa = W1p[l][:, mc * 128:(mc + 1) * 128].rearrange("(kc p) m -> p kc m", p=128)
                sg = W1p[l][:, DI2 + mc * 128: DI2 + (mc + 1) * 128].rearrange("(kc p) m -> p kc m", p=128)
                bi_ = mc % 2
                ea = P.step("gpsimd",
                            lambda eng, s=sa, bi_=bi_: eng.dma_start(
                                out=w1s[bi_ * 2][:, :].rearrange("p (kc m) -> p kc m", m=128), in_=s),
                            waits=[w1_consumed[bi_ * 2]], inc="DG", amt=16)
                eg = P.step("gpsimd",
                            lambda eng, s=sg, bi_=bi_: eng.dma_start(
                                out=w1s[bi_ * 2 + 1][:, :].rearrange("p (kc m) -> p kc m", m=128), in_=s),
                            waits=[w1_consumed[bi_ * 2 + 1]], inc="DG", amt=16)
                for it2 in range(2):
                    pa, pg = (0, 1) if (mc + it2) % 2 == 0 else (2, 3)
                    w0a = [ea, psm_free[pa]] + (xbf if mc == 0 and it2 == 0 and hf == 0 else [])
                    w0g = [eg, psm_free[pg]]
                    for kc in range(NCH):
                        eva = P.step("tensor",
                                     lambda eng, mc=mc, it2=it2, kc=kc, pa=pa, t0=t0, bi_=bi_: eng.matmul(
                                         cx.psm[pa][:, :],
                                         w1s[bi_ * 2][:, kc * 128:(kc + 1) * 128],
                                         cx.xb[:, kc * CW + 1 + t0 + it2 * 512: kc * CW + 1 + t0 + (it2 + 1) * 512],
                                         start=(kc == 0), stop=(kc == NCH - 1)),
                                     waits=w0a if kc == 0 else (),
                                     inc="T" if kc == NCH - 1 else None)
                    for kc in range(NCH):
                        evg = P.step("tensor",
                                     lambda eng, mc=mc, it2=it2, kc=kc, pg=pg, t0=t0, bi_=bi_: eng.matmul(
                                         cx.psm[pg][:, :],
                                         w1s[bi_ * 2 + 1][:, kc * 128:(kc + 1) * 128],
                                         cx.xb[:, kc * CW + 1 + t0 + it2 * 512: kc * CW + 1 + t0 + (it2 + 1) * 512],
                                         start=(kc == 0), stop=(kc == NCH - 1)),
                                     waits=w0g if kc == 0 else (),
                                     inc="T" if kc == NCH - 1 else None)
                    w1_consumed[bi_ * 2] = eva
                    w1_consumed[bi_ * 2 + 1] = evg
                    gel = P.step("scalar",
                                 lambda eng, mc=mc, pg=pg, l=l: eng.activation(
                                     t0b[:, :], cx.psm[pg][:, :], AF.Gelu,
                                     bias=stash["b1g"][:, l * 22 + mc: l * 22 + mc + 1]),
                                 waits=[evg] + pevs + (m_evs[-1:] if m_evs else [])
                                 + ([mst_war] if mc == 0 and it2 == 0 and hf == 0 else []),
                                 inc="S")
                    mev = P.step("vector",
                                 lambda eng, mc=mc, it2=it2, pa=pa, l=l: eng.scalar_tensor_tensor(
                                     Mst[:, mc * 1024 + it2 * 512: mc * 1024 + (it2 + 1) * 512],
                                     cx.psm[pa][:, :], stash["b1a"][:, l * 22 + mc: l * 22 + mc + 1],
                                     t0b[:, :], OP.add, OP.mult),
                                 waits=[eva, gel], inc="V")
                    psm_free[pa] = mev
                    psm_free[pg] = mev
                    m_evs.append(mev)
            w2_consumed = list(getattr(cx, "b1_w2c", [None, None]))
            for ec in range(8):
                s2 = W2p[l][:, ec * 128:(ec + 1) * 128].rearrange("(kc p) m -> p kc m", p=128)
                e2 = P.step("gpsimd",
                            lambda eng, s=s2, ec=ec: eng.dma_start(
                                out=w2s[ec % 2][:, :].rearrange("p (kc m) -> p kc m", m=128), in_=s),
                            waits=[w2_consumed[ec % 2]]
                            + ([w1_consumed[2], w1_consumed[3]] if ec <= 1 else []),
                            inc="DG", amt=16)
                for it2 in range(2):
                    pi = (ec * 2 + it2) % 4
                    w0 = [e2, psm_free[pi]] + (m_evs[-4:] if ec == 0 and it2 == 0 else [])
                    for kc in range(22):
                        ev = P.step("tensor",
                                    lambda eng, ec=ec, it2=it2, kc=kc, pi=pi: eng.matmul(
                                        cx.psm[pi][:, :],
                                        w2s[ec % 2][:, kc * 128:(kc + 1) * 128],
                                        Mst[:, kc * 1024 + it2 * 512: kc * 1024 + (it2 + 1) * 512],
                                        start=(kc == 0), stop=(kc == 21)),
                                    waits=w0 if kc == 0 else (),
                                    inc="T" if kc == 21 else None)
                    mm_done = ev
                    w2_consumed[ec % 2] = mm_done
                    rev = P.step("vector",
                                 lambda eng, ec=ec, it2=it2, t0=t0, pi=pi, l=l: eng.scalar_tensor_tensor(
                                     cx.xT[:, ec * CW + 1 + t0 + it2 * 512: ec * CW + 1 + t0 + (it2 + 1) * 512],
                                     cx.psm[pi][:, :], stash["b2s"][:, l * NCH + ec: l * NCH + ec + 1],
                                     cx.xT[:, ec * CW + 1 + t0 + it2 * 512: ec * CW + 1 + t0 + (it2 + 1) * 512],
                                     OP.add, OP.add),
                                 waits=[mm_done] + pevs, inc="V")
                    psm_free[pi] = rev
                    resid_evs.append(rev)
                    if it2 == 1:
                        ffn_ck[ec] = rev
                        if hf == 1 and l + 1 < B1_DEPTH:
                            xb_evs.append(P.step("scalar",
                                                 lambda eng, ec=ec: eng.activation(
                                                     cx.xb[:, ec * CW + 1: ec * CW + 1 + N],
                                                     cx.xT[:, ec * CW + 1: ec * CW + 1 + N],
                                                     AF.Copy),
                                                 waits=[rev], inc="S"))
            cx.b1_w2c = w2_consumed
            cx.b1_w2c_alias = w2_consumed[0]
            cx.b1_w2c_alias2 = w2_consumed[1]
            cx.b1_w1c = w1_consumed
        cx.b1_mst_war = ev
        ffn_done = resid_evs[-8:]
        if l + 1 < B1_DEPTH:
            xb_ready = xb_evs[-1]
        prep_done_prev = None

    if B1_SKIP_LN2:
        return resid_evs[-8:]
    out_evs = emit_ln(cx, "postx", src_ready=resid_evs[-8:], resid_dram=resid_dram,
                      tag="b1ln2", scratch_free=resid_evs[-1], src_chunks=ffn_ck)
    return out_evs


def host_b1_inputs(inp):
    f = np.float32
    d = {}
    d["b1_Wqkv"] = np.ascontiguousarray(np.asarray(inp["L0_ffn_Wqkv"], f))
    d["b1_Wo"] = np.ascontiguousarray(np.asarray(inp["L0_ffn_Wo"], f))
    W1 = np.asarray(inp["L0_ffn_W1"], f)
    W2 = np.asarray(inp["L0_ffn_W2"], f)
    DI, DI2 = 2730, 2816
    W1p = np.zeros((6, 1024, 2 * DI2), f)
    W1p[:, :, :DI] = W1[:, :, :DI]
    W1p[:, :, DI2:DI2 + DI] = W1[:, :, DI:]
    W2p = np.zeros((6, DI2, 1024), f)
    W2p[:, :DI, :] = W2
    d["b1_W1p"] = W1p
    d["b1_W2p"] = W2p
    qs = np.asarray(inp["L0_ffn_q_scale"], f)
    ks = np.asarray(inp["L0_ffn_k_scale"], f)
    d["b1_qsc"] = np.ascontiguousarray(np.tile(qs, (1, 2)).T)
    d["b1_ksc"] = np.ascontiguousarray(np.tile(ks, (1, 2)).T)
    d["b1_lng"] = np.ascontiguousarray(np.asarray(inp["L0_ffn_ln_g"], f).reshape(6, 8, 128)
                                       .transpose(2, 0, 1).reshape(128, 48))
    d["b1_lnb"] = np.ascontiguousarray(np.asarray(inp["L0_ffn_ln_b"], f).reshape(6, 8, 128)
                                       .transpose(2, 0, 1).reshape(128, 48))
    b1 = np.asarray(inp["L0_ffn_b1"], f)
    b1ap = np.zeros((6, DI2), f); b1ap[:, :DI] = b1[:, :DI]
    b1gp = np.zeros((6, DI2), f); b1gp[:, :DI] = b1[:, DI:]
    d["b1_b1as"] = np.ascontiguousarray(b1ap.reshape(6, 22, 128).transpose(2, 0, 1).reshape(128, 132))
    d["b1_b1gs"] = np.ascontiguousarray(b1gp.reshape(6, 22, 128).transpose(2, 0, 1).reshape(128, 132))
    d["b1_b2s"] = np.ascontiguousarray(np.asarray(inp["L0_ffn_b2"], f).reshape(6, 8, 128)
                                       .transpose(2, 0, 1).reshape(128, 48))
    return d


def build_b1_program():
    nc = bass.Bass("TRN2", target_bir_lowering=False, debug=False)
    cx = build_base(nc)
    cx.b1_eb_free = [None, None]
    emit_init(cx)
    emit_load_x(cx)
    out_evs = emit_b1(cx, resid_dram=cx.xin)
    emit_store_x(cx, out_evs)
    with nc.Block() as block:
        cx.P.emit(block)
    return nc, cx



def kernel(**inputs):
    x = np.ascontiguousarray(np.asarray(inputs["L0_attn_x"], np.float32))
    nc, cx = build_fused_program()
    wd = host_fused_inputs(inputs, cx)
    in_maps = []
    for c in range(8):
        m = dict(wd)
        m["xT_in"] = np.ascontiguousarray(x[c].T)
        in_maps.append(m)
    res = run_bass_kernel_spmd(nc, in_maps, core_ids=list(range(8)))
    return np.stack([np.asarray(r["xT_out"]).T for r in res.results]).astype(np.float32)


# version 07 (final)
# speedup vs baseline: 1.0585x; 1.0070x over previous
import numpy as np
import concourse.bass as bass
import concourse.mybir as mybir
from concourse.bass_utils import run_bass_kernel_spmd

AF = mybir.ActivationFunctionType
OP = mybir.AluOpType
AX = mybir.AxisListType
F32 = mybir.dt.float32
BF16 = mybir.dt.bfloat16

N = 2048
D = 1024
NCH = 8
CW = N + 1
H = 16
DH = 64
NSC = 16
NIT = 4

ENGS = ("tensor", "vector", "scalar", "sync", "gpsimd")


class Prog:

    NLANE = 4

    def __init__(self, nc):
        self.nc = nc
        self.steps = {e: [] for e in ENGS}
        self.sems = {}
        self.cnt = {}
        self._sem_ctx = []
        self.lane_rr = {}

    def sem(self, name):
        if name not in self.sems:
            cm = self.nc.semaphore(name)
            h = cm.__enter__()
            self._sem_ctx.append(cm)
            self.sems[name] = h
            self.cnt[name] = 0
        return name

    ENG_SEM = {"vector": "V", "scalar": "S", "gpsimd": "GC"}

    def step(self, eng, fn, waits=(), inc=None, amt=1):
        w = tuple(ev for ev in waits if ev is not None)
        ev = None
        if inc is not None and amt == 16:
            li = self.lane_rr.get(eng, 0)
            self.lane_rr[eng] = (li + 1) % self.NLANE
            lane = f"{inc}{li}"
            self.sem(lane)
            if self.cnt[lane] > 0:
                w = w + ((lane, self.cnt[lane]),)
            self.cnt[lane] += 16
            ev = (lane, self.cnt[lane])
            self.steps[eng].append((fn, w, (lane, 16)))
            return ev
        es = self.ENG_SEM.get(eng)
        if es is not None:
            inc = es
            amt = 1
        if inc is not None:
            self.sem(inc)
            if es is not None and self.cnt[inc] > 0:
                w = w + ((inc, self.cnt[inc]),)
            self.cnt[inc] += amt
            ev = (inc, self.cnt[inc])
        self.steps[eng].append((fn, w, (inc, amt) if inc else None))
        return ev

    def emit(self, block):
        for e in ENGS:
            if self.steps[e]:
                self._emit_engine(block, e)

    def _emit_engine(self, block, e):
        prog = self

        def runner(eng):
            seen = {}
            for fn, waits, inc in prog.steps[e]:
                for (s, v) in waits:
                    if seen.get(s, 0) < v:
                        eng.wait_ge(prog.sems[s], v)
                        seen[s] = v
                inst = fn(eng)
                if inc is not None:
                    if isinstance(inst, (list, tuple)):
                        inst = inst[-1]
                    inst.then_inc(prog.sems[inc[0]], inc[1])

        getattr(block, e)(runner)


class Arena:
    def __init__(self, nc):
        self.nc = nc
        self.top = (nc.sbuf_base + 63) // 64 * 64
        self.limit = nc.sbuf_top
        self.tensors = {}

    @staticmethod
    def _size(shape, dtype):
        es = 2 if dtype == BF16 else 4
        n = 1
        for s in shape[1:]:
            n *= s
        return n * es

    def alloc(self, name, shape, dtype, align=64):
        off = (self.top + align - 1) // align * align
        t = self.alloc_at(name, shape, dtype, off)
        self.top = off + self._size(shape, dtype)
        return t

    def alloc_at(self, name, shape, dtype, off):
        t = self.nc.alloc_sbuf_tensor_at(name, shape, dtype, offset=off)
        end = off + self._size(shape, dtype)
        assert end <= self.limit, f"SBUF overflow at {name}: {end}"
        self.tensors[name] = (t, off)
        return t

    def off(self, name):
        return self.tensors[name][1]


def host_consts():
    import ml_dtypes
    bf = ml_dtypes.bfloat16
    c = {}
    a = np.zeros((128, 128), np.float32); a[:, :64] = 1
    c["statA"] = a.astype(bf)
    a = np.zeros((128, 128), np.float32); a[:, 64:] = 1
    c["statB"] = a.astype(bf)
    c["rep64f"] = np.full((128, 128), 1.0 / 64, np.float32)
    c["ones1r"] = np.ones((1, 512), np.float32).astype(bf)
    a = np.zeros((64, 2), np.float32); a[:, 0] = 1
    c["c64a"] = a.astype(bf)
    a = np.zeros((64, 2), np.float32); a[:, 1] = 1
    c["c64b"] = a.astype(bf)
    a = np.zeros((1, 2), np.float32); a[0, 0] = 1
    c["c1a"] = a.astype(bf)
    a = np.zeros((1, 2), np.float32); a[0, 1] = 1
    c["c1b"] = a.astype(bf)
    c["ones64"] = np.ones((128, 64), np.float32).astype(bf)
    c["sc_k"] = np.array([[-1.0 / 64], [1.0]], np.float32)
    c["sc_q"] = np.array([[1.0], [-1.0 / 64]], np.float32)
    imm = np.array([1e-5, -0.5, 1.0 / 1024, 1.0 / 32, 32.0, 8.0, -1.0, 0.0],
                   np.float32)
    c["immc"] = np.tile(imm[None, :], (128, 1))
    rot = np.zeros((128, 128), np.float32)
    for b in (0, 64):
        for i in range(32):
            rot[b + 32 + i, b + i] = -1.0
            rot[b + i, b + 32 + i] = 1.0
    c["rot"] = rot.astype(bf)
    return c


def host_phase_consts():
    import ml_dtypes
    bf = ml_dtypes.bfloat16
    c = {}
    hr = np.zeros((16, 16 * 64), np.float32)
    for h in range(16):
        hr[h, h * 64:(h + 1) * 64] = 1
    c["hrep"] = hr.astype(bf)
    lm = np.zeros((128, 4 * 512), np.float32)
    for v in range(4):
        for p in range(128):
            lm[p, v * 512 + 128 * v + p:(v + 1) * 512] = 1
    c["l1m"] = lm.astype(bf)
    hs = np.zeros((128, 8 * 16), np.float32)
    ht = np.zeros((16, 8 * 128), np.float32)
    for cc in range(8):
        for p in range(128):
            h = 2 * cc + (1 if p >= 64 else 0)
            hs[p, cc * 16 + h] = 1
            ht[h, cc * 128 + p] = 1
    c["hsel"] = hs.astype(bf)
    c["hselT"] = ht.astype(bf)
    lm0 = np.zeros((128, 4 * 256), np.float32)
    for jc in range(4):
        for p in range(128):
            jl = jc * 128 + p
            lo, hi = max(0, jl - 256), min(255, jl)
            if lo <= hi:
                lm0[p, jc * 256 + lo: jc * 256 + hi + 1] = 1
    c["l0m"] = lm0.astype(bf)
    dh, wnd, nn = 64, 256, 2048
    half = np.arange(0, dh, 2, dtype=np.float64)
    inv_freq = 1.0 / 10000.0 ** (half / dh)
    t = np.arange(nn, dtype=np.float64)
    fr = t[:, None] * inv_freq[None, :]
    base = (half + 0.4 * dh) / (1.4 * dh)
    power = (t - nn // 2) / (wnd // 2)
    sc = base[None, :] ** power[:, None]
    cosf = np.cos(fr); sinf = np.sin(fr)
    qc = np.concatenate([cosf * sc, cosf * sc], 1).T * 8
    kc_ = np.concatenate([cosf / sc, cosf / sc], 1).T
    qsn = np.concatenate([-sinf * sc, sinf * sc], 1).T * 8
    ksn = np.concatenate([-sinf / sc, sinf / sc], 1).T
    qsp = np.concatenate([sinf * sc, sinf * sc], 1).T * 8
    ksp = np.concatenate([sinf / sc, sinf / sc], 1).T
    c["rtAq"] = np.ascontiguousarray(np.vstack([qc, qc]).astype(np.float32)).astype(bf)
    c["rtAk"] = np.ascontiguousarray(np.vstack([kc_, kc_]).astype(np.float32)).astype(bf)
    c["rtBq"] = np.ascontiguousarray(np.vstack([qsp, qsp]).astype(np.float32)).astype(bf)
    c["rtBk"] = np.ascontiguousarray(np.vstack([ksp, ksp]).astype(np.float32)).astype(bf)
    return c


class Ctx:
    pass


def build_base(nc):
    P = Prog(nc)
    ar = Arena(nc)
    cx = Ctx()
    cx.nc, cx.P, cx.ar = nc, P, ar
    cx.pbig = [nc.alloc_psum_tensor(f"pbig{i}", [128, 1024], F32) for i in range(2)]
    cx.psm = [nc.alloc_psum_tensor(f"psm{i}", [128, 512], F32) for i in range(4)]

    cx.xT = ar.alloc("xT", [128, NCH * CW], F32)
    cx.xb = ar.alloc("xb", [128, NCH * CW], BF16)

    xoff = ar.off("xT")
    cx.subout = ar.alloc_at("subout", [128, NCH * N], BF16, xoff)
    cx.OT = ar.alloc_at("OT", [128, NCH * N], BF16, xoff + NCH * N * 2)

    hc = host_consts()
    cx.C = {}
    cx.const_evs = []
    for name, arr in hc.items():
        dt = F32 if arr.dtype == np.float32 else BF16
        dr = nc.dram_tensor("c_" + name, list(arr.shape), dt, kind="ExternalInput").ap()
        sb = ar.alloc("s_" + name, list(arr.shape), dt)
        ev = P.step("sync", lambda eng, dr=dr, sb=sb: eng.dma_start(out=sb[:, :], in_=dr[:, :]),
                    inc="DS", amt=16)
        cx.C[name] = sb
        cx.const_evs.append(ev)
    cx.host_const_arrays = hc

    cx.ln_xs = [ar.alloc(f"ln_xs{i}", [128, N], BF16) for i in range(2)]
    cx.ln_sq = [ar.alloc(f"ln_sq{i}", [128, N], BF16) for i in range(2)]
    f12 = ar.alloc("ln_f12", [128, 512], F32)
    cx.ln_f1 = f12
    cx.ln_f2 = f12
    cx.f1ap = f12[0:64, :]
    cx.f2ap = f12[64:128, :]

    cx.vR = ar.alloc("vR", [128, NSC * 128], BF16)
    cx.Vc = ar.alloc("Vc", [128, NSC * 128], BF16)
    cx.ln_tmp = [ar.alloc_at(f"ln_tmp{i}", [128, 512], F32, ar.off("Vc") + i * 2048) for i in range(2)]
    cx.xo_t = [ar.alloc_at(f"xo_t{i}", [128, 512], F32, ar.off("vR") + i * 2048) for i in range(2)]
    cx.trec = ar.alloc_at("trec", [64, 512], F32, ar.off("ln_f12"))
    wso = ar.limit - 6144
    cx.ws = {m: ar.alloc_at(f"ws_{m}", [128, 8 * 128], BF16, wso + i * 2048)
             for i, m in enumerate(("q", "k", "v"))}
    cx.wos = [ar.alloc_at("wos0", [128, 8 * 128], BF16, wso),
              ar.alloc_at("wos1", [128, 8 * 128], BF16, wso + 2048)]
    cx.Ebuf = [ar.alloc(f"Ebuf{i}", [128, 1024], BF16) for i in range(2)]
    cx.wev = {m: None for m in ("q", "k", "v")}
    cx.strip_consumed = {m: None for m in ("q", "k", "v")}
    cx.psm_free = [None] * 4
    cx.pb_free = [None, None]

    cx.b0_bq32 = ar.alloc("b0_bq32", [128, NCH], F32)
    cx.b0_bkc = ar.alloc("b0_bkc", [128, NCH], F32)
    cx.b0_boc = ar.alloc("b0_boc", [128, NCH], F32)
    cx.b0_bvc = ar.alloc("b0_bvc", [128, NCH], F32)
    cx.b0_boc2 = ar.alloc("b0_boc2", [128, NCH], F32)
    cx.b0_ncacc = ar.alloc("b0_ncacc", [128, 32], F32)
    cx.b0_nc16 = ar.alloc("b0_nc16", [128, 16], F32)
    cx.b0_cln16 = ar.alloc("b0_cln16", [128, 16], F32)
    cx.b0_c16 = ar.alloc("b0_c16", [128, 16], F32)
    cx.b1_stash = {}
    for nm, wdt in (("qsc", 6), ("ksc", 6), ("lng", 48), ("lnb", 48),
                    ("b1a", 132), ("b1g", 132), ("b2s", 48)):
        cx.b1_stash[nm] = ar.alloc("b1s_" + nm, [128, wdt], F32)
    cx.b3_bb = ar.alloc("b3_bc", [128, 4], F32)
    cx.b3_bb2 = ar.alloc("b3_bc2", [128, 4], F32)
    cx.ln_gbt = ar.alloc_at("ln_gbt", [128, 512], F32, ar.off("ln_sq1"))
    cx.b1_t0 = ar.alloc_at("b1_t0", [128, 512], BF16, ar.off("ln_xs1"))
    cx.b1_t1 = ar.alloc_at("b1_t1", [128, 512], BF16, ar.off("ln_xs1") + 1024)
    cx.b1_t2 = ar.alloc_at("b1_t2", [128, 512], BF16, ar.off("ln_xs1") + 2048)

    pc = host_phase_consts()
    cx.host_pc_arrays = pc
    cx.PCdr = {}
    for name, arr in pc.items():
        cx.PCdr[name] = nc.dram_tensor("pc_" + name, list(arr.shape), BF16,
                                       kind="ExternalInput").ap()
    cx.R_OFF = (ar.top + 63) // 64 * 64
    cx.R_END = ar.limit - 6144
    cx.sqb = ar.alloc_at("sqb", [64, N], BF16, cx.R_OFF + 65536)
    cx.b1_sqq = ar.alloc_at("b1_sqq", [128, 512], BF16, cx.R_OFF + 66048)
    cx.b1_q2c = ar.alloc_at("b1_q2c", [16, 512], F32, cx.R_OFF + 67072)
    cx.b1_rfq = ar.alloc_at("b1_rfq", [16, 512], BF16, cx.R_OFF + 69120)
    cx.b1_rfq2 = ar.alloc_at("b1_rfq2", [16, 512], BF16, cx.R_OFF + 70144)
    cx.pc_free_ev = None
    cx.bs_free_ev = None
    return cx


def load_pc(cx, names, waits=()):
    off = cx.R_OFF
    out = {}
    evs = []
    for name in names:
        arr = cx.host_pc_arrays[name]
        sb = cx.ar.alloc_at(f"pc_{name}_{len(cx.ar.tensors)}", list(arr.shape), BF16, off)
        off += Arena._size(list(arr.shape), BF16)
        off = (off + 63) // 64 * 64
        ev = cx.P.step("sync",
                       lambda eng, sb=sb, dr=cx.PCdr[name]: eng.dma_start(out=sb[:, :], in_=dr[:, :]),
                       waits=list(waits), inc="DS", amt=16)
        out[name] = sb
        evs.append(ev)
    return out, evs


def strip_src(w, c):
    return w.rearrange("(kc p) (mc m) -> p kc mc m", p=128, m=128)[:, :, c, :]


def load_strip(cx, m, wdr, c):
    dst = cx.ws[m][:, :].rearrange("p (kc q) -> p kc q", q=128)
    ev = cx.P.step("gpsimd",
                   lambda eng, t=dst, s=strip_src(wdr, c): eng.dma_start(out=t, in_=s),
                   waits=[cx.strip_consumed[m]], inc="DG", amt=16)
    cx.wev[m] = ev
    return ev


def emit_outproj(cx, wo, bias_col, OT_last_ev, strip_war, pevs, bv_col=None,
                 bias_out=None):
    P, C = cx.P, cx.C
    wo_evs = {}
    wo_consumed = [None, None]

    def load_wo_strip(ec):
        dst = cx.wos[ec % 2][:, :].rearrange("p (kc q) -> p kc q", q=128)
        war = [wo_consumed[ec % 2]]
        if ec < 2:
            war += strip_war
        wo_evs[ec] = P.step("gpsimd",
                            lambda eng, d=dst, ec=ec: eng.dma_start(out=d, in_=strip_src(wo, ec)),
                            waits=war, inc="DG", amt=16)

    load_wo_strip(0)
    load_wo_strip(1)
    sub_evs = []
    bo_evs = {}
    for ec in range(8):
        if bv_col is not None:
            for dc in range(NCH):
                bmm = P.step("tensor",
                             lambda eng, ec=ec, dc=dc: eng.matmul(
                                 cx.pbig[0][:, 0:1],
                                 cx.wos[ec % 2][:, dc * 128:(dc + 1) * 128],
                                 bv_col[:, dc:dc + 1],
                                 start=(dc == 0), stop=(dc == NCH - 1)),
                             waits=[wo_evs[ec], cx.pb_free[0]] + pevs if dc == 0 else (),
                             inc="T" if dc == NCH - 1 else None)
            bo_evs[ec] = P.step("vector",
                                lambda eng, ec=ec: eng.tensor_tensor(
                                    bias_out[:, ec:ec + 1], bias_col[:, ec:ec + 1],
                                    cx.pbig[0][:, 0:1], OP.add),
                                waits=[bmm], inc="V")
            cx.pb_free[0] = bo_evs[ec]
        for it in range(NIT):
            w0 = [wo_evs[ec], cx.psm_free[it]] + ([OT_last_ev] if ec == 0 and it == 0 else [])
            for dc in range(NCH):
                ev = P.step("tensor",
                            lambda eng, ec=ec, it=it, dc=dc: eng.matmul(
                                cx.psm[it][:, :],
                                cx.wos[ec % 2][:, dc * 128:(dc + 1) * 128],
                                cx.OT[:, dc * N + it * 512: dc * N + (it + 1) * 512],
                                start=(dc == 0), stop=(dc == NCH - 1)),
                            waits=w0 if dc == 0 else (),
                            inc="T" if dc == NCH - 1 else None)
            mm_done = ev
            if bv_col is not None:
                sev = P.step("scalar",
                             lambda eng, ec=ec, it=it: eng.activation(
                                 cx.subout[:, ec * N + it * 512: ec * N + (it + 1) * 512],
                                 cx.psm[it][:, :], AF.Identity, bias=bias_out[:, ec:ec + 1]),
                             waits=[mm_done, bo_evs[ec]] + pevs, inc="S")
            elif bias_col is not None:
                sev = P.step("scalar",
                             lambda eng, ec=ec, it=it: eng.activation(
                                 cx.subout[:, ec * N + it * 512: ec * N + (it + 1) * 512],
                                 cx.psm[it][:, :], AF.Identity, bias=bias_col[:, ec:ec + 1]),
                             waits=[mm_done] + pevs, inc="S")
            else:
                sev = P.step("scalar",
                             lambda eng, ec=ec, it=it: eng.activation(
                                 cx.subout[:, ec * N + it * 512: ec * N + (it + 1) * 512],
                                 cx.psm[it][:, :], AF.Copy),
                             waits=[mm_done] + pevs, inc="S")
            cx.psm_free[it] = sev
            sub_evs.append(sev)
        wo_consumed[ec % 2] = mm_done
        if ec + 2 < 8:
            load_wo_strip(ec + 2)
    cx.strip_consumed["q"] = wo_consumed[0]
    cx.strip_consumed["k"] = wo_consumed[1]
    return sub_evs


def emit_init(cx):
    P = cx.P
    xTz = cx.xT[:, :].rearrange("p (c w) -> p c w", w=CW)[:, :, 0:1]
    xbz = cx.xb[:, :].rearrange("p (c w) -> p c w", w=CW)[:, :, 0:1]
    e1 = P.step("vector", lambda eng: eng.memset(xTz, 0.0), inc="V")
    e2 = P.step("vector", lambda eng: eng.memset(xbz, 0.0), inc="V")
    cx.init_evs = [e1, e2] + cx.const_evs


def emit_load_x(cx):
    P, nc = cx.P, cx.nc
    xin = nc.dram_tensor("xT_in", [D, N], F32, kind="ExternalInput").ap()
    cx.xin = xin
    src = xin.rearrange("(c p) n -> p c n", p=128)
    dst = cx.xT[:, :].rearrange("p (c w) -> p c w", w=CW)[:, :, 1:]
    cx.x_ready = P.step("sync", lambda eng: eng.dma_start(out=dst, in_=src),
                        waits=cx.init_evs, inc="DS", amt=16)


def emit_spill_x(cx, waits):
    P, nc = cx.P, cx.nc
    if not hasattr(cx, "xspill"):
        cx.xspill = nc.dram_tensor("xspill", [D, N], F32, kind="Internal").ap()
    dst = cx.xspill.rearrange("(c p) n -> p c n", p=128)
    srcv = cx.xT[:, :].rearrange("p (c w) -> p c w", w=CW)[:, :, 1:]
    ev = P.step("sync", lambda eng: eng.dma_start(out=dst, in_=srcv),
                waits=list(waits), inc="DS", amt=16)
    cx.xgate = ev
    return cx.xspill, ev


def emit_store_x(cx, wait_evs):
    P, nc = cx.P, cx.nc
    xout = nc.dram_tensor("xT_out", [D, N], F32, kind="ExternalOutput").ap()
    dst = xout.rearrange("(c p) n -> p c n", p=128)
    src = cx.xT[:, :].rearrange("p (c w) -> p c w", w=CW)[:, :, 1:]
    ev = P.step("sync", lambda eng: eng.dma_start(out=dst, in_=src),
                waits=wait_evs, inc="DS", amt=16)
    P.step("sync", lambda eng: eng.nop(), waits=[ev])
    return ev


def emit_dbg(cx, sb_ap, shape, dtype, waits):
    P, nc = cx.P, cx.nc
    dr = nc.dram_tensor("dbg", list(shape), dtype, kind="ExternalOutput").ap()
    ev = P.step("sync", lambda eng: eng.dma_start(out=dr[:, :], in_=sb_ap), waits=waits,
                inc="DS", amt=16)
    P.step("sync", lambda eng: eng.nop(), waits=[ev])
    return ev


def emit_ln(cx, kind, src_ready, shifted=False, gain_bias=None,
            resid_dram=None, out_xb=False, out_xT=False, sub_nt=N, tag="",
            scratch_free=None, src_chunks=None):
    P, C = cx.P, cx.C
    pstat = cx.psm[:4] if kind == "_unused" else None
    pstat = cx.psm
    prepI, prepM = cx.pbig[0], cx.pbig[1]

    def src_f32(c, sh):
        base = c * CW + 1 - (1 if sh else 0)
        return cx.xT[:, base: base + N]

    def src_bf(c, sh):
        assert not sh
        return cx.subout[:, c * N: c * N + N]

    mm_evs = [None] * NIT
    last_mm_per_buf = [None, None]
    xs_ev = [None, None]
    sq_ev = [None, None]
    stat_done = [None] * NIT
    for c in range(NCH):
        sh = shifted and c >= 4
        b = 0
        if kind in ("pre", "postx"):
            xsb = cx.ln_xs[b]
            xs_ev[b] = P.step("scalar",
                              lambda eng, o=xsb, i=src_f32(c, sh): eng.activation(o[:, :], i, AF.Copy),
                              waits=[cx.x_ready, last_mm_per_buf[b]]
                              + ([src_chunks[c]] if src_chunks else list(src_ready)), inc="S")
            xsrc = xsb
            xsl = lambda it, t=xsb: t[:, it * 512:(it + 1) * 512]
            sqi = src_f32(c, sh)
        else:
            xsl = lambda it, c=c, sh=sh: src_bf(c, sh)[:, it * 512:(it + 1) * 512]
            sqi = src_bf(c, sh)
        sqb = cx.ln_sq[b]
        sq_ev[b] = P.step("scalar",
                          lambda eng, o=sqb, i=sqi: eng.activation(o[:, :], i, AF.Square),
                          waits=[last_mm_per_buf[b]]
                          + ([src_chunks[c]] if src_chunks else list(src_ready)), inc="S")
        for it in range(NIT):
            w = [xs_ev[b], sq_ev[b]]
            if c == 0:
                w += [cx.psm_free[it]]
            P.step("tensor",
                   lambda eng, ps=pstat[it], x=xsl(it), c=c: eng.matmul(
                       ps[:, :], C["statA"][:, :], x, start=(c == 0), stop=False),
                   waits=w)
            ev = P.step("tensor",
                        lambda eng, ps=pstat[it], s=sqb, it=it, c=c: eng.matmul(
                            ps[:, :], C["statB"][:, :], s[:, it * 512:(it + 1) * 512],
                            start=False, stop=(c == NCH - 1)),
                        inc="T")
            mm_evs[it] = ev
        last_mm_per_buf[b] = mm_evs[NIT - 1]
    for it in range(NIT):
        stat_done[it] = mm_evs[it]

    out_evs = []
    eI = [None] * NIT
    eM = [None] * NIT
    prev_eI = None
    prev_eM = None
    for it in range(NIT):
        pst = pstat[it]
        e1 = P.step("scalar",
                    lambda eng, pst=pst: eng.activation(cx.f1ap, pst[0:64, :],
                                                        AF.Square, scale=C["immc"][0:64, 2:3]),
                    waits=[stat_done[it], prev_eM], inc="S")
        e2 = P.step("vector",
                    lambda eng, pst=pst: eng.scalar_tensor_tensor(
                        cx.f2ap, pst[64:128, :], 1.0 / D, cx.f1ap,
                        OP.mult, OP.subtract),
                    waits=[e1, prev_eI], inc="V")
        e3 = P.step("scalar",
                    lambda eng: eng.activation(cx.f1ap, cx.f2ap,
                                               AF.Ln, bias=C["immc"][0:64, 0:1]),
                    waits=[e2], inc="S")
        e4 = P.step("scalar",
                    lambda eng: eng.activation(cx.f2ap, cx.f1ap,
                                               AF.Exp, scale=C["immc"][0:64, 1:2]),
                    inc="S")
        e5 = P.step("vector",
                    lambda eng, pst=pst: eng.scalar_tensor_tensor(
                        cx.f1ap, pst[0:64, :], 1.0 / D, cx.f2ap,
                        OP.mult, OP.mult),
                    waits=[e4], inc="V")
        rI = cx.pbig[it // 2][:, (it % 2) * 512:(it % 2) * 512 + 512]
        eI[it] = P.step("tensor",
                        lambda eng, rI=rI: eng.matmul(rI, C["rep64f"][64:128, :],
                                                      cx.f2ap, start=True, stop=True),
                        waits=[e4], inc="T")
        eM[it] = P.step("tensor",
                        lambda eng, pst=pst: eng.matmul(pst[:, :], C["rep64f"][0:64, :],
                                                        cx.f1ap, start=True, stop=True),
                        waits=[e5, e2, e1], inc="T")
        prev_eI, prev_eM = eI[it], eM[it]

    c_order = list(range(NCH))
    it_order = list(range(NIT))
    if kind == "post":
        c_order = list(reversed(c_order))
        it_order = list(reversed(it_order))
    tmp_ev = [None, None]
    rM_last = dict(zip(range(NIT), cx.psm_free))
    rI_last = dict(zip(range(2), cx.pb_free))
    cix = 0
    for c in c_order:
        for it in it_order:
            sh = shifted and c >= 4
            tb = cx.ln_tmp[cix % 2]
            rI = cx.pbig[it // 2][:, (it % 2) * 512:(it % 2) * 512 + 512]
            rM = pstat[it]
            if kind == "pre":
                x_sl = src_f32(c, sh)[:, it * 512:(it + 1) * 512]
                a1 = P.step("vector",
                            lambda eng, tb=tb, x=x_sl, rI=rI: eng.tensor_tensor(
                                tb[:, :], x, rI, OP.mult),
                            waits=[eI[it], tmp_ev[cix % 2]], inc="V")
                dst = cx.xb[:, c * CW + 1 + it * 512: c * CW + 1 + (it + 1) * 512]
                d2 = cx.ln_gbt[:, :] if gain_bias is not None else dst
                a2 = P.step("vector",
                            lambda eng, tb=tb, d=d2, rM=rM: eng.tensor_tensor(
                                d, tb[:, :], rM[:, :], OP.subtract),
                            waits=[eM[it]] + ([out_evs[-1]] if gain_bias is not None and out_evs else []),
                            inc="V")
                tmp_ev[cix % 2] = a2
                out_evs.append(a2)
                rM_last[it] = a2
                rI_last[it // 2] = a1
                if gain_bias is not None:
                    g_sb, b_sb = gain_bias
                    a3 = P.step("scalar",
                                lambda eng, d=dst, c=c: eng.activation(
                                    d, cx.ln_gbt[:, :], AF.Identity, bias=b_sb[:, c:c + 1],
                                    scale=g_sb[:, c:c + 1]),
                                waits=[a2], inc="S")
                    out_evs.append(a3)
                if out_xT:
                    xd = src_f32(c, False)[:, it * 512:(it + 1) * 512]
                    a4 = P.step("vector",
                                lambda eng, tb=tb, xd=xd, rM=rM: eng.tensor_tensor(
                                    xd, tb[:, :], rM[:, :], OP.subtract),
                                waits=[eM[it], getattr(cx, "xgate", None)], inc="V")
                    tmp_ev[cix % 2] = a4
                    out_evs.append(a4)
                    rM_last[it] = a4
            else:
                if kind == "postx":
                    s_sl = src_f32(c, False)[:, it * 512:(it + 1) * 512]
                else:
                    s_sl = src_bf(c, False)[:, it * 512:(it + 1) * 512]
                a1 = P.step("vector",
                            lambda eng, tb=tb, s=s_sl, rI=rI: eng.tensor_tensor(
                                tb[:, :], s, rI, OP.mult),
                            waits=[eI[it], tmp_ev[cix % 2]], inc="V")
                a2 = P.step("vector",
                            lambda eng, tb=tb, rM=rM: eng.tensor_tensor(
                                tb[:, :], tb[:, :], rM[:, :], OP.subtract),
                            waits=[eM[it]], inc="V")
                xo = cx.xo_t[cix % 2]
                src = resid_dram.rearrange("(c p) n -> p c n", p=128)[:, c, it * 512:(it + 1) * 512]
                d1 = P.step("sync",
                            lambda eng, xo=xo, src=src: eng.dma_start(out=xo[:, :], in_=src),
                            waits=[tmp_ev[cix % 2] or scratch_free], inc="DS", amt=16)
                dst = src_f32(c, False)[:, it * 512:(it + 1) * 512]
                a3 = P.step("vector",
                            lambda eng, tb=tb, xo=xo, d=dst: eng.tensor_tensor(
                                d, tb[:, :], xo[:, :], OP.add),
                            waits=[d1], inc="V")
                tmp_ev[cix % 2] = a3
                out_evs.append(a3)
                rM_last[it] = a2
                rI_last[it // 2] = a1
            cix += 1
    for it in range(NIT):
        cx.psm_free[it] = rM_last[it]
    for ih in range(2):
        cx.pb_free[ih] = rI_last[ih]
    return out_evs


def emit_b0(cx, weights_ready, resid_dram):
    P, nc, C, ar = cx.P, cx.nc, cx.C, cx.ar

    wq = nc.dram_tensor("b0_wqT", [D, D], F32, kind="ExternalInput").ap()
    wk = nc.dram_tensor("b0_wkT", [D, D], F32, kind="ExternalInput").ap()
    wv = nc.dram_tensor("b0_wvT", [D, D], F32, kind="ExternalInput").ap()
    wo = nc.dram_tensor("b0_woT", [D, D], F32, kind="ExternalInput").ap()
    bqs = nc.dram_tensor("b0_bqs", [128, NCH], F32, kind="ExternalInput").ap()
    bks = nc.dram_tensor("b0_bks", [128, NCH], F32, kind="ExternalInput").ap()
    bos = nc.dram_tensor("b0_bos", [128, NCH], F32, kind="ExternalInput").ap()
    bvs = nc.dram_tensor("b0_bvs", [128, NCH], F32, kind="ExternalInput").ap()

    qaug = cx.ln_xs
    kaug = cx.ln_sq
    vR = cx.vR
    Vc = cx.Vc
    sqb = cx.sqb
    trec = cx.trec
    ws = cx.ws
    ST = ar.alloc_at("b0_ST", [128, NSC * N], BF16, cx.R_OFF)
    ncacc, nc16, cln16, c16 = cx.b0_ncacc, cx.b0_nc16, cx.b0_cln16, cx.b0_c16
    cx.b0_r1_prev = None
    bq32, bkc, boc = cx.b0_bq32, cx.b0_bkc, cx.b0_boc
    bvc, boc2 = cx.b0_bvc, cx.b0_boc2

    pevs = []
    pevs.append(P.step("sync", lambda eng: eng.dma_start(out=bq32[:, :], in_=bqs[:, :]), inc="DS", amt=16))
    pevs.append(P.step("sync", lambda eng: eng.dma_start(out=bkc[:, :], in_=bks[:, :]), inc="DS", amt=16))
    pevs.append(P.step("sync", lambda eng: eng.dma_start(out=boc[:, :], in_=bos[:, :]), inc="DS", amt=16))
    bvcb = ar.alloc_at("b0_bvcb", [128, NCH], BF16, ar.off("b0_bvc") + 32)
    pevs.append(P.step("gpsimd", lambda eng: eng.dma_start(out=bvcb[:, :], in_=bvs[:, :]), inc="DG", amt=16))
    bq_ev = P.step("vector", lambda eng: eng.tensor_scalar_mul(bq32[:, :], bq32[:, :], 1.0 / 32),
                   waits=pevs[:1], inc="V")
    pevs.append(bq_ev)

    ln1_evs = emit_ln(cx, "pre", src_ready=weights_ready, tag="b0ln1")

    ln1_done = [ln1_evs[-1], ln1_evs[-2]]
    xb_full = ln1_evs

    wev = cx.wev
    strip_consumed = cx.strip_consumed

    for m, wdr in (("q", wq), ("k", wk), ("v", wv)):
        load_strip(cx, m, wdr, 0)


    pb_free = cx.pb_free
    psm_free = cx.psm_free
    OT_evs = []
    proj_read_xb = []
    Vc_free = None
    sq_free = None
    q2_parity = 0

    ST_T_budget = None

    for c in range(NCH):
        b = 0
        evac_evs = []
        for (mat, augA, augB, bias, scale) in (
                ("q", qaug[0], qaug[1], bq32, "r32"),
                ("k", kaug[0], kaug[1], bkc, None)):
            for it in range(NIT):
                mm_w = [wev[mat], psm_free[it]] + (xb_full if c == 0 and it == 0 else [])
                for kc in range(NCH):
                    ev = P.step("tensor",
                                lambda eng, it=it, kc=kc, mat=mat, b=b, c=c: eng.matmul(
                                    cx.psm[it][:, :],
                                    ws[mat][:, kc * 128:(kc + 1) * 128],
                                    cx.xb[:, kc * CW + 1 + it * 512: kc * CW + 1 + (it + 1) * 512],
                                    start=(kc == 0), stop=(kc == NCH - 1)),
                                waits=mm_w if kc == 0 else (),
                                inc="T" if kc == NCH - 1 else None)
                    if kc == NCH - 1:
                        mm_done = ev
                proj_read_xb.append(mm_done)
                for par, aug in ((0, augA), (1, augB)):
                    eev = P.step("scalar",
                                 lambda eng, it=it, par=par, aug=aug, bias=bias, c=c, scale=scale: eng.activation(
                                     aug[0:64, it * 512:(it + 1) * 512],
                                     cx.psm[it][par * 64:(par + 1) * 64, :],
                                     AF.Identity, bias=bias[par * 64:(par + 1) * 64, c:c + 1],
                                     scale=(C["immc"][par * 64:(par + 1) * 64, 3:4]
                                            if scale == "r32" else 1.0)),
                                 waits=[mm_done] + pevs, inc="S")
                    evac_evs.append(eev)
                psm_free[it] = evac_evs[-1]
        strip_consumed["q"] = evac_evs[7]
        strip_consumed["k"] = evac_evs[-1]
        v_evs = []
        for g in range(4):
            it = g
            grp_w = [wev["v"], psm_free[it], pevs[3]]
            for sci in range(4):
                sc = g * 4 + sci
                for kc in range(NCH):
                    ev = P.step("tensor",
                                lambda eng, it=it, sci=sci, sc=sc, kc=kc, b=b: eng.matmul(
                                    cx.psm[it][:, sci * 128:(sci + 1) * 128],
                                    cx.xb[:, kc * CW + 1 + sc * 128: kc * CW + 1 + (sc + 1) * 128],
                                    ws["v"][:, kc * 128:(kc + 1) * 128],
                                    start=(kc == 0 and sci == 0),
                                    stop=(kc == NCH - 1 and sci == 3),
                                    skip_group_check=True),
                                waits=grp_w if (sci == 0 and kc == 0) else (),
                                inc="T" if (kc == NCH - 1 and sci == 3) else None)
            grp_done = ev
            for sci in range(4):
                sc = g * 4 + sci
                dst = vR[:, sc * 128:(sc + 1) * 128]
                vev = P.step("scalar",
                             lambda eng, it=it, sci=sci, d=dst: eng.activation(
                                 d, cx.psm[it][:, sci * 128:(sci + 1) * 128], AF.Copy),
                             waits=[grp_done], inc="S")
                v_evs.append(vev)
            psm_free[it] = v_evs[-1]
        strip_consumed["v"] = v_evs[-1]
        v_done = v_evs[-1]
        if c + 1 < NCH:
            for m, wdr in (("q", wq), ("k", wk), ("v", wv)):
                load_strip(cx, m, wdr, c + 1)

        for par in range(2):
            h = 2 * c + par
            qa, ka = qaug[par], kaug[par]
            sq1 = P.step("scalar",
                         lambda eng, qa=qa: eng.activation(sqb[:, :], qa[0:64, :], AF.Square, scale=C["immc"][0:64, 4:5]),
                         waits=[evac_evs[-1], sq_free], inc="S")
            q2evs = []
            for half in range(2):
                P.step("tensor",
                       lambda eng, half=half: eng.matmul(
                           cx.pbig[0][0:2, half * 512:(half + 1) * 512],
                           C["c1a"][:, :], C["ones1r"][0:1, 0:512],
                           start=True, stop=False),
                       waits=[sq1, pb_free[0]] if half == 0 else ())
                ev = P.step("tensor",
                            lambda eng, half=half: eng.matmul(
                                cx.pbig[0][0:2, half * 512:(half + 1) * 512],
                                C["c64b"][:, :], sqb[:, half * 512: (half + 1) * 512],
                                start=False, stop=True),
                            inc="T")
                q2evs.append(ev)
            q2evs2 = []
            for half in range(2):
                P.step("tensor",
                       lambda eng, half=half: eng.matmul(
                           cx.pbig[1][0:2, half * 512:(half + 1) * 512],
                           C["c1a"][:, :], C["ones1r"][0:1, 0:512],
                           start=True, stop=False),
                       waits=[sq1, pb_free[1]] if half == 0 else ())
                ev = P.step("tensor",
                            lambda eng, half=half: eng.matmul(
                                cx.pbig[1][0:2, half * 512:(half + 1) * 512],
                                C["c64b"][:, :], sqb[:, 1024 + half * 512: 1024 + (half + 1) * 512],
                                start=False, stop=True),
                            inc="T")
                q2evs2.append(ev)
            eq = P.step("scalar",
                        lambda eng, qa=qa: eng.activation(
                            qa[64:66, 0:1024], cx.pbig[0][0:2, :], AF.Copy,
                            scale=C["sc_q"][:, 0:1]),
                        waits=q2evs, inc="S")
            eq2 = P.step("scalar",
                         lambda eng, qa=qa: eng.activation(
                             qa[64:66, 1024:2048], cx.pbig[1][0:2, :], AF.Copy,
                             scale=C["sc_q"][:, 0:1]),
                         waits=q2evs2, inc="S")
            sq2 = P.step("scalar",
                         lambda eng, ka=ka: eng.activation(sqb[:, :], ka[0:64, :], AF.Square),
                         waits=[eq, eq2] + q2evs + q2evs2, inc="S")
            k2evs = []
            for (pb, base) in ((0, 0), (1, 1024)):
                for half in range(2):
                    P.step("tensor",
                           lambda eng, pb=pb, half=half: eng.matmul(
                               cx.pbig[pb][0:2, half * 512:(half + 1) * 512],
                               C["c1b"][:, :], C["ones1r"][0:1, 0:512],
                               start=True, stop=False),
                           waits=[sq2] if half == 0 else ())
                    ev = P.step("tensor",
                                lambda eng, pb=pb, base=base, half=half: eng.matmul(
                                    cx.pbig[pb][0:2, half * 512:(half + 1) * 512],
                                    C["c64a"][:, :], sqb[:, base + half * 512: base + (half + 1) * 512],
                                    start=False, stop=True),
                                inc="T")
                    k2evs.append(ev)
            ek = P.step("scalar",
                        lambda eng, ka=ka: eng.activation(
                            ka[64:66, 0:1024], cx.pbig[0][0:2, :], AF.Copy,
                            scale=C["sc_k"][:, 0:1]),
                        waits=k2evs[:2], inc="S")
            ek2 = P.step("scalar",
                         lambda eng, ka=ka: eng.activation(
                             ka[64:66, 1024:2048], cx.pbig[1][0:2, :], AF.Copy,
                             scale=C["sc_k"][:, 0:1]),
                         waits=k2evs[2:], inc="S")
            sq_free = ek2
            pb_free[0] = ek
            pb_free[1] = ek2

            exp_evs = []
            stq_waits = [eq, eq2, ek, ek2]
            for jc in range(NSC):
                for tp in range(2):
                    pbi = (jc * 2 + tp) % 2
                    w = [pb_free[pbi]] + (stq_waits if jc == 0 and tp == 0 else [])
                    P.step("tensor",
                           lambda eng, pbi=pbi, jc=jc, tp=tp, ka=ka, qa=qa: eng.matmul(
                               cx.pbig[pbi][:, 0:512],
                               ka[0:66, jc * 128:(jc + 1) * 128],
                               qa[0:66, tp * 1024: tp * 1024 + 512],
                               start=True, stop=True),
                           waits=w)
                    mev = P.step("tensor",
                                 lambda eng, pbi=pbi, jc=jc, tp=tp, ka=ka, qa=qa: eng.matmul(
                                     cx.pbig[pbi][:, 512:1024],
                                     ka[0:66, jc * 128:(jc + 1) * 128],
                                     qa[0:66, tp * 1024 + 512: tp * 1024 + 1024],
                                     start=True, stop=True),
                                 inc="T")
                    eev = P.step("scalar",
                                 lambda eng, pbi=pbi, jc=jc, tp=tp: eng.activation(
                                     ST[:, jc * N + tp * 1024: jc * N + (tp + 1) * 1024],
                                     cx.pbig[pbi][:, :], AF.Exp,
                                     accum_out=ncacc[:, jc * 2 + tp: jc * 2 + tp + 1]),
                                 waits=[mev] + ([cx.b0_r1_prev] if jc == 0 and tp == 0 else []),
                                 inc="S")
                    exp_evs.append(eev)
                    pb_free[pbi] = eev
            r1 = P.step("vector",
                        lambda eng: eng.tensor_reduce(
                            nc16[:, :], ncacc[:, :].rearrange("p (j t) -> p j t", t=2),
                            AX.X, OP.add),
                        waits=[exp_evs[-1], exp_evs[-2]], inc="V")
            cx.b0_r1_prev = r1
            r2 = P.step("scalar",
                        lambda eng: eng.activation(cln16[:, :], nc16[:, :], AF.Ln),
                        waits=[r1], inc="S")
            r3 = P.step("scalar",
                        lambda eng: eng.activation(c16[:, :], cln16[:, :], AF.Exp, scale=C["immc"][:, 1:2]),
                        inc="S")
            vc_evs = []
            for sc in range(NSC):
                vsrc = vR[:, sc * 128 + 64 * par: sc * 128 + 64 * par + 64]
                vev = P.step("vector",
                             lambda eng, sc=sc, vsrc=vsrc: eng.tensor_scalar(
                                 Vc[:, sc * 128:sc * 128 + 64], vsrc,
                                 c16[:, sc:sc + 1], None, OP.mult),
                             waits=[r3, v_done, Vc_free], inc="V")
                vev2 = P.step("vector",
                              lambda eng, sc=sc: eng.tensor_scalar(
                                  Vc[:, sc * 128 + 64:(sc + 1) * 128], C["ones64"][:, :],
                                  c16[:, sc:sc + 1], None, OP.mult),
                              inc="V")
                vc_evs.append(vev2)
            fin_evs = []
            for it in range(NIT):
                for jc in range(NSC):
                    ev = P.step("tensor",
                                lambda eng, it=it, jc=jc: eng.matmul(
                                    cx.psm[it][:, :],
                                    Vc[:, jc * 128:(jc + 1) * 128],
                                    ST[:, jc * N + it * 512: jc * N + (it + 1) * 512],
                                    start=(jc == 0), stop=(jc == NSC - 1)),
                                waits=([vc_evs[-1], psm_free[it]] + exp_evs[-1:]) if jc == 0 else (),
                                inc="T" if jc == NSC - 1 else None)
                pv_done = ev
                rc = P.step("vector",
                            lambda eng, it=it: eng.reciprocal(trec[:, :], cx.psm[it][64:128, :]),
                            waits=[pv_done], inc="V")
                fo = P.step("vector",
                            lambda eng, it=it, h=h: eng.tensor_tensor(
                                cx.OT[(h % 2) * 64:(h % 2) * 64 + 64,
                                      (h // 2) * N + it * 512:(h // 2) * N + (it + 1) * 512],
                                cx.psm[it][0:64, :], trec[:, :], OP.mult),
                            waits=[rc, getattr(cx, "xgate", None)], inc="V")
                psm_free[it] = fo
                fin_evs.append(fo)
                OT_evs.append(fo)
            Vc_free = fin_evs[-1]

    sub_evs = emit_outproj(cx, wo, boc, OT_evs[-1],
                           [strip_consumed["q"], strip_consumed["k"]], pevs,
                           bv_col=bvcb, bias_out=boc2)

    out_evs = emit_ln(cx, "post", src_ready=sub_evs[-8:], resid_dram=resid_dram, tag="b0ln2",
                      src_chunks=[sub_evs[4 * c_ + 3] for c_ in range(NCH)],
                      scratch_free=OT_evs[-1])
    return out_evs


def emit_b2(cx, resid_dram, src_ready=()):
    P, nc, C, ar = cx.P, cx.nc, cx.C, cx.ar
    wqkv = nc.dram_tensor("b2_wqkv", [D, 3 * D], F32, kind="ExternalInput").ap()
    wout = nc.dram_tensor("b2_wout", [D, D], F32, kind="ExternalInput").ap()
    glogd = nc.dram_tensor("b2_glog", [16, N], F32, kind="ExternalInput").ap()

    qaug = cx.ln_xs
    kaug = cx.ln_sq
    vR, Vc, ws = cx.vR, cx.Vc, cx.ws
    wev, strip_consumed = cx.wev, cx.strip_consumed
    psm_free, pb_free = cx.psm_free, cx.pb_free
    gsig = getattr(cx, "b2_gsig", None)
    if gsig is None:
        gsig = ar.alloc_at("b2_gsig", [16, N], BF16, cx.R_OFF + 8192)
        glog_sb = ar.alloc_at("b2_glog", [16, N], BF16, cx.R_OFF + 12288)
        cx.b2_gsig, cx.b2_glog = gsig, glog_sb
    glog_sb = cx.b2_glog
    PCt, pc_evs = load_pc(cx, ["l1m", "hrep"], waits=[cx.pc_free_ev])

    gev = P.step("gpsimd", lambda eng: eng.dma_start(out=glog_sb[:, :], in_=glogd[:, :]),
                 waits=[cx.bs_free_ev], inc="DG", amt=16)
    sev = P.step("scalar", lambda eng: eng.activation(gsig[:, :], glog_sb[:, :], AF.Sigmoid),
                 waits=[gev], inc="S")
    cx.b2_glog_done = sev

    ln1_evs = emit_ln(cx, "pre", src_ready=list(src_ready), tag="b2ln1")
    xb_full = ln1_evs

    vones = cx.Vc[:, :].rearrange("p (s b) -> p s b", b=128)[:, :, 64:]
    ones_ev = P.step("vector", lambda eng: eng.memset(vones, 1.0), inc="V")

    for m, off in (("q", 0), ("k", 8), ("v", 16)):
        load_strip(cx, m, wqkv, off)

    OT_evs = []
    Vc_free = None
    eb_free = [None, None]
    for c in range(NCH):
        evac_evs = []
        for (mat, augA, augB) in (("q", qaug[0], qaug[1]), ("k", kaug[0], kaug[1])):
            for it in range(NIT):
                mm_w = [wev[mat], psm_free[it]] + (xb_full if c == 0 and it == 0 else [])
                for kc in range(NCH):
                    ev = P.step("tensor",
                                lambda eng, it=it, kc=kc, mat=mat: eng.matmul(
                                    cx.psm[it][:, :],
                                    ws[mat][:, kc * 128:(kc + 1) * 128],
                                    cx.xb[:, kc * CW + 1 + it * 512: kc * CW + 1 + (it + 1) * 512],
                                    start=(kc == 0), stop=(kc == NCH - 1)),
                                waits=mm_w if kc == 0 else (),
                                inc="T" if kc == NCH - 1 else None)
                    if kc == NCH - 1:
                        mm_done = ev
                for par, aug in ((0, augA), (1, augB)):
                    eev = P.step("scalar",
                                 lambda eng, it=it, par=par, aug=aug: eng.activation(
                                     aug[0:64, it * 512:(it + 1) * 512],
                                     cx.psm[it][par * 64:(par + 1) * 64, :], AF.Copy),
                                 waits=[mm_done], inc="S")
                    evac_evs.append(eev)
                psm_free[it] = evac_evs[-1]
        strip_consumed["q"] = evac_evs[7]
        strip_consumed["k"] = evac_evs[-1]
        v_evs = []
        for g in range(4):
            it = g
            grp_w = [wev["v"], psm_free[it]]
            for sci in range(4):
                sc = g * 4 + sci
                for kc in range(NCH):
                    ev = P.step("tensor",
                                lambda eng, it=it, sci=sci, sc=sc, kc=kc: eng.matmul(
                                    cx.psm[it][:, sci * 128:(sci + 1) * 128],
                                    cx.xb[:, kc * CW + 1 + sc * 128: kc * CW + 1 + (sc + 1) * 128],
                                    ws["v"][:, kc * 128:(kc + 1) * 128],
                                    start=(kc == 0 and sci == 0), stop=(kc == NCH - 1 and sci == 3),
                                    skip_group_check=True),
                                waits=grp_w if (sci == 0 and kc == 0) else (),
                                inc="T" if (kc == NCH - 1) else None)
            grp_done = ev
            for sci in range(4):
                sc = g * 4 + sci
                vev = P.step("scalar",
                             lambda eng, it=it, sci=sci, sc=sc: eng.activation(
                                 vR[:, sc * 128:(sc + 1) * 128],
                                 cx.psm[it][:, sci * 128:(sci + 1) * 128], AF.Copy),
                             waits=[grp_done], inc="S")
                v_evs.append(vev)
            psm_free[it] = v_evs[-1]
        strip_consumed["v"] = v_evs[-1]
        v_done = v_evs[-1]
        if c + 1 < NCH:
            for m, off in (("q", 0), ("k", 8), ("v", 16)):
                load_strip(cx, m, wqkv, off + c + 1)

        for par in range(2):
            h = 2 * c + par
            qa, ka = qaug[par], kaug[par]
            gq_evs = []
            for it in range(NIT):
                gm = P.step("tensor",
                            lambda eng, it=it, h=h: eng.matmul(
                                cx.pbig[0][0:64, (it % 2) * 512:(it % 2) * 512 + 512],
                                PCt["hrep"][:, h * 64:(h + 1) * 64],
                                gsig[:, it * 512:(it + 1) * 512],
                                start=True, stop=True),
                            waits=[sev, evac_evs[7], pc_evs[1],
                                   pb_free[0] if it < 2 else gq_evs[it - 2]],
                            inc="T")
                gq = P.step("vector",
                            lambda eng, it=it, qa=qa: eng.scalar_tensor_tensor(
                                qa[0:64, it * 512:(it + 1) * 512],
                                qa[0:64, it * 512:(it + 1) * 512], 0.125,
                                cx.pbig[0][0:64, (it % 2) * 512:(it % 2) * 512 + 512],
                                OP.mult, OP.mult),
                            waits=[gm], inc="V")
                gq_evs.append(gq)
            pb_free[0] = gq_evs[-1]
            vc_evs = []
            for sc in range(NSC):
                vev = P.step("vector",
                             lambda eng, sc=sc, par=par: eng.tensor_copy(
                                 Vc[:, sc * 128:sc * 128 + 64],
                                 vR[:, sc * 128 + 64 * par: sc * 128 + 64 * par + 64]),
                             waits=[ones_ev, v_done, Vc_free], inc="V")
                vc_evs.append(vev)
            fin_evs = []
            pv_last = {}
            glist = []
            for jc in range(NSC):
                its = list(range(jc // 4, NIT))
                glist += [(jc, gi, grp) for gi, grp in
                          enumerate(its[i:i + 2] for i in range(0, len(its), 2))]
            g_stash = {}

            def b2_st(gx):
                jc, gi, grp = glist[gx]
                pbi = gx % 2
                ebi = gx % 2
                stw = [pb_free[pbi]] + ([gq_evs[-1], vc_evs[-1], evac_evs[-1]]
                                        if gx == 0 else [])
                ev = None
                for k2, it in enumerate(grp):
                    ev = P.step("tensor",
                                lambda eng, pbi=pbi, k2=k2, it=it, jc=jc, qa=qa, ka=ka: eng.matmul(
                                    cx.pbig[pbi][:, k2 * 512:(k2 + 1) * 512],
                                    ka[0:64, jc * 128:(jc + 1) * 128],
                                    qa[0:64, it * 512:(it + 1) * 512],
                                    start=True, stop=True),
                                waits=stw if k2 == 0 else (), inc="T")
                st_done = ev
                wexp = [st_done, eb_free[ebi]]
                eev = P.step("scalar",
                             lambda eng, pbi=pbi, ebi=ebi, w=len(grp): eng.activation(
                                 cx.Ebuf[ebi][:, 0:w * 512],
                                 cx.pbig[pbi][:, 0:w * 512], AF.Exp),
                             waits=wexp, inc="S")
                pb_free[pbi] = eev
                mev = eev
                if grp[0] == jc // 4:
                    mev = P.step("vector",
                                 lambda eng, ebi=ebi, jc=jc: eng.tensor_tensor(
                                     cx.Ebuf[ebi][:, 0:512], cx.Ebuf[ebi][:, 0:512],
                                     PCt["l1m"][:, (jc % 4) * 512:(jc % 4 + 1) * 512],
                                     OP.mult),
                                 waits=[eev, pc_evs[0]], inc="V")
                g_stash[gx] = mev

            def b2_pv(gx):
                jc, gi, grp = glist[gx]
                ebi = gx % 2
                mev = g_stash.pop(gx)
                pv_evs = []
                for k2, it in enumerate(grp):
                    ev = P.step("tensor",
                                lambda eng, ebi=ebi, k2=k2, it=it, jc=jc: eng.matmul(
                                    cx.psm[it][:, :],
                                    Vc[:, jc * 128:(jc + 1) * 128],
                                    cx.Ebuf[ebi][:, k2 * 512:(k2 + 1) * 512],
                                    start=(jc == 0), stop=(jc == 4 * it + 3)),
                                waits=[mev] + ([psm_free[it]] if jc == 0 else []),
                                inc="T")
                    pv_evs.append(ev)
                    pv_last[it] = ev
                    if jc == 4 * it + 3:
                        rc = P.step("vector",
                                    lambda eng, it=it: eng.reciprocal(
                                        cx.trec[:, :], cx.psm[it][64:128, :]),
                                    waits=[ev], inc="V")
                        fo = P.step("vector",
                                    lambda eng, it=it, h=h: eng.tensor_tensor(
                                        cx.OT[(h % 2) * 64:(h % 2) * 64 + 64,
                                              (h // 2) * N + it * 512:(h // 2) * N + (it + 1) * 512],
                                        cx.psm[it][0:64, :], cx.trec[:, :], OP.mult),
                                    waits=[rc, getattr(cx, "xgate", None)], inc="V")
                        psm_free[it] = fo
                        fin_evs.append(fo)
                        OT_evs.append(fo)
                eb_free[ebi] = pv_evs[-1]

            for gx in range(len(glist)):
                b2_st(gx)
                if gx > 0:
                    b2_pv(gx - 1)
            b2_pv(len(glist) - 1)
            Vc_free = fin_evs[-1]

    cx.b2_lastread = OT_evs[-1]
    sub_evs = emit_outproj(cx, wout, None, OT_evs[-1],
                           [strip_consumed["q"], strip_consumed["k"]], [])
    out_evs = emit_ln(cx, "post", src_ready=sub_evs[-8:], resid_dram=resid_dram, tag="b2ln2",
                      src_chunks=[sub_evs[4 * c_ + 3] for c_ in range(NCH)],
                      scratch_free=OT_evs[-1])
    return out_evs


def emit_b3(cx, resid_dram, src_ready=()):
    P, nc, C, ar = cx.P, cx.nc, cx.C, cx.ar
    AR = nc.dram_tensor("b3_AR", [4, 512, 512], F32, kind="ExternalInput").ap()
    BR = nc.dram_tensor("b3_BR", [4, 512, 512], F32, kind="ExternalInput").ap()
    AI = nc.dram_tensor("b3_AI", [4, 512, 512], F32, kind="ExternalInput").ap()
    BI = nc.dram_tensor("b3_BI", [4, 512, 512], F32, kind="ExternalInput").ap()
    brs = nc.dram_tensor("b3_brs", [128, 4], F32, kind="ExternalInput").ap()
    bis = nc.dram_tensor("b3_bis", [128, 4], F32, kind="ExternalInput").ap()

    if not hasattr(cx, "b3_wr"):
        o = cx.R_OFF
        cx.b3_wr = ar.alloc_at("b3_wr", [128, 4 * 512], BF16, o)
        cx.b3_wi = ar.alloc_at("b3_wi", [128, 4 * 512], BF16, o + 4096)
        cx.b3_wineg = ar.alloc_at("b3_wineg", [128, 4 * 512], BF16, o + 8192)
        cx.b3_a = [ar.alloc_at(f"b3_a{i}", [128, 4 * 512], BF16, o + 12288 + i * 4096) for i in range(2)]
        cx.b3_b = [ar.alloc_at(f"b3_b{i}", [128, 4 * 512], BF16, o + 20480 + i * 4096) for i in range(2)]
        cx.b3_bst = [ar.alloc_at(f"b3_bst{i}", [128, 4 * 512], F32, o + 28672 + i * 8192) for i in range(2)]
    wr, wi, wineg = cx.b3_wr, cx.b3_wi, cx.b3_wineg
    abuf, bbuf, bst = cx.b3_a, cx.b3_b, cx.b3_bst
    ld_gate = getattr(cx, "b2_glog_done", None) or cx.bs_free_ev
    evac_gate = getattr(cx, "b2_lastread", None)
    brc, bic = cx.b3_bb, cx.b3_bb2

    pevs = []
    pevs.append(P.step("sync", lambda eng: eng.dma_start(out=brc[:, :], in_=brs[:, :]), inc="DS", amt=16))
    pevs.append(P.step("sync", lambda eng: eng.dma_start(out=bic[:, :], in_=bis[:, :]), inc="DS", amt=16))

    psm_free = cx.psm_free
    wr_evs = []
    ld_consumed = [None, None]
    bst_consumed = [None, None]
    for half, (Adr, Bdr, wdst) in enumerate(((AR, BR, wr), (AI, BI, wi))):
        mm_last = None
        for cc in range(4):
            bi = cc % 2
            src_a = Adr[cc].rearrange("(kc p) m -> p kc m", p=128)
            src_b = Bdr[cc].rearrange("(kc p) m -> p kc m", p=128)
            ea = P.step("gpsimd",
                        lambda eng, d=abuf[bi], s=src_a: eng.dma_start(
                            out=d[:, :].rearrange("p (kc m) -> p kc m", m=512), in_=s),
                        waits=[ld_consumed[bi], ld_gate], inc="DG", amt=16)
            eb0 = P.step("sync",
                         lambda eng, d=bst[bi], s=src_b: eng.dma_start(
                             out=d[:, :].rearrange("p (kc m) -> p kc m", m=512), in_=s),
                         waits=[bst_consumed[bi], ld_gate if cc < 2 else None],
                         inc="DS", amt=16)
            eb = P.step("scalar",
                        lambda eng, d=bbuf[bi], s=bst[bi]: eng.activation(
                            d[:, :], s[:, :], AF.Copy),
                        waits=[eb0, ld_consumed[bi]], inc="S")
            bst_consumed[bi] = eb
            for mc in range(4):
                for kc in range(4):
                    w0 = [ea, eb, psm_free[mc]] if kc == 0 else ()
                    ev = P.step("tensor",
                                lambda eng, mc=mc, kc=kc, bi=bi, cc=cc, half=half: eng.matmul(
                                    cx.psm[mc][:, :],
                                    abuf[bi][:, kc * 512 + mc * 128: kc * 512 + (mc + 1) * 128],
                                    bbuf[bi][:, kc * 512:(kc + 1) * 512],
                                    start=(cc == 0 and kc == 0), stop=(cc == 3 and kc == 3)),
                                waits=w0, inc="T" if kc == 3 else None)
            ld_consumed[bi] = ev
            mm_last = ev
        for mc in range(4):
            e1 = P.step("scalar",
                        lambda eng, mc=mc, wdst=wdst: eng.activation(
                            wdst[:, mc * 512:(mc + 1) * 512], cx.psm[mc][:, :], AF.Copy),
                        waits=[mm_last, evac_gate] + cx.init_evs, inc="S")
            if half == 1:
                e2 = P.step("scalar",
                            lambda eng, mc=mc: eng.activation(
                                wineg[:, mc * 512:(mc + 1) * 512], cx.psm[mc][:, :],
                                AF.Identity, scale=C["immc"][:, 6:7]),
                            inc="S")
                psm_free[mc] = e2
            else:
                psm_free[mc] = e1
            wr_evs.append(e1)
    w_ready = wr_evs[-1]

    ln1_evs = emit_ln(cx, "pre", src_ready=list(src_ready), tag="b3ln1")
    xb_full = ln1_evs

    sub_evs = []
    for half in range(2):
        terms = ((wr, 0), (wineg, 4)) if half == 0 else ((wi, 0), (wr, 4))
        bias = brc if half == 0 else bic
        for oc in range(4):
            for it in range(NIT):
                w0 = [w_ready, psm_free[it]] + (xb_full if half == 0 and oc == 0 and it == 0 else [])
                first = True
                for wt, choff in terms:
                    for kc in range(4):
                        ev = P.step("tensor",
                                    lambda eng, wt=wt, choff=choff, kc=kc, oc=oc, it=it,
                                    first=first, last=(choff == terms[1][1] and kc == 3): eng.matmul(
                                        cx.psm[it][:, :],
                                        wt[:, kc * 512 + oc * 128: kc * 512 + (oc + 1) * 128],
                                        cx.xb[:, (choff + kc) * CW + 1 + it * 512:
                                              (choff + kc) * CW + 1 + (it + 1) * 512],
                                        start=first, stop=last),
                                    waits=w0 if first else (), inc="T" if (choff == terms[1][1] and kc == 3) else None)
                        first = False
                mm_done = ev
                sev = P.step("scalar",
                             lambda eng, half=half, oc=oc, it=it, bias=bias: eng.activation(
                                 cx.subout[:, (half * 4 + oc) * N + it * 512:
                                           (half * 4 + oc) * N + (it + 1) * 512],
                                 cx.psm[it][:, :], AF.Identity, bias=bias[:, oc:oc + 1]),
                             waits=[mm_done, getattr(cx, "xgate", None)] + pevs, inc="S")
                psm_free[it] = sev
                sub_evs.append(sev)

    out_evs = emit_ln(cx, "post", src_ready=sub_evs[-8:], resid_dram=resid_dram, tag="b3ln2",
                      src_chunks=[sub_evs[4 * c_ + 3] for c_ in range(NCH)],
                      scratch_free=sub_evs[-1])
    return out_evs


def host_b3_inputs(inp):
    f = np.float32
    d = {}
    G1R = np.asarray(inp["L1_ffn_G1_R"], f); G2R = np.asarray(inp["L1_ffn_G2_R"], f)
    G1I = np.asarray(inp["L1_ffn_G1_I"], f); G2I = np.asarray(inp["L1_ffn_G2_I"], f)

    def build_AB(G1, G2):
        A = np.zeros((4, 512, 512), f)
        B = np.zeros((4, 512, 512), f)
        for c in range(4):
            G1c = G1[0, :, c, :]
            G2c = G2[c, :, 0, :]
            for g in range(16):
                A[c, np.arange(32)[:, None] * 16 + g, np.arange(32)[None, :] * 16 + g] = G1c
            for b in range(32):
                B[c, b * 16 + np.arange(16)[:, None], b * 16 + np.arange(16)[None, :]] = G2c.T
        return np.ascontiguousarray(A), np.ascontiguousarray(B)

    d["b3_AR"], d["b3_BR"] = build_AB(G1R, G2R)
    d["b3_AI"], d["b3_BI"] = build_AB(G1I, G2I)
    d["b3_brs"] = np.ascontiguousarray(np.asarray(inp["L1_ffn_bias_real"], f).reshape(4, 128).T)
    d["b3_bis"] = np.ascontiguousarray(np.asarray(inp["L1_ffn_bias_imag"], f).reshape(4, 128).T)
    return d


def host_b2_inputs(inp):
    f = np.float32
    d = {}
    d["b2_wqkv"] = np.ascontiguousarray(np.asarray(inp["L1_attn_w_qkv"], f))
    d["b2_wout"] = np.ascontiguousarray(np.asarray(inp["L1_attn_w_out"], f))
    bg = np.asarray(inp["L1_attn_bandha_gate"], f)
    cycles = np.array([[5, 6, 7, 8][i % 4] for i in range(16)])
    phases = np.arange(N)[None, :] % cycles[:, None]
    d["b2_glog"] = np.ascontiguousarray(np.take_along_axis(bg, phases, axis=1).astype(f))
    return d


def host_b0_inputs(inp):
    f = np.float32
    d = {}
    d["b0_wqT"] = np.ascontiguousarray(np.asarray(inp["L0_attn_wq"], f).T)
    d["b0_wkT"] = np.ascontiguousarray(np.asarray(inp["L0_attn_wk"], f).T)
    d["b0_wvT"] = np.ascontiguousarray(np.asarray(inp["L0_attn_wv"], f).T)
    d["b0_woT"] = np.ascontiguousarray(np.asarray(inp["L0_attn_wo"], f).T)
    d["b0_bqs"] = np.ascontiguousarray(np.asarray(inp["L0_attn_bq"], f).reshape(8, 128).T)
    d["b0_bks"] = np.ascontiguousarray(np.asarray(inp["L0_attn_bk"], f).reshape(8, 128).T)
    d["b0_bos"] = np.ascontiguousarray(np.asarray(inp["L0_attn_bo"], f).reshape(8, 128).T)
    d["b0_bvs"] = np.ascontiguousarray(np.asarray(inp["L0_attn_bv"], f).reshape(8, 128).T)
    return d


def host_const_inputs(cx):
    d = {"c_" + k: v for k, v in cx.host_const_arrays.items()}
    d.update({"pc_" + k: v for k, v in cx.host_pc_arrays.items()})
    return d


def build_b2_program():
    nc = bass.Bass("TRN2", target_bir_lowering=False, debug=False)
    cx = build_base(nc)
    emit_init(cx)
    emit_load_x(cx)
    out_evs = emit_b2(cx, resid_dram=cx.xin)
    emit_store_x(cx, out_evs)
    with nc.Block() as block:
        cx.P.emit(block)
    return nc, cx


def build_fused_program():
    nc = bass.Bass("TRN2", target_bir_lowering=False, debug=False)
    cx = build_base(nc)
    cx.b1_eb_free = [None, None]
    emit_init(cx)
    emit_load_x(cx)
    ev0 = emit_b0(cx, weights_ready=[], resid_dram=cx.xin)
    sc1, sp1 = emit_spill_x(cx, waits=ev0[-4:])
    xTz = cx.xT[:, :].rearrange("p (c w) -> p c w", w=CW)[:, :, 0:1]
    rz = cx.P.step("vector", lambda eng: eng.memset(xTz, 0.0), waits=[ev0[-1]], inc="V")
    cx.pc_free_ev = ev0[-1]
    cx.bs_free_ev = ev0[-1]
    ev1 = emit_b1(cx, resid_dram=sc1, src_ready=ev0[-4:] + [rz])
    cx.pc_free_ev = ev1[-1]
    cx.bs_free_ev = ev1[-1]
    sc2, sp2 = emit_spill_x(cx, waits=ev1[-4:])
    ev2 = emit_b2(cx, resid_dram=sc2, src_ready=ev1[-4:])
    cx.pc_free_ev = ev2[-1]
    cx.bs_free_ev = ev2[-1]
    sc3, sp3 = emit_spill_x(cx, waits=ev2[-4:])
    ev3 = emit_b3(cx, resid_dram=sc3, src_ready=ev2[-4:])
    emit_store_x(cx, ev3)
    with nc.Block() as block:
        cx.P.emit(block)
    return nc, cx


def host_fused_inputs(inp, cx):
    d = {}
    d.update(host_b0_inputs(inp))
    d.update(host_b1_inputs(inp))
    d.update(host_b2_inputs(inp))
    d.update(host_b3_inputs(inp))
    d.update(host_const_inputs(cx))
    return d


def build_b3_program():
    nc = bass.Bass("TRN2", target_bir_lowering=False, debug=False)
    cx = build_base(nc)
    emit_init(cx)
    emit_load_x(cx)
    out_evs = emit_b3(cx, resid_dram=cx.xin)
    emit_store_x(cx, out_evs)
    with nc.Block() as block:
        cx.P.emit(block)
    return nc, cx


def build_b0_program():
    nc = bass.Bass("TRN2", target_bir_lowering=False, debug=False)
    cx = build_base(nc)
    emit_init(cx)
    emit_load_x(cx)
    out_evs = emit_b0(cx, weights_ready=[], resid_dram=cx.xin)
    emit_store_x(cx, out_evs)
    with nc.Block() as block:
        cx.P.emit(block)
    return nc, cx


def run_b0(x_full, inp):
    nc, cx = build_b0_program()
    wd = host_b0_inputs(inp)
    cd = host_const_inputs(cx)
    in_maps = []
    for corei in range(8):
        m = dict(wd)
        m.update(cd)
        m["xT_in"] = np.ascontiguousarray(x_full[corei].T)
        in_maps.append(m)
    res = run_bass_kernel_spmd(nc, in_maps, core_ids=list(range(8)))
    return np.stack([r["xT_out"].T for r in res.results])



B1_DEPTH = 6
B1_ATTN_ONLY = False
B1_SKIP_LN2 = False


def emit_b1(cx, resid_dram, src_ready=()):
    P, nc, C, ar = cx.P, cx.nc, cx.C, cx.ar
    DEPTH = 6
    DI2 = 2816
    Wqkv = nc.dram_tensor("b1_Wqkv", [DEPTH, D, 3 * D], F32, kind="ExternalInput").ap()
    Wo = nc.dram_tensor("b1_Wo", [DEPTH, D, D], F32, kind="ExternalInput").ap()
    W1p = nc.dram_tensor("b1_W1p", [DEPTH, D, 2 * DI2], F32, kind="ExternalInput").ap()
    W2p = nc.dram_tensor("b1_W2p", [DEPTH, DI2, D], F32, kind="ExternalInput").ap()
    qsc = nc.dram_tensor("b1_qsc", [128, DEPTH], F32, kind="ExternalInput").ap()
    ksc = nc.dram_tensor("b1_ksc", [128, DEPTH], F32, kind="ExternalInput").ap()
    lng = nc.dram_tensor("b1_lng", [128, DEPTH * NCH], F32, kind="ExternalInput").ap()
    lnb = nc.dram_tensor("b1_lnb", [128, DEPTH * NCH], F32, kind="ExternalInput").ap()
    b1a = nc.dram_tensor("b1_b1as", [128, DEPTH * 22], F32, kind="ExternalInput").ap()
    b1g = nc.dram_tensor("b1_b1gs", [128, DEPTH * 22], F32, kind="ExternalInput").ap()
    b2s = nc.dram_tensor("b1_b2s", [128, DEPTH * NCH], F32, kind="ExternalInput").ap()

    if not hasattr(cx, "b1_qq"):
        o = cx.R_OFF + 20992
        cx.b1_qq = ar.alloc_at("b1_qq", [128, NCH * 512], BF16, o)
        cx.b1_kq = ar.alloc_at("b1_kq", [128, NCH * 768], BF16, o + 8192)
        cx.b1_vq = ar.alloc_at("b1_vq", [128, 6 * 1024], BF16, o + 20480)
        cx.b1_OTq = ar.alloc_at("b1_OTq", [128, NCH * 512], BF16, o + 32768)
        cx.b1_wv = ar.alloc_at("b1_wv", [128, NCH * 256], BF16, o + 40960)
        cx.b1_Mst = ar.alloc_at("b1_Mst", [128, 22 * 1024], BF16, cx.R_OFF)
        cx.b1_w1s = [ar.alloc_at(f"b1_w1s{i}", [128, 8 * 128], BF16, cx.R_OFF + 45056 + i * 2048)
                     for i in range(4)]
        cx.b1_w2s = [ar.alloc_at(f"b1_w2s{i}", [128, 22 * 128], BF16, cx.R_OFF + 53248 + i * 5632)
                     for i in range(2)]
    qq, kq, vq, OTq = cx.b1_qq, cx.b1_kq, cx.b1_vq, cx.b1_OTq
    wvf, sqq = cx.b1_wv, cx.b1_sqq
    t0b, t1b, t2b = cx.b1_t0, cx.b1_t1, cx.b1_t2
    q2c, rfq, rfq2 = cx.b1_q2c, cx.b1_rfq, cx.b1_rfq2
    Mst, w1s, w2s = cx.b1_Mst, cx.b1_w1s, cx.b1_w2s

    stash = cx.b1_stash
    pevs = []
    for name, dr_ in (("qsc", qsc), ("ksc", ksc), ("lng", lng), ("lnb", lnb),
                      ("b1a", b1a), ("b1g", b1g), ("b2s", b2s)):
        sb = stash[name]
        pevs.append(P.step("sync", lambda eng, sb=sb, dr_=dr_: eng.dma_start(out=sb[:, :], in_=dr_[:, :]),
                           inc="DS", amt=16))

    psm_free = cx.psm_free
    pb_free = cx.pb_free

    ln1_evs = emit_ln(cx, "pre", src_ready=list(src_ready), out_xT=True, tag="b1ln1")
    xb_full = ln1_evs

    xb_ready = ln1_evs[-1]

    vq_tail_ev = None
    kq_tail_ev = None

    pc_names = ["hsel", "hselT", "l0m", "rtAq", "rtAk", "rtBq", "rtBk"]
    for l in range(B1_DEPTH):
        PCt, pc_evs = load_pc(cx, pc_names,
                              waits=[cx.pc_free_ev if l == 0 else None,
                                     getattr(cx, "b1_mst_war", None)])
        lpevs = pevs + pc_evs
        wql = Wqkv[l]
        wo_l = Wo[l]
        OT_evs = []
        resid_evs = []
        prep_done_prev = None
        for qt in range(4):
            tok0 = qt * 512
            pevq = {}
            for c in range(NCH):
                for mi, (mat, moff2) in enumerate((("q", 0), ("k", 8))):
                    lev = load_strip(cx, mat, wql, moff2 + c)
                    pb = (2 * c + mi) % 2
                    w0 = [lev, psm_free[pb], xb_ready] + ([prep_done_prev] if c == 0 else [])
                    for kc in range(NCH):
                        sh = 1 if kc >= 4 else 0
                        ev = P.step("tensor",
                                    lambda eng, c=c, kc=kc, mat=mat, sh=sh, tok0=tok0, pb=pb: eng.matmul(
                                        cx.psm[pb][:, :],
                                        cx.ws[mat][:, kc * 128:(kc + 1) * 128],
                                        cx.xb[:, kc * CW + 1 - sh + tok0: kc * CW + 1 - sh + tok0 + 512],
                                        start=(kc == 0), stop=(kc == NCH - 1)),
                                    waits=w0 if kc == 0 else (),
                                    inc="T" if kc == NCH - 1 else None)
                    if mat == "q":
                        eev = P.step("scalar",
                                     lambda eng, c=c, pb=pb: eng.activation(
                                         qq[:, c * 512:(c + 1) * 512], cx.psm[pb][:, :], AF.Copy),
                                     waits=[ev], inc="S")
                    else:
                        eev = P.step("scalar",
                                     lambda eng, c=c, pb=pb: eng.activation(
                                         kq[:, c * 768 + 256: c * 768 + 768], cx.psm[pb][:, :], AF.Copy),
                                     waits=[ev, kq_tail_ev], inc="S")
                    pevq[(mat, c)] = eev
                    psm_free[pb] = eev
                    cx.strip_consumed[mat] = ev
            for dt_ in range(4):
                wvsrc = wql[:, 2 * D + dt_ * 256: 2 * D + (dt_ + 1) * 256].rearrange(
                    "(kc p) m -> p kc m", p=128)
                wv_ev = P.step("gpsimd",
                               lambda eng, s=wvsrc: eng.dma_start(
                                   out=wvf[:, :].rearrange("p (kc m) -> p kc m", m=256), in_=s),
                               waits=[getattr(cx, "b1_wv_consumed", None),
                                      getattr(cx, "b1_mst_war", None)], inc="DG", amt=16)
                for sci in range(4):
                    scg = qt * 4 + sci
                    slot = 2 + sci
                    pi = 2 + (sci % 2)
                    w0 = [wv_ev, psm_free[pi], vq_tail_ev]
                    for kc in range(NCH):
                        ev = P.step("tensor",
                                    lambda eng, sci=sci, scg=scg, kc=kc, dt_=dt_, pi=pi: eng.matmul(
                                        cx.psm[pi][:, 0:256],
                                        cx.xb[:, kc * CW + 1 - (1 if kc >= 4 else 0) + scg * 128:
                                              kc * CW + 1 - (1 if kc >= 4 else 0) + (scg + 1) * 128],
                                        wvf[:, kc * 256:(kc + 1) * 256],
                                        start=(kc == 0), stop=(kc == NCH - 1)),
                                    waits=w0 if kc == 0 else (),
                                    inc="T" if kc == NCH - 1 else None)
                    vev = P.step("scalar",
                                 lambda eng, slot=slot, dt_=dt_, pi=pi: eng.activation(
                                     vq[:, slot * 1024 + dt_ * 256: slot * 1024 + (dt_ + 1) * 256],
                                     cx.psm[pi][0:128, 0:256], AF.Copy),
                                 waits=[ev], inc="S")
                    psm_free[pi] = vev
                cx.b1_wv_consumed = ev
            v_done = vev
            matspec = {}
            m3s = {}
            last_sq = prep_done_prev
            for mi, mat in enumerate(("q", "k")):
                buf = qq if mat == "q" else kq
                boff = 0 if mat == "q" else 256
                bw = 512 if mat == "q" else 768
                scol = stash["qsc" if mat == "q" else "ksc"][:, l:l + 1]
                tA = PCt["rtAq"] if mat == "q" else PCt["rtAk"]
                tB = PCt["rtBq"] if mat == "q" else PCt["rtBk"]
                rf = rfq if mat == "q" else rfq2
                bank = 1 if mat == "q" else 3
                pbq = 1 - mi
                matspec[mat] = (buf, boff, bw, scol, tA, tB, rf, bank)
                q2mm = []
                for c in range(NCH):
                    sev = P.step("scalar",
                                 lambda eng, buf=buf, c=c, boff=boff, bw=bw: eng.activation(
                                     sqq[:, :], buf[:, c * bw + boff: c * bw + boff + 512], AF.Square),
                                 waits=[pevq[(mat, c)], last_sq], inc="S")
                    ev = P.step("tensor",
                                lambda eng, c=c, hs=PCt["hsel"], pbq=pbq: eng.matmul(
                                    cx.pbig[pbq][0:16, 0:512], hs[:, c * 16:(c + 1) * 16],
                                    sqq[:, :], start=(c == 0), stop=(c == NCH - 1)),
                                waits=[sev, pb_free[pbq] if c == 0 else None] + (lpevs if c == 0 else []),
                                inc="T")
                    q2mm.append(ev)
                    last_sq = ev
                m1 = P.step("vector",
                            lambda eng, pbq=pbq: eng.tensor_scalar_max(
                                q2c[:, :], cx.pbig[pbq][0:16, 0:512], 1e-24),
                            waits=[q2mm[-1]] + ([m3s["q"]] if mat == "k" else []), inc="V")
                m2 = P.step("scalar",
                            lambda eng: eng.activation(q2c[:, :], q2c[:, :], AF.Ln),
                            waits=[m1], inc="S")
                m3 = P.step("scalar",
                            lambda eng, rf=rf: eng.activation(rf[:, :], q2c[:, :], AF.Exp,
                                                              scale=C["immc"][0:16, 1:2]),
                            inc="S")
                pb_free[pbq] = m1
                m3s[mat] = m3
            prep_ev = {}

            def chunk_prep(c):
                for mat in ("q", "k"):
                    buf, boff, bw, scol, tA, tB, rf, bank = matspec[mat]
                    rm = P.step("tensor",
                                lambda eng, c=c, ht=PCt["hselT"], rf=rf, bank=bank: eng.matmul(
                                    cx.psm[bank][:, :], ht[:, c * 128:(c + 1) * 128],
                                    rf[:, :], start=True, stop=True),
                                waits=[m3s[mat] if c == 0 else None, psm_free[bank]], inc="T")
                    e1 = P.step("vector",
                                lambda eng, buf=buf, c=c, boff=boff, bw=bw, scol=scol, bank=bank:
                                eng.scalar_tensor_tensor(
                                    t0b[:, :], buf[:, c * bw + boff: c * bw + boff + 512], scol,
                                    cx.psm[bank][:, :], OP.mult, OP.mult),
                                waits=[rm] + lpevs, inc="V")
                    psm_free[bank] = e1
                    e2 = P.step("vector",
                                lambda eng, tA=tA, tok0=tok0: eng.tensor_tensor(
                                    t1b[:, :], t0b[:, :], tA[:, tok0: tok0 + 512], OP.mult),
                                inc="V")
                    rtm = P.step("tensor",
                                 lambda eng, bank=bank: eng.matmul(
                                     cx.psm[bank][:, :], C["rot"][:, :], t0b[:, :],
                                     start=True, stop=True),
                                 waits=[e1], inc="T")
                    e3 = P.step("vector",
                                lambda eng, tB=tB, tok0=tok0, bank=bank: eng.tensor_tensor(
                                    t2b[:, :], cx.psm[bank][:, :], tB[:, tok0: tok0 + 512], OP.mult),
                                waits=[rtm], inc="V")
                    psm_free[bank] = e3
                    e8 = P.step("vector",
                                lambda eng, buf=buf, c=c, boff=boff, bw=bw: eng.tensor_tensor(
                                    buf[:, c * bw + boff: c * bw + boff + 512], t1b[:, :], t2b[:, :],
                                    OP.add),
                                inc="V")
                    prep_ev[(mat, c)] = e8
            la_fin = []
            items = [(bq, 2 * hc + par) for hc in range(NCH) for par in (0, 1) for bq in (0, 1)]
            la_stash = {}

            def st_phase(idx):
                bq, h = items[idx]
                B = 2 * qt + bq
                first_blk = (B == 0)
                njc = 2 if first_blk else 4
                jc0 = 2 if first_blk else 0
                par = h % 2
                hc = h // 2
                pbi = idx % 2
                ebi = pbi
                stw = [pb_free[pbi], prep_ev[("q", h // 2)], prep_ev[("k", h // 2)]]
                ev = None
                for j2 in range(njc):
                    jc = jc0 + j2
                    ev = P.step("tensor",
                                lambda eng, pbi=pbi, j2=j2, jc=jc, par=par, hc=hc, bq=bq: eng.matmul(
                                    cx.pbig[pbi][:, j2 * 256:(j2 + 1) * 256],
                                    kq[par * 64:(par + 1) * 64,
                                       hc * 768 + bq * 256 + jc * 128: hc * 768 + bq * 256 + (jc + 1) * 128],
                                    qq[par * 64:(par + 1) * 64, hc * 512 + bq * 256: hc * 512 + (bq + 1) * 256],
                                    start=True, stop=True),
                                waits=stw if j2 == 0 else (), inc="T")
                st_done = ev
                eev = P.step("scalar",
                             lambda eng, pbi=pbi, ebi=ebi, njc=njc: eng.activation(
                                 cx.Ebuf[ebi][:, 0:njc * 256], cx.pbig[pbi][:, 0:njc * 256], AF.Exp),
                             waits=[st_done, cx.b1_eb_free[ebi]], inc="S")
                pb_free[pbi] = eev
                mev = P.step("vector",
                             lambda eng, ebi=ebi, njc=njc, jc0=jc0, lm=PCt["l0m"]: eng.tensor_tensor(
                                 cx.Ebuf[ebi][:, 0:njc * 256], cx.Ebuf[ebi][:, 0:njc * 256],
                                 lm[:, jc0 * 256:(jc0 + njc) * 256], OP.mult),
                             waits=[eev], inc="V")
                la_stash[idx] = (mev, njc, jc0)

            def pv_phase(idx):
                bq, h = items[idx]
                mev, njc, jc0 = la_stash.pop(idx)
                pbi = idx % 2
                ebi = pbi
                pvw = [mev, psm_free[pbi * 2]]
                ev = None
                for j2 in range(njc):
                    jc = jc0 + j2
                    slot = 2 * bq + jc
                    P.step("tensor",
                           lambda eng, j2=j2, slot=slot, h=h, pbi=pbi, ebi=ebi, njc=njc: eng.matmul(
                               cx.psm[pbi * 2][0:64, 0:256],
                               vq[:, slot * 1024 + h * 64: slot * 1024 + (h + 1) * 64],
                               cx.Ebuf[ebi][:, j2 * 256:(j2 + 1) * 256],
                               start=(j2 == 0), stop=(j2 == njc - 1),
                               skip_group_check=True),
                           waits=pvw if j2 == 0 else ())
                    ev = P.step("tensor",
                                lambda eng, j2=j2, pbi=pbi, ebi=ebi, njc=njc: eng.matmul(
                                    cx.psm[pbi * 2][64:128, 0:256],
                                    C["ones64"][:, :],
                                    cx.Ebuf[ebi][:, j2 * 256:(j2 + 1) * 256],
                                    start=(j2 == 0), stop=(j2 == njc - 1),
                                    skip_group_check=True),
                                inc="T" if j2 == njc - 1 else None)
                pv_done = ev
                cx.b1_eb_free[ebi] = pv_done
                rc = P.step("vector",
                            lambda eng, pbi=pbi: eng.reciprocal(
                                cx.trec[:, 0:256], cx.psm[pbi * 2][64:128, 0:256]),
                            waits=[pv_done], inc="V")
                fo = P.step("vector",
                            lambda eng, pbi=pbi, h=h, bq=bq: eng.tensor_tensor(
                                OTq[(h % 2) * 64:(h % 2) * 64 + 64,
                                    (h // 2) * 512 + bq * 256:(h // 2) * 512 + (bq + 1) * 256],
                                cx.psm[pbi * 2][0:64, 0:256], cx.trec[:, 0:256], OP.mult),
                            waits=[rc], inc="V")
                psm_free[pbi * 2] = fo
                la_fin.append(fo)
                OT_evs.append(fo)

            next_st = 0
            chunk_prep(0)
            for c in range(1, NCH):
                chunk_prep(c)
                while next_st < 4 * c:
                    st_phase(next_st)
                    if next_st > 0:
                        pv_phase(next_st - 1)
                    next_st += 1
            while next_st < len(items):
                st_phase(next_st)
                if next_st > 0:
                    pv_phase(next_st - 1)
                next_st += 1
            pv_phase(len(items) - 1)
            prep_done = prep_ev[("k", NCH - 1)]
            if qt < 3:
                bc1 = P.step("vector",
                             lambda eng: eng.tensor_copy(
                                 vq[:, 0: 2 * 1024], vq[:, 4 * 1024: 6 * 1024]),
                             waits=[la_fin[-1], la_fin[-2]], inc="V")
                bc2 = P.step("vector",
                             lambda eng: eng.tensor_copy(
                                 kq[:, :].rearrange("p (c w) -> p c w", w=768)[:, :, 0:256],
                                 kq[:, :].rearrange("p (c w) -> p c w", w=768)[:, :, 512:768]),
                             inc="V")
                vq_tail_ev = bc1
                kq_tail_ev = bc2
            else:
                vq_tail_ev = None
                kq_tail_ev = None
            wo_evs = {}
            wo_consumed = [getattr(cx, "b1_woc0", None), getattr(cx, "b1_woc1", None)]

            def load_wo(ec, qt=qt):
                dst = cx.wos[ec % 2][:, :].rearrange("p (kc q) -> p kc q", q=128)
                war = [wo_consumed[ec % 2]]
                if qt == 0 and ec < 2:
                    war += [cx.strip_consumed["q"], cx.strip_consumed["k"]]
                wo_evs[ec] = P.step("gpsimd",
                                    lambda eng, d=dst, s=strip_src(wo_l, ec): eng.dma_start(
                                        out=d, in_=s),
                                    waits=war, inc="DG", amt=16)

            load_wo(0)
            load_wo(1)
            for ec in range(8):
                pi = ec % 4
                w0 = [wo_evs[ec], psm_free[pi]] + ([OT_evs[-1]] if ec == 0 else [])
                for dc in range(NCH):
                    ev = P.step("tensor",
                                lambda eng, ec=ec, dc=dc, pi=pi: eng.matmul(
                                    cx.psm[pi][:, :],
                                    cx.wos[ec % 2][:, dc * 128:(dc + 1) * 128],
                                    OTq[:, dc * 512:(dc + 1) * 512],
                                    start=(dc == 0), stop=(dc == NCH - 1)),
                                waits=w0 if dc == 0 else (),
                                inc="T" if dc == NCH - 1 else None)
                mm_done = ev
                wo_consumed[ec % 2] = mm_done
                rev = P.step("vector",
                             lambda eng, ec=ec, tok0=tok0, pi=pi: eng.tensor_tensor(
                                 cx.xT[:, ec * CW + 1 + tok0: ec * CW + 1 + tok0 + 512],
                                 cx.xT[:, ec * CW + 1 + tok0: ec * CW + 1 + tok0 + 512],
                                 cx.psm[pi][:, :], OP.add),
                             waits=[mm_done], inc="V")
                psm_free[pi] = rev
                resid_evs.append(rev)
                if ec + 2 < 8:
                    load_wo(ec + 2)
            cx.b1_woc0, cx.b1_woc1 = wo_consumed
            cx.strip_consumed["q"] = wo_consumed[0]
            cx.strip_consumed["k"] = wo_consumed[1]
            prep_done_prev = prep_done
        attn_done = resid_evs[-8:]
        if B1_ATTN_ONLY and l == B1_DEPTH - 1:
            break

        gb = (stash["lng"][:, l * NCH:(l + 1) * NCH], stash["lnb"][:, l * NCH:(l + 1) * NCH])
        ln_evs = emit_ln(cx, "pre", src_ready=attn_done, shifted=True, gain_bias=gb,
                         tag=f"b1l{l}ffnln", src_chunks=attn_done)
        xbf = ln_evs
        mst_war = getattr(cx, "b1_mst_war", None)
        ffn_ck = [None] * NCH
        xb_evs = []
        for hf in range(2):
            t0 = hf * 1024
            m_evs = []
            w1_consumed = list(getattr(cx, "b1_w1c", [None] * 4))
            for mc in range(22):
                sa = W1p[l][:, mc * 128:(mc + 1) * 128].rearrange("(kc p) m -> p kc m", p=128)
                sg = W1p[l][:, DI2 + mc * 128: DI2 + (mc + 1) * 128].rearrange("(kc p) m -> p kc m", p=128)
                bi_ = mc % 2
                ea = P.step("gpsimd",
                            lambda eng, s=sa, bi_=bi_: eng.dma_start(
                                out=w1s[bi_ * 2][:, :].rearrange("p (kc m) -> p kc m", m=128), in_=s),
                            waits=[w1_consumed[bi_ * 2]], inc="DG", amt=16)
                eg = P.step("gpsimd",
                            lambda eng, s=sg, bi_=bi_: eng.dma_start(
                                out=w1s[bi_ * 2 + 1][:, :].rearrange("p (kc m) -> p kc m", m=128), in_=s),
                            waits=[w1_consumed[bi_ * 2 + 1]], inc="DG", amt=16)
                for it2 in range(2):
                    pa, pg = (0, 1) if (mc + it2) % 2 == 0 else (2, 3)
                    w0a = [ea, psm_free[pa]] + (xbf if mc == 0 and it2 == 0 and hf == 0 else [])
                    w0g = [eg, psm_free[pg]]
                    for kc in range(NCH):
                        eva = P.step("tensor",
                                     lambda eng, mc=mc, it2=it2, kc=kc, pa=pa, t0=t0, bi_=bi_: eng.matmul(
                                         cx.psm[pa][:, :],
                                         w1s[bi_ * 2][:, kc * 128:(kc + 1) * 128],
                                         cx.xb[:, kc * CW + 1 + t0 + it2 * 512: kc * CW + 1 + t0 + (it2 + 1) * 512],
                                         start=(kc == 0), stop=(kc == NCH - 1)),
                                     waits=w0a if kc == 0 else (),
                                     inc="T" if kc == NCH - 1 else None)
                    for kc in range(NCH):
                        evg = P.step("tensor",
                                     lambda eng, mc=mc, it2=it2, kc=kc, pg=pg, t0=t0, bi_=bi_: eng.matmul(
                                         cx.psm[pg][:, :],
                                         w1s[bi_ * 2 + 1][:, kc * 128:(kc + 1) * 128],
                                         cx.xb[:, kc * CW + 1 + t0 + it2 * 512: kc * CW + 1 + t0 + (it2 + 1) * 512],
                                         start=(kc == 0), stop=(kc == NCH - 1)),
                                     waits=w0g if kc == 0 else (),
                                     inc="T" if kc == NCH - 1 else None)
                    w1_consumed[bi_ * 2] = eva
                    w1_consumed[bi_ * 2 + 1] = evg
                    gel = P.step("scalar",
                                 lambda eng, mc=mc, pg=pg, l=l: eng.activation(
                                     t0b[:, :], cx.psm[pg][:, :], AF.Gelu,
                                     bias=stash["b1g"][:, l * 22 + mc: l * 22 + mc + 1]),
                                 waits=[evg] + pevs + (m_evs[-1:] if m_evs else [])
                                 + ([mst_war] if mc == 0 and it2 == 0 and hf == 0 else []),
                                 inc="S")
                    mev = P.step("vector",
                                 lambda eng, mc=mc, it2=it2, pa=pa, l=l: eng.scalar_tensor_tensor(
                                     Mst[:, mc * 1024 + it2 * 512: mc * 1024 + (it2 + 1) * 512],
                                     cx.psm[pa][:, :], stash["b1a"][:, l * 22 + mc: l * 22 + mc + 1],
                                     t0b[:, :], OP.add, OP.mult),
                                 waits=[eva, gel], inc="V")
                    psm_free[pa] = mev
                    psm_free[pg] = mev
                    m_evs.append(mev)
            w2_consumed = list(getattr(cx, "b1_w2c", [None, None]))
            for ec in range(8):
                s2 = W2p[l][:, ec * 128:(ec + 1) * 128].rearrange("(kc p) m -> p kc m", p=128)
                e2 = P.step("gpsimd",
                            lambda eng, s=s2, ec=ec: eng.dma_start(
                                out=w2s[ec % 2][:, :].rearrange("p (kc m) -> p kc m", m=128), in_=s),
                            waits=[w2_consumed[ec % 2]]
                            + ([w1_consumed[2], w1_consumed[3]] if ec <= 1 else []),
                            inc="DG", amt=16)
                for it2 in range(2):
                    pi = (ec * 2 + it2) % 4
                    w0 = [e2, psm_free[pi]] + (m_evs[-4:] if ec == 0 and it2 == 0 else [])
                    for kc in range(22):
                        ev = P.step("tensor",
                                    lambda eng, ec=ec, it2=it2, kc=kc, pi=pi: eng.matmul(
                                        cx.psm[pi][:, :],
                                        w2s[ec % 2][:, kc * 128:(kc + 1) * 128],
                                        Mst[:, kc * 1024 + it2 * 512: kc * 1024 + (it2 + 1) * 512],
                                        start=(kc == 0), stop=(kc == 21)),
                                    waits=w0 if kc == 0 else (),
                                    inc="T" if kc == 21 else None)
                    mm_done = ev
                    w2_consumed[ec % 2] = mm_done
                    rev = P.step("vector",
                                 lambda eng, ec=ec, it2=it2, t0=t0, pi=pi, l=l: eng.scalar_tensor_tensor(
                                     cx.xT[:, ec * CW + 1 + t0 + it2 * 512: ec * CW + 1 + t0 + (it2 + 1) * 512],
                                     cx.psm[pi][:, :], stash["b2s"][:, l * NCH + ec: l * NCH + ec + 1],
                                     cx.xT[:, ec * CW + 1 + t0 + it2 * 512: ec * CW + 1 + t0 + (it2 + 1) * 512],
                                     OP.add, OP.add),
                                 waits=[mm_done] + pevs, inc="V")
                    psm_free[pi] = rev
                    resid_evs.append(rev)
                    if it2 == 1:
                        ffn_ck[ec] = rev
                        if hf == 1 and l + 1 < B1_DEPTH:
                            xb_evs.append(P.step("scalar",
                                                 lambda eng, ec=ec: eng.activation(
                                                     cx.xb[:, ec * CW + 1: ec * CW + 1 + N],
                                                     cx.xT[:, ec * CW + 1: ec * CW + 1 + N],
                                                     AF.Copy),
                                                 waits=[rev], inc="S"))
            cx.b1_w2c = w2_consumed
            cx.b1_w2c_alias = w2_consumed[0]
            cx.b1_w2c_alias2 = w2_consumed[1]
            cx.b1_w1c = w1_consumed
        cx.b1_mst_war = ev
        ffn_done = resid_evs[-8:]
        if l + 1 < B1_DEPTH:
            xb_ready = xb_evs[-1]
        prep_done_prev = None

    if B1_SKIP_LN2:
        return resid_evs[-8:]
    out_evs = emit_ln(cx, "postx", src_ready=resid_evs[-8:], resid_dram=resid_dram,
                      tag="b1ln2", scratch_free=resid_evs[-1], src_chunks=ffn_ck)
    return out_evs


def host_b1_inputs(inp):
    f = np.float32
    d = {}
    d["b1_Wqkv"] = np.ascontiguousarray(np.asarray(inp["L0_ffn_Wqkv"], f))
    d["b1_Wo"] = np.ascontiguousarray(np.asarray(inp["L0_ffn_Wo"], f))
    W1 = np.asarray(inp["L0_ffn_W1"], f)
    W2 = np.asarray(inp["L0_ffn_W2"], f)
    DI, DI2 = 2730, 2816
    W1p = np.zeros((6, 1024, 2 * DI2), f)
    W1p[:, :, :DI] = W1[:, :, :DI]
    W1p[:, :, DI2:DI2 + DI] = W1[:, :, DI:]
    W2p = np.zeros((6, DI2, 1024), f)
    W2p[:, :DI, :] = W2
    d["b1_W1p"] = W1p
    d["b1_W2p"] = W2p
    qs = np.asarray(inp["L0_ffn_q_scale"], f)
    ks = np.asarray(inp["L0_ffn_k_scale"], f)
    d["b1_qsc"] = np.ascontiguousarray(np.tile(qs, (1, 2)).T)
    d["b1_ksc"] = np.ascontiguousarray(np.tile(ks, (1, 2)).T)
    d["b1_lng"] = np.ascontiguousarray(np.asarray(inp["L0_ffn_ln_g"], f).reshape(6, 8, 128)
                                       .transpose(2, 0, 1).reshape(128, 48))
    d["b1_lnb"] = np.ascontiguousarray(np.asarray(inp["L0_ffn_ln_b"], f).reshape(6, 8, 128)
                                       .transpose(2, 0, 1).reshape(128, 48))
    b1 = np.asarray(inp["L0_ffn_b1"], f)
    b1ap = np.zeros((6, DI2), f); b1ap[:, :DI] = b1[:, :DI]
    b1gp = np.zeros((6, DI2), f); b1gp[:, :DI] = b1[:, DI:]
    d["b1_b1as"] = np.ascontiguousarray(b1ap.reshape(6, 22, 128).transpose(2, 0, 1).reshape(128, 132))
    d["b1_b1gs"] = np.ascontiguousarray(b1gp.reshape(6, 22, 128).transpose(2, 0, 1).reshape(128, 132))
    d["b1_b2s"] = np.ascontiguousarray(np.asarray(inp["L0_ffn_b2"], f).reshape(6, 8, 128)
                                       .transpose(2, 0, 1).reshape(128, 48))
    return d


def build_b1_program():
    nc = bass.Bass("TRN2", target_bir_lowering=False, debug=False)
    cx = build_base(nc)
    cx.b1_eb_free = [None, None]
    emit_init(cx)
    emit_load_x(cx)
    out_evs = emit_b1(cx, resid_dram=cx.xin)
    emit_store_x(cx, out_evs)
    with nc.Block() as block:
        cx.P.emit(block)
    return nc, cx



def kernel(**inputs):
    x = np.ascontiguousarray(np.asarray(inputs["L0_attn_x"], np.float32))
    nc, cx = build_fused_program()
    wd = host_fused_inputs(inputs, cx)
    in_maps = []
    for c in range(8):
        m = dict(wd)
        m["xT_in"] = np.ascontiguousarray(x[c].T)
        in_maps.append(m)
    res = run_bass_kernel_spmd(nc, in_maps, core_ids=list(range(8)))
    return np.stack([np.asarray(r["xT_out"]).T for r in res.results]).astype(np.float32)
